# version 05
# baseline (speedup 1.0000x reference)
import numpy as np
import ml_dtypes
import concourse.bass as bass
import concourse.mybir as mybir
from concourse.bass_utils import run_bass_kernel_spmd
from contextlib import ExitStack

F32 = mybir.dt.float32
BF16 = mybir.dt.bfloat16
I32 = mybir.dt.int32
AF = mybir.ActivationFunctionType
ALU = mybir.AluOpType
AX = mybir.AxisListType

B, S, D = 2, 4096, 4096
NCORES = 8
TOK = 1024
NT = TOK // 128
NH, QR, KVR, RD, ND, VD = 32, 1536, 512, 64, 128, 128
QK = ND + RD
EPS_RMS = 1e-6
EPS_LN = 1e-5
SM_SCALE = QK ** (-0.5)
INTER = 11008
RANK = 1024
E_EXP, R_LORA = 8, 16

BF = ml_dtypes.bfloat16


def host_consts():
    inv_freq = 10000.0 ** (-(np.arange(0, RD, 2, dtype=np.float64) / RD))
    t = np.arange(S, dtype=np.float64)
    freqs = np.outer(t, inv_freq)
    emb = np.concatenate([freqs, freqs], -1)
    Gmat = np.zeros((128, 4), dtype=np.float32)
    for r in range(128):
        Gmat[r, r // 32] = 1.0
    return dict(
        ident_bf=np.eye(128, dtype=BF),
        ident_f32=np.eye(128, dtype=np.float32),
        ones_f32=np.ones((1, 128), dtype=np.float32),
        ones_bf=np.ones((1, 128), dtype=BF),
        cos_fm=np.ascontiguousarray(np.cos(emb).T.astype(np.float32)),
        sin_fm=np.ascontiguousarray(np.sin(emb).T.astype(np.float32)),
        Gmat=Gmat,
    )


class Ctr:

    def __init__(self, h):
        self.h = h
        self.v = 0

    def inc(self, instr, n=1):
        instr.then_inc(self.h, n)
        self.v += n


class DStream:

    def __init__(self, p, nslots=2):
        self.sems = [p.sem() for _ in range(nslots)]
        self.n = nslots
        self.i = 0

    def issue(self, instr):
        s = self.sems[self.i % self.n]
        instr.then_inc(s.h, 16)
        s.v += 16
        self.i += 1
        return self.i - 1

    def wait(self, eng, idx):
        eng.wait_ge(self.sems[idx % self.n].h, 16 * (idx // self.n + 1))

    def wait_all(self, eng):
        for s in self.sems:
            if s.v:
                eng.wait_ge(s.h, s.v)


class Prog:
    def __init__(self):
        self.nc = bass.Bass("TRN2", target_bir_lowering=False, debug=False)
        self.ctx = ExitStack()
        self._sem_n = 0

    def sb(self, name, shape, dt=F32):
        return self.ctx.enter_context(
            self.nc.sbuf_tensor(name, shape, dt, allow_name_mangling=True))

    def ps(self, name, shape, dt=F32):
        return self.ctx.enter_context(
            self.nc.psum_tensor(name, shape, dt, allow_name_mangling=True))

    def sem(self, name=None):
        self._sem_n += 1
        name = name or f"s{self._sem_n}"
        return Ctr(self.ctx.enter_context(self.nc.semaphore(name)))

    def inp(self, name, shape, dt=F32):
        return self.nc.dram_tensor(name, list(shape), dt, kind="ExternalInput").ap()

    def outp(self, name, shape, dt=F32):
        return self.nc.dram_tensor(name, list(shape), dt, kind="ExternalOutput").ap()


def chunks_ap(tensor_sb, tok, c0, tt, nch):
    nch_total = tensor_sb.shape[1] // tok
    return tensor_sb[:, :].rearrange("p (c t) -> p c t", c=nch_total)[
        :, c0:c0 + nch, tt * 128:(tt + 1) * 128]


def chunk_ap(tensor_sb, tok, c, lo, w):
    return tensor_sb[:, c * tok + lo:c * tok + lo + w]


def emit_rope_fm(v, out_bf, src, cos, sin, tmp, tmp2, w, chain, prev_done=None):
    if chain.v:
        v.wait_ge(chain.h, chain.v)
    if prev_done is not None:
        v.wait_ge(prev_done[0].h, prev_done[1])
    v.tensor_mul(tmp[0:32, 0:w], src[32:64, 0:w], sin[0:32, 0:w])
    v.tensor_mul(tmp[32:64, 0:w], src[0:32, 0:w], sin[32:64, 0:w])
    v.tensor_mul(tmp2[0:32, 0:w], src[0:32, 0:w], cos[0:32, 0:w])
    chain.inc(v.tensor_mul(tmp2[32:64, 0:w], src[32:64, 0:w], cos[32:64, 0:w]))
    v.wait_ge(chain.h, chain.v)
    chain.inc(v.tensor_sub(out_bf[0:32, 0:w], tmp2[0:32, 0:w], tmp[0:32, 0:w]))
    return v.tensor_add(out_bf[32:64, 0:w], tmp2[32:64, 0:w], tmp[32:64, 0:w])


def emit_bm(p, nc, mask_rows, Gmat_d, bmp, bm_nch, entry_waits=()):
    g_ld = p.sem()
    m_ld = [p.sem(), p.sem()]
    ev6 = p.sem()
    mm6 = p.sem()
    ev6b = p.sem()
    ch6 = p.sem()
    stBM = p.sem()
    with nc.sbuf_tensor("bm_G", [128, 4], F32, allow_name_mangling=True) as Gmat_s, \
         nc.sbuf_tensor("mbuf", [128, 2 * D], I32, allow_name_mangling=True) as mbuf, \
         nc.sbuf_tensor("mf32", [128, 2 * D], F32, allow_name_mangling=True) as mf32, \
         nc.sbuf_tensor("mred", [128, 128], F32, allow_name_mangling=True) as mred, \
         nc.sbuf_tensor("bms", [4, bm_nch * 128], F32, allow_name_mangling=True) as bms, \
         nc.psum_tensor("pbm", [4, 128], F32, allow_name_mangling=True) as pbm, \
         nc.Block() as blk:
        @blk.sync
        def _(s):
            for ctr, val in entry_waits:
                s.wait_ge(ctr.h, val)
            g_ld.inc(s.dma_start(out=Gmat_s[:], in_=Gmat_d), 16)
            for ch in range(bm_nch):
                if ch >= 2:
                    s.wait_ge(ev6.h, ch - 1)
                m_ld[ch % 2].inc(s.dma_start(
                    out=mbuf[:, (ch % 2) * D:(ch % 2 + 1) * D],
                    in_=mask_rows[ch * 128:(ch + 1) * 128, :]), 16)

        @blk.vector
        def _(v):
            for step in range(bm_nch + 2):
                if step < bm_nch:
                    v.wait_ge(m_ld[step % 2].h, 16 * (step // 2 + 1))
                    if step >= 2:
                        v.wait_ge(ev6.h, step - 1)
                    ch6.inc(v.tensor_copy(mf32[:, (step % 2) * D:(step % 2 + 1) * D],
                                          mbuf[:, (step % 2) * D:(step % 2 + 1) * D]))
                if 1 <= step <= bm_nch:
                    rch = step - 1
                    v.wait_ge(ch6.h, rch + 1)
                    if rch >= 1:
                        v.wait_ge(mm6.h, rch)
                    ev6.inc(v.tensor_reduce(
                        mred[:, :],
                        mf32[:, (rch % 2) * D:(rch % 2 + 1) * D].rearrange(
                            "p (a b) -> p a b", b=32),
                        AX.X, ALU.add))
                if step >= 2:
                    cch = step - 2
                    v.wait_ge(mm6.h, cch + 1)
                    ev6b.inc(v.tensor_copy(bms[:, cch * 128:(cch + 1) * 128], pbm[:]))

        @blk.tensor
        def _(t):
            for ctr, val in entry_waits:
                t.wait_ge(ctr.h, val)
            t.wait_ge(g_ld.h, 16)
            for ch in range(bm_nch):
                t.wait_ge(ev6.h, ch + 1)
                if ch >= 1:
                    t.wait_ge(ev6b.h, ch)
                mm6.inc(t.matmul(pbm[:], Gmat_s[:, :], mred[:, :],
                                 start=True, stop=True))

        @blk.scalar
        def _(s):
            s.wait_ge(ev6b.h, bm_nch)
            stBM.inc(s.dma_start(
                out=bmp.rearrange("(c p) w -> p c w", p=4),
                in_=bms[:, :].rearrange("p (c w) -> p c w", c=bm_nch)), 16)
    return [stBM]


def emit_launch_A(p, io, do_q=True, do_kv=True, do_bm=True, bm_nch=4):
    nc = p.nc
    _es = ExitStack()
    def _sbl(name, shape, dt=F32):
        return _es.enter_context(
            nc.sbuf_tensor(name, shape, dt, allow_name_mangling=True))
    xT_src = io["xT"]
    w_qa = io.get("w_qa")
    g_qa = io.get("g_qa")
    w_qb = io.get("w_qb")
    w_kva = io["w_kva"]
    g_kva = io["g_kva"]
    w_kvb = io["w_kvb"]
    cos_own = io["cos_own"]
    sin_own = io["sin_own"]
    mask_rows = io.get("mask_rows")
    Gmat_d = io.get("Gmat")
    ident_bf_d = io["ident_bf"]
    ident_f32_d = io["ident_f32"]
    ones_f32_d = io["ones_f32"]
    Qf = io.get("Qf")
    KN = io["KN"]
    KR = io["KR"]
    Vt = io["Vt"]
    bmp = io.get("bmp")

    ident_bf = _sbl("ident_bf_s", [128, 128], BF16)
    ident_f32 = _sbl("ident_f32_s", [128, 128])
    ones_f32 = _sbl("ones_f32_s", [1, 128])
    cos_sb = _sbl("cos_sb", [RD, TOK])
    sin_sb = _sbl("sin_sb", [RD, TOK])
    g_qa_b = _sbl("g_qa_b", [128, QR])
    g_kva_b = _sbl("g_kva_b", [128, KVR])
    g_row = _sbl("g_row", [1, QR])
    g_row2 = _sbl("g_row2", [1, KVR])
    qcT = _sbl("qcT", [128, 12 * TOK], BF16)
    kcT = _sbl("kcT", [128, 4 * TOK], BF16)
    kr_f32 = _sbl("kr_f32", [128, NT * RD])
    rope_tmp = _sbl("rope_tmp", [64, TOK])
    rope_tmp2 = _sbl("rope_tmp2", [64, TOK])
    rope_chain = p.sem()

    all_stores = []

    def new_store(nslots=1):
        cs = [p.sem() for _ in range(nslots)]
        all_stores.extend(cs)
        return cs

    c_ld = p.sem()
    mm0 = p.sem()
    ev0 = p.sem()
    with nc.psum_tensor("pg", [128, QR], allow_name_mangling=True) as pg, \
         nc.psum_tensor("pg2", [128, KVR], allow_name_mangling=True) as pg2, \
         nc.Block() as blk:
        @blk.sync
        def _(s):
            pairs = [(ident_bf[:], ident_bf_d), (ident_f32[:], ident_f32_d),
                     (ones_f32[:], ones_f32_d),
                     (cos_sb[:], cos_own), (sin_sb[:], sin_own)]
            if do_kv:
                pairs.append((g_row2[:], g_kva))
            if do_q:
                pairs.append((g_row[:], g_qa))
            for dst, srcd in pairs:
                c_ld.inc(s.dma_start(out=dst, in_=srcd), 16)

        @blk.tensor
        def _(t):
            t.wait_ge(c_ld.h, c_ld.v)
            last = None
            if do_q:
                for j in range(3):
                    last = t.matmul(pg[:, j * 512:(j + 1) * 512], ones_f32[:],
                                    g_row[0:1, j * 512:(j + 1) * 512],
                                    start=True, stop=True)
            if do_kv:
                last = t.matmul(pg2[:], ones_f32[:], g_row2[0:1, :],
                                start=True, stop=True)
            mm0.inc(last)

        @blk.vector
        def _(v):
            v.wait_ge(mm0.h, 1)
            last = None
            if do_q:
                last = v.tensor_copy(g_qa_b[:], pg[:])
            if do_kv:
                last = v.tensor_copy(g_kva_b[:], pg2[:])
            ev0.inc(last)

    with nc.sbuf_tensor("xT", [128, 32 * TOK], BF16, allow_name_mangling=True) as xT, \
         nc.sbuf_tensor("qc_bf", [128, NT * QR], BF16, allow_name_mangling=True) as qc_bf, \
         nc.sbuf_tensor("kc_bf", [128, NT * KVR], BF16, allow_name_mangling=True) as kc_bf, \
         nc.sbuf_tensor("ss_q", [128, NT * 3], F32, allow_name_mangling=True) as ss_q, \
         nc.sbuf_tensor("ss_k", [128, NT], F32, allow_name_mangling=True) as ss_k:

        x_ld = [p.sem(), p.sem()]
        with nc.Block() as blk:
            @blk.gpsimd
            def _(g):
                for c in range(32):
                    x_ld[c % 2].inc(g.dma_start(
                        out=chunk_ap(xT, TOK, c, 0, TOK),
                        in_=xT_src[c * 128:(c + 1) * 128, :]), 16)
        xT_ready = [(x_ld[0], x_ld[0].v), (x_ld[1], x_ld[1].v), (ev0, ev0.v)]

        def proj(w_dram, nkc, dout, out_bf, ss, tag, prev_psum_free,
                 prev_sbuf_free=(), extra_cols=None, extra_dst=None):
            w_ld = [p.sem() for _ in range(4)]
            w_cst = p.sem()
            w_fr = p.sem()
            ev = p.sem()
            sq = p.sem()
            ndg = dout // 512
            ngrp = ndg + (1 if extra_cols else 0)
            with nc.sbuf_tensor(f"wbuf_{tag}", [128, 4 * 512], BF16, allow_name_mangling=True) as wbuf, \
                 nc.sbuf_tensor(f"wraw_{tag}", [128, 4 * 512], F32, allow_name_mangling=True) as wraw, \
                 nc.sbuf_tensor(f"scr_{tag}", [128, 2 * 512], F32, allow_name_mangling=True) as scr, \
                 nc.psum_tensor(f"pq_{tag}", [128, 4096], allow_name_mangling=True) as pq, \
                 nc.Block() as blk:
                def _gw(dg):
                    return 512 if dg < ndg else extra_cols[1] - extra_cols[0]

                @blk.sync
                def _(s):
                    for ctr, val in xT_ready:
                        s.wait_ge(ctr.h, val)
                    for ctr, val in prev_sbuf_free:
                        s.wait_ge(ctr.h, val)
                    for dg in range(ngrp):
                        for kc in range(nkc):
                            idx = dg * nkc + kc
                            if idx >= 4:
                                s.wait_ge(w_cst.h, idx - 3)
                            if dg < ndg:
                                srcd = w_dram[kc * 128:(kc + 1) * 128,
                                              dg * 512:(dg + 1) * 512]
                            else:
                                srcd = w_dram[kc * 128:(kc + 1) * 128,
                                              extra_cols[0]:extra_cols[1]]
                            w = _gw(dg)
                            w_ld[idx % 4].inc(s.dma_start(
                                out=wraw[:, (idx % 4) * 512:(idx % 4) * 512 + w],
                                in_=srcd), 16)

                @blk.tensor
                def _(t):
                    for ctr, val in prev_psum_free:
                        t.wait_ge(ctr.h, val)
                    for dg in range(ngrp):
                        w = _gw(dg)
                        if dg > 0:
                            t.wait_ge(ev.h, dg * NT)
                            t.wait_ge(sq.h, min(dg, ndg) * NT)
                        for kc in range(nkc):
                            idx = dg * nkc + kc
                            t.wait_ge(w_cst.h, idx + 1)
                            last = None
                            for tt in range(NT):
                                last = t.matmul(
                                    pq[:, tt * 512:tt * 512 + w],
                                    chunk_ap(xT, TOK, kc, tt * 128, 128),
                                    wbuf[:, (idx % 4) * 512:(idx % 4) * 512 + w],
                                    start=(kc == 0), stop=(kc == nkc - 1))
                            w_fr.inc(last)

                @blk.vector
                def _(v):
                    def wcast(idx):
                        w = _gw(idx // nkc)
                        v.wait_ge(w_ld[idx % 4].h, 16 * (idx // 4 + 1))
                        if idx >= 4:
                            v.wait_ge(w_fr.h, idx - 3)
                        w_cst.inc(v.tensor_copy(
                            wbuf[:, (idx % 4) * 512:(idx % 4) * 512 + w],
                            wraw[:, (idx % 4) * 512:(idx % 4) * 512 + w]))

                    def emit_ev(dg):
                        w = _gw(dg)
                        v.wait_ge(w_fr.h, (dg + 1) * nkc)
                        for tt in range(NT):
                            if dg < ndg:
                                dst = out_bf[:, tt * dout + dg * 512:
                                             tt * dout + dg * 512 + w]
                            else:
                                dst = extra_dst[:, tt * w:(tt + 1) * w]
                            ev.inc(v.tensor_copy(dst, pq[:, tt * 512:tt * 512 + w]))

                    for dg in range(ngrp):
                        for kc in range(nkc):
                            wcast(dg * nkc + kc)
                            if kc == 0 and dg > 0:
                                emit_ev(dg - 1)
                    emit_ev(ngrp - 1)

                @blk.scalar
                def _(s):
                    for dg in range(ndg):
                        for tt in range(NT):
                            s.wait_ge(ev.h, dg * NT + tt + 1)
                            idx = dg * NT + tt
                            if idx >= 2:
                                s.wait_ge(sq.h, idx - 1)
                            sq.inc(s.activation(
                                scr[:, (idx % 2) * 512:(idx % 2 + 1) * 512],
                                pq[:, tt * 512:(tt + 1) * 512],
                                AF.Square,
                                accum_out=ss[:, tt * ndg + dg:tt * ndg + dg + 1]))
            return w_fr, ev, sq

        def rmsnorm_T(src_bf, ss, g_bcast, dim, dstT, sq_prev, tag,
                      psum_free_waits):
            ev_a = p.sem()
            sq_b = p.sem()
            ev_c = p.sem()
            tp_d = p.sem()
            ev_e = p.sem()
            ch1 = p.sem()
            ch2 = p.sem()
            ch3 = p.sem()
            ndg = dim // 512
            nch = dim // 128
            nb = (nch + 7) // 8
            sq_prev_base = sq_prev.v - ndg * NT
            with nc.sbuf_tensor(f"ssum_{tag}", [128, NT], F32, allow_name_mangling=True) as ssum, \
                 nc.sbuf_tensor(f"rms_{tag}", [128, NT], F32, allow_name_mangling=True) as rms, \
                 nc.sbuf_tensor(f"rcp_{tag}", [128, NT], F32, allow_name_mangling=True) as rcp, \
                 nc.sbuf_tensor(f"rcp2_{tag}", [128, NT], F32, allow_name_mangling=True) as rcp2, \
                 nc.sbuf_tensor(f"ybf_{tag}", [128, 2 * dim], BF16, allow_name_mangling=True) as ybf, \
                 nc.psum_tensor(f"ptr_{tag}", [128, 2048], BF16, allow_name_mangling=True) as ptr, \
                 nc.Block() as blk:
                @blk.vector
                def _(v):
                    if ndg == 1:
                        for tt in range(NT):
                            v.wait_ge(sq_prev.h, sq_prev_base + tt + 1)
                            ev_a.inc(v.tensor_copy(ssum[:, tt:tt + 1], ss[:, tt:tt + 1]))
                    else:
                        for tt in range(NT):
                            v.wait_ge(sq_prev.h, sq_prev_base + (ndg - 1) * NT + tt + 1)
                            ch1.inc(v.tensor_add(ssum[:, tt:tt + 1],
                                                 ss[:, tt * ndg:tt * ndg + 1],
                                                 ss[:, tt * ndg + 1:tt * ndg + 2]))
                        for tt in range(NT):
                            v.wait_ge(ch1.h, tt + 1)
                            ev_a.inc(v.tensor_add(ssum[:, tt:tt + 1], ssum[:, tt:tt + 1],
                                                  ss[:, tt * ndg + 2:tt * ndg + 3]))

                @blk.scalar
                def _(s):
                    for tt in range(NT):
                        s.wait_ge(ev_a.h, tt + 1)
                        sq_b.inc(s.activation(rms[:, tt:tt + 1], ssum[:, tt:tt + 1],
                                              AF.Sqrt, scale=1.0 / dim))

                @blk.vector
                def _(v):
                    for tt in range(NT):
                        v.wait_ge(sq_b.h, tt + 1)
                        ch2.inc(v.tensor_scalar_add(rcp[:, tt:tt + 1],
                                                    rms[:, tt:tt + 1], EPS_RMS))
                    for tt in range(NT):
                        v.wait_ge(ch2.h, tt + 1)
                        ch3.inc(v.reciprocal(rcp2[:, tt:tt + 1], rcp[:, tt:tt + 1]))
                    def emit_evac(v, tt):
                        for b in range(nb):
                            n_in_b = min(8, nch - b * 8)
                            idx = tt * nb + b
                            v.wait_ge(tp_d.h, idx + 1)
                            po = (idx % 2) * 1024
                            ev_e.inc(v.tensor_copy(
                                chunks_ap(dstT, TOK, b * 8, tt, n_in_b),
                                ptr[:, po:po + n_in_b * 128]))

                    for tt in range(NT):
                        v.wait_ge(ch3.h, tt + 1)
                        if tt >= 2:
                            v.wait_ge(tp_d.h, (tt - 1) * nb)
                        last = None
                        for dg in range(ndg):
                            last = v.scalar_tensor_tensor(
                                ybf[:, (tt % 2) * dim + dg * 512:(tt % 2) * dim + (dg + 1) * 512],
                                src_bf[:, tt * dim + dg * 512:tt * dim + (dg + 1) * 512],
                                rcp2[:, tt:tt + 1],
                                g_bcast[:, dg * 512:(dg + 1) * 512],
                                ALU.mult, ALU.mult)
                        ev_c.inc(last)
                        if tt >= 1:
                            emit_evac(v, tt - 1)
                    emit_evac(v, NT - 1)

                @blk.tensor
                def _(t):
                    for ctr, val in psum_free_waits:
                        t.wait_ge(ctr.h, val)
                    for tt in range(NT):
                        t.wait_ge(ev_c.h, tt + 1)
                        for b in range(nb):
                            n_in_b = min(8, nch - b * 8)
                            idx = tt * nb + b
                            if idx >= 2:
                                t.wait_ge(ev_e.h, idx - 1)
                            po = (idx % 2) * 1024
                            last = None
                            for c8 in range(n_in_b):
                                c = b * 8 + c8
                                last = t.transpose(
                                    ptr[:, po + c8 * 128:po + (c8 + 1) * 128],
                                    ybf[:, (tt % 2) * dim + c * 128:(tt % 2) * dim + (c + 1) * 128],
                                    ident_bf[:])
                            tp_d.inc(last)
            return ev_c, ev_e, tp_d

        evT_q = evT_k = ev3 = None
        if do_q:
            wfr2, ev2, sq2 = proj(w_qa, 32, QR, qc_bf, ss_q, "qc",
                                  prev_psum_free=list(xT_ready))
            evc_q, evT_q, tpd_q = rmsnorm_T(qc_bf, ss_q, g_qa_b, QR, qcT,
                                            sq2, "qn",
                                            psum_free_waits=[(ev2, 3 * NT),
                                                             (sq2, 3 * NT)])
        if do_kv:
            if do_q:
                wfr3, ev3, sq3 = proj(w_kva, 32, KVR, kc_bf, ss_k, "kc",
                                      prev_psum_free=[(ev2, 3 * NT), (sq2, 3 * NT),
                                                      (evT_q, evT_q.v)],
                                      prev_sbuf_free=[(wfr2, 96)],
                                      extra_cols=(KVR, KVR + RD), extra_dst=kr_f32)
            else:
                wfr3, ev3, sq3 = proj(w_kva, 32, KVR, kc_bf, ss_k, "kc",
                                      prev_psum_free=list(xT_ready),
                                      extra_cols=(KVR, KVR + RD), extra_dst=kr_f32)
            evc_k, evT_k, tpd_k = rmsnorm_T(kc_bf, ss_k, g_kva_b, KVR, kcT,
                                            sq3, "kn",
                                            psum_free_waits=[(ev3, 2 * NT),
                                                             (sq3, NT)])

    if do_kv:
        tp3c = p.sem()
        ev3c = p.sem()
        stKR = new_store(1)[0]
        with nc.sbuf_tensor("kr_bf", [RD, TOK], BF16, allow_name_mangling=True) as kr_bf, \
             nc.psum_tensor("prope", [RD, TOK], allow_name_mangling=True) as prope, \
             nc.Block() as blk:
            @blk.tensor
            def _(t):
                t.wait_ge(evT_k.h, evT_k.v)
                t.wait_ge(ev3.h, 2 * NT)
                last = None
                for tt in range(NT):
                    last = t.transpose(prope[:, tt * 128:(tt + 1) * 128],
                                       kr_f32[:, tt * RD:(tt + 1) * RD],
                                       ident_f32[:])
                tp3c.inc(last)

            @blk.vector
            def _(v):
                v.wait_ge(tp3c.h, 1)
                ev3c.inc(emit_rope_fm(v, kr_bf, prope, cos_sb, sin_sb,
                                      rope_tmp, rope_tmp2, TOK, rope_chain))

            @blk.scalar
            def _(s):
                s.wait_ge(ev3c.h, 1)
                stKR.inc(s.dma_start(out=KR, in_=kr_bf[:]), 16)

    if do_q:
        w_ld4 = [p.sem() for _ in range(4)]
        w_cst4 = p.sem()
        mm4 = p.sem()
        ev4 = p.sem()
        stQ = new_store(2)
        with nc.sbuf_tensor("wqb_sb", [128, 2 * 12 * 768], BF16, allow_name_mangling=True) as wqb_sb, \
             nc.sbuf_tensor("wqb_raw", [128, 4 * 768], F32, allow_name_mangling=True) as wqb_raw, \
             nc.sbuf_tensor("qhi_bf", [128, 2 * TOK], BF16, allow_name_mangling=True) as qhi_bf, \
             nc.sbuf_tensor("qlo_bf", [64, 2 * TOK], BF16, allow_name_mangling=True) as qlo_bf, \
             nc.psum_tensor("phi", [128, 2 * TOK], allow_name_mangling=True) as phi, \
             nc.psum_tensor("plo", [64, 2 * TOK], allow_name_mangling=True) as plo, \
             nc.Block() as blk:
            @blk.sync
            def _(s):
                s.wait_ge(evT_q.h, evT_q.v)
                if do_kv:
                    s.wait_ge(evT_k.h, evT_k.v)
                    s.wait_ge(stKR.h, 16)
                for hg in range(8):
                    for c in range(12):
                        idx = hg * 12 + c
                        if idx >= 4:
                            s.wait_ge(w_cst4.h, idx - 3)
                        w_ld4[idx % 4].inc(s.dma_start(
                            out=wqb_raw[:, (idx % 4) * 768:(idx % 4 + 1) * 768],
                            in_=w_qb[c * 128:(c + 1) * 128, hg * 768:(hg + 1) * 768]), 16)

            @blk.tensor
            def _(t):
                if do_kv:
                    t.wait_ge(ev3c.h, 1)
                for h in range(NH):
                    hg, hh = divmod(h, 4)
                    par = h % 2
                    t.wait_ge(w_cst4.h, 12 * (hg + 1))
                    if h >= 2:
                        t.wait_ge(ev4.h, h - 1)
                    last = None
                    for c in range(12):
                        wb = (hg % 2) * 9216 + c * 768 + hh * 192
                        for half in range(2):
                            t.matmul(phi[:, par * TOK + half * 512:par * TOK + half * 512 + 512],
                                     wqb_sb[:, wb:wb + 128],
                                     chunk_ap(qcT, TOK, c, half * 512, 512),
                                     start=(c == 0), stop=(c == 11))
                            last = t.matmul(
                                plo[:, par * TOK + half * 512:par * TOK + half * 512 + 512],
                                wqb_sb[:, wb + 128:wb + 192],
                                chunk_ap(qcT, TOK, c, half * 512, 512),
                                start=(c == 0), stop=(c == 11))
                    mm4.inc(last)

            @blk.vector
            def _(v):
                def qcast(idx):
                    hg_ = idx // 12
                    v.wait_ge(w_ld4[idx % 4].h, 16 * (idx // 4 + 1))
                    if idx % 12 == 0 and hg_ >= 2:
                        v.wait_ge(mm4.h, 4 * (hg_ - 1))
                    w_cst4.inc(v.tensor_copy(
                        wqb_sb[:, (hg_ % 2) * 9216 + (idx % 12) * 768:
                               (hg_ % 2) * 9216 + (idx % 12 + 1) * 768],
                        wqb_raw[:, (idx % 4) * 768:(idx % 4 + 1) * 768]))

                def qrope(h):
                    par = h % 2
                    v.wait_ge(mm4.h, h + 1)
                    if h >= 2:
                        v.wait_ge(stQ[h % 2].h, 32 * (h // 2))
                    v.tensor_copy(qhi_bf[:, par * TOK:(par + 1) * TOK],
                                  phi[:, par * TOK:(par + 1) * TOK])
                    pd = ((ev3c, 1) if do_kv else None) if h == 0 else (ev4, h)
                    ev4.inc(emit_rope_fm(
                        v, qlo_bf[:, par * TOK:(par + 1) * TOK],
                        plo[:, par * TOK:(par + 1) * TOK],
                        cos_sb, sin_sb, rope_tmp, rope_tmp2, TOK, rope_chain,
                        prev_done=pd))

                for hg in range(8):
                    for c in range(12):
                        qcast(hg * 12 + c)
                    if hg >= 1:
                        for hh in range(4):
                            qrope((hg - 1) * 4 + hh)
                for hh in range(4):
                    qrope(7 * 4 + hh)

            @blk.scalar
            def _(s):
                for h in range(NH):
                    par = h % 2
                    s.wait_ge(ev4.h, h + 1)
                    stQ[h % 2].inc(s.dma_start(out=Qf[h * QK:h * QK + ND, :],
                                               in_=qhi_bf[:, par * TOK:(par + 1) * TOK]), 16)
                    stQ[h % 2].inc(s.dma_start(out=Qf[h * QK + ND:(h + 1) * QK, :],
                                               in_=qlo_bf[:, par * TOK:(par + 1) * TOK]), 16)

    if do_kv:
        kv_ld = [p.sem(), p.sem()]
        kv_cst = p.sem()
        mm5 = p.sem()
        ev5 = p.sem()
        mm5v = p.sem()
        ev5v = p.sem()
        stKN = new_store(2)
        stV = new_store(2)
        with nc.sbuf_tensor("wkvb_sb", [128, 4 * 8192], BF16, allow_name_mangling=True) as wkvb_sb, \
             nc.sbuf_tensor("wkvb_raw", [128, 2 * 4096], F32, allow_name_mangling=True) as wkvb_raw, \
             nc.sbuf_tensor("kn_bf", [128, 2 * TOK], BF16, allow_name_mangling=True) as kn_bf, \
             nc.sbuf_tensor("v_bf", [128, 2 * NH * VD], BF16, allow_name_mangling=True) as v_bf, \
             nc.psum_tensor("pkn", [128, 2 * TOK], allow_name_mangling=True) as pkn, \
             nc.psum_tensor("pv", [128, 2 * 512], allow_name_mangling=True) as pv, \
             nc.Block() as blk:
            @blk.sync
            def _(s):
                if do_q:
                    s.wait_ge(mm4.h, NH)
                    s.wait_ge(stQ[0].h, stQ[0].v)
                    s.wait_ge(stQ[1].h, stQ[1].v)
                else:
                    s.wait_ge(evT_k.h, evT_k.v)
                    s.wait_ge(stKR.h, 16)
                for idx in range(8):
                    r, hf = divmod(idx, 2)
                    if idx >= 2:
                        s.wait_ge(kv_cst.h, idx - 1)
                    kv_ld[idx % 2].inc(s.dma_start(
                        out=wkvb_raw[:, (idx % 2) * 4096:(idx % 2 + 1) * 4096],
                        in_=w_kvb[r * 128:(r + 1) * 128,
                                  hf * 4096:(hf + 1) * 4096]), 16)

            @blk.tensor
            def _(t):
                t.wait_ge(kv_cst.h, 8)
                if do_q:
                    t.wait_ge(ev4.h, NH)
                else:
                    t.wait_ge(ev3c.h, 1)
                for h in range(NH):
                    par = h % 2
                    if h >= 2:
                        t.wait_ge(ev5.h, h - 1)
                    last = None
                    for r in range(4):
                        for half in range(2):
                            last = t.matmul(
                                pkn[:, par * TOK + half * 512:par * TOK + (half + 1) * 512],
                                wkvb_sb[:, r * 8192 + h * 256:r * 8192 + h * 256 + 128],
                                chunk_ap(kcT, TOK, r, half * 512, 512),
                                start=(r == 0), stop=(r == 3))
                    mm5.inc(last)
                for tt in range(NT):
                    for hg in range(8):
                        idx = tt * 8 + hg
                        par = idx % 2
                        if idx >= 2:
                            t.wait_ge(ev5v.h, idx - 1)
                        last = None
                        for r in range(4):
                            rhs = wkvb_sb[:, :].rearrange(
                                "p (r h c) -> p r h c", r=4, h=NH)[
                                :, r, hg * 4:(hg + 1) * 4, 128:256]
                            last = t.matmul(
                                pv[:, par * 512:(par + 1) * 512],
                                chunk_ap(kcT, TOK, r, tt * 128, 128),
                                rhs, start=(r == 0), stop=(r == 3))
                        mm5v.inc(last)

            @blk.vector
            def _(v):
                for idx in range(8):
                    r, hf = divmod(idx, 2)
                    v.wait_ge(kv_ld[idx % 2].h, 16 * (idx // 2 + 1))
                    kv_cst.inc(v.tensor_copy(
                        wkvb_sb[:, r * 8192 + hf * 4096:
                                r * 8192 + (hf + 1) * 4096],
                        wkvb_raw[:, (idx % 2) * 4096:(idx % 2 + 1) * 4096]))
                for h in range(NH):
                    par = h % 2
                    v.wait_ge(mm5.h, h + 1)
                    if h >= 2:
                        v.wait_ge(stKN[h % 2].h, 16 * (h // 2))
                    ev5.inc(v.tensor_copy(kn_bf[:, par * TOK:(par + 1) * TOK],
                                          pkn[:, par * TOK:(par + 1) * TOK]))
                for tt in range(NT):
                    for hg in range(8):
                        idx = tt * 8 + hg
                        par = idx % 2
                        v.wait_ge(mm5v.h, idx + 1)
                        if hg == 0 and tt >= 2:
                            v.wait_ge(stV[tt % 2].h, 16 * (tt // 2))
                        ev5v.inc(v.tensor_copy(
                            v_bf[:, (tt % 2) * NH * VD + hg * 512:(tt % 2) * NH * VD + (hg + 1) * 512],
                            pv[:, par * 512:(par + 1) * 512]))

            @blk.scalar
            def _(s):
                for h in range(NH):
                    par = h % 2
                    s.wait_ge(ev5.h, h + 1)
                    stKN[h % 2].inc(s.dma_start(out=KN[h * ND:(h + 1) * ND, :],
                                                in_=kn_bf[:, par * TOK:(par + 1) * TOK]), 16)
                for tt in range(NT):
                    s.wait_ge(ev5v.h, (tt + 1) * 8)
                    stV[tt % 2].inc(s.dma_start(out=Vt[tt * 128:(tt + 1) * 128, :],
                                                in_=v_bf[:, (tt % 2) * NH * VD:(tt % 2 + 1) * NH * VD]), 16)

    if do_bm:
        ew = []
        if do_kv:
            ew = [(mm5v, NT * 8), (ev5v, NT * 8)] + \
                [(cs, cs.v) for cs in (stKN + stV)]
        all_stores.extend(emit_bm(p, nc, mask_rows, Gmat_d, bmp, bm_nch,
                                  entry_waits=ew))

    _es.close()
    return all_stores


def build_launch_A():
    p = Prog()
    io = dict(
        xT=p.inp("xT", [D, TOK]),
        w_qa=p.inp("w_qa", [D, QR]),
        g_qa=p.inp("g_qa", [1, QR]),
        w_qb=p.inp("w_qb", [QR, NH * QK]),
        w_kva=p.inp("w_kva", [D, KVR + RD]),
        g_kva=p.inp("g_kva", [1, KVR]),
        w_kvb=p.inp("w_kvb", [KVR, NH * (ND + VD)]),
        cos_own=p.inp("cos_own", [RD, TOK]),
        sin_own=p.inp("sin_own", [RD, TOK]),
        mask_rows=p.inp("mask_rows", [512, D], I32),
        Gmat=p.inp("Gmat", [128, 4]),
        ident_bf=p.inp("ident_bf", [128, 128], BF16),
        ident_f32=p.inp("ident_f32", [128, 128]),
        ones_f32=p.inp("ones_f32", [1, 128]),
        Qf=p.outp("Qf", [NH * QK, TOK], BF16),
        KN=p.outp("KN", [NH * ND, TOK], BF16),
        KR=p.outp("KR", [RD, TOK], BF16),
        Vt=p.outp("Vt", [TOK, NH * VD], BF16),
        bmp=p.outp("bmp", [16, 128]),
    )
    all_stores = emit_launch_A(p, io)
    with p.nc.Block() as blk:
        @blk.sync
        def _(s):
            for cs in all_stores:
                if cs.v:
                    s.wait_ge(cs.h, cs.v)
    return p


def emit_launch_B(p, io):
    nc = p.nc
    _es = ExitStack()
    def _sbl(name, shape, dt=F32):
        return _es.enter_context(
            nc.sbuf_tensor(name, shape, dt, allow_name_mangling=True))
    Qf = io["Qf"]
    KN_b = io["KN_b"]
    KR_b = io["KR_b"]
    Vt_b = io["Vt_b"]
    ident_bf_d = io["ident_bf"]
    attn = io["attn"]

    ident_bf = _sbl("b_ident", [128, 128], BF16)
    KR_sb = _sbl("b_KR", [RD, S], BF16)
    attn_sb = _sbl("b_attn", [128, NT * NH * VD], BF16)
    qhi = _sbl("b_qhi", [128, 2 * TOK], BF16)
    qlo = _sbl("b_qlo", [64, 2 * TOK], BF16)
    KN_h = _sbl("b_KN", [128, 2 * S], BF16)
    V_h = _sbl("b_V", [128, 2 * S], BF16)
    p_sb = _sbl("b_p", [128, S], BF16)
    pT_sb = _sbl("b_pT", [128, S], BF16)
    m_col = _sbl("b_m", [128, 1])
    bias_col = _sbl("b_bias", [128, 1])
    s8 = _sbl("b_s8", [128, 8])
    S_col = _sbl("b_S", [128, 2])
    tmpO = _sbl("b_tmpO", [128, VD])

    c_ld = p.sem()
    qld = [p.sem(), p.sem()]
    mmq = p.sem()
    dv = p.sem()
    ex = p.sem()
    tp_b = p.sem()
    evp = p.sem()
    mm_av = p.sem()
    dn = p.sem()
    st_at = p.sem()

    ps = _es.enter_context(nc.psum_tensor(
        "b_ps", [128, 4096], F32, allow_name_mangling=True))
    ptr_v = ps[:, 512:1536].bitcast(BF16)
    po = ps[:, 0:VD]

    NIT = NH * NT

    with nc.Block() as blk:
        @blk.sync
        def _(s):
            c_ld.inc(s.dma_start(out=ident_bf[:], in_=ident_bf_d), 16)
            c_ld.inc(s.dma_start(out=KR_sb[:], in_=KR_b), 16)

        @blk.gpsimd
        def _(g):
            for h in range(NH):
                par = h % 2
                if h >= 2:
                    g.wait_ge(mm_av.h, (h - 1) * NT)
                qld[par].inc(g.dma_start(
                    out=qhi[:, par * TOK:(par + 1) * TOK],
                    in_=Qf[h * QK:h * QK + ND, :]), 16)
                qld[par].inc(g.dma_start(
                    out=qlo[:, par * TOK:(par + 1) * TOK],
                    in_=Qf[h * QK + ND:(h + 1) * QK, :]), 16)
                qld[par].inc(g.dma_start(
                    out=KN_h[:, par * S:(par + 1) * S],
                    in_=KN_b[h * ND:(h + 1) * ND, :]), 16)
                vsrc = Vt_b.rearrange("(kt p) c -> p kt c", p=128)
                for j in range(8):
                    qld[par].inc(g.dma_start(
                        out=V_h[:, par * S + j * 512:par * S + (j + 1) * 512]
                            .rearrange("p (kt v) -> p kt v", kt=4),
                        in_=vsrc[:, 4 * j:4 * (j + 1), h * VD:(h + 1) * VD]), 16)

        @blk.tensor
        def _(t):
            t.wait_ge(c_ld.h, 32)
            for it in range(NIT):
                h, qt = divmod(it, NT)
                par = h % 2
                if qt == 0:
                    t.wait_ge(qld[par].h, 16 * 11 * (h // 2 + 1))
                if it >= 1:
                    t.wait_ge(ex.h, 8 * it)
                    t.wait_ge(evp.h, 4 * it)
                    t.wait_ge(dn.h, 3 * it)
                for c4 in range(8):
                    t.matmul(ps[:, c4 * 512:(c4 + 1) * 512],
                             qhi[:, par * TOK + qt * 128:par * TOK + (qt + 1) * 128],
                             KN_h[:, par * S + c4 * 512:par * S + (c4 + 1) * 512],
                             start=True, stop=False)
                    last = t.matmul(ps[:, c4 * 512:(c4 + 1) * 512],
                                    qlo[:, par * TOK + qt * 128:par * TOK + (qt + 1) * 128],
                                    KR_sb[:, c4 * 512:(c4 + 1) * 512],
                                    start=False, stop=True)
                mmq.inc(last)
                t.wait_ge(ex.h, 8 * (it + 1))
                for b in range(4):
                    if b >= 2:
                        t.wait_ge(evp.h, 4 * it + b - 1)
                    po_off = (b % 2) * 1024
                    for c8 in range(8):
                        kt = b * 8 + c8
                        last = t.transpose(
                            ptr_v[:, po_off + c8 * 128:po_off + (c8 + 1) * 128],
                            p_sb[:, kt * 128:(kt + 1) * 128],
                            ident_bf[:])
                    tp_b.inc(last)
                for b in range(4):
                    t.wait_ge(evp.h, 4 * it + b + 1)
                    for c8 in range(8):
                        kt = b * 8 + c8
                        last = t.matmul(
                            po, pT_sb[:, kt * 128:(kt + 1) * 128],
                            V_h[:, par * S + kt * 128:par * S + (kt + 1) * 128],
                            start=(kt == 0), stop=(kt == 31))
                mm_av.inc(last)

        @blk.vector
        def _(v):
            for it in range(NIT):
                h, qt = divmod(it, NT)
                v.wait_ge(mmq.h, it + 1)
                if it >= 1:
                    v.wait_ge(dv.h, 2 * it)
                dv.inc(v.tensor_reduce(m_col[:], ps[:, :], AX.X, ALU.max,
                                       negate=True))
                v.wait_ge(dv.h, 2 * it + 1)
                dv.inc(v.tensor_scalar_mul(bias_col[:], m_col[:], SM_SCALE))
                if it >= 1:
                    v.wait_ge(mm_av.h, it)
                for b in range(4):
                    v.wait_ge(tp_b.h, 4 * it + b + 1)
                    po_off = (b % 2) * 1024
                    evp.inc(v.tensor_copy(
                        pT_sb[:, b * 1024:(b + 1) * 1024],
                        ptr_v[:, po_off:po_off + 1024]))
                v.wait_ge(ex.h, 8 * (it + 1))
                if it >= 1:
                    v.wait_ge(dn.h, 3 * it - 1)
                dn.inc(v.tensor_reduce(S_col[:, 0:1], s8[:], AX.X, ALU.add))
                v.wait_ge(dn.h, 3 * it + 1)
                dn.inc(v.reciprocal(S_col[:, 1:2], S_col[:, 0:1]))
                v.wait_ge(dn.h, 3 * it + 2)
                v.wait_ge(mm_av.h, it + 1)
                dn.inc(v.tensor_scalar_mul(
                    attn_sb[:, qt * NH * VD + h * VD:qt * NH * VD + (h + 1) * VD],
                    po, S_col[:, 1:2]))

        @blk.scalar
        def _(s):
            for it in range(NIT):
                h, qt = divmod(it, NT)
                s.wait_ge(dv.h, 2 * (it + 1))
                if it >= 1:
                    s.wait_ge(tp_b.h, 4 * it)
                    s.wait_ge(dn.h, 3 * it - 2)
                for c4 in range(8):
                    ex.inc(s.activation(
                        p_sb[:, c4 * 512:(c4 + 1) * 512],
                        ps[:, c4 * 512:(c4 + 1) * 512],
                        AF.Exp, bias=bias_col[:], scale=SM_SCALE,
                        accum_out=s8[:, c4:c4 + 1]))

        @blk.vector
        def _(v):
            v.wait_ge(dn.h, 3 * NIT)
            v.engine_nop()

        @blk.scalar
        def _(s):
            s.wait_ge(dn.h, 3 * NIT)
            for qt in range(NT):
                st_at.inc(s.dma_start(
                    out=attn[qt * 128:(qt + 1) * 128, :],
                    in_=attn_sb[:, qt * NH * VD:(qt + 1) * NH * VD]), 16)

    _es.close()
    return [st_at]


def emit_launch_B2(p, io):
    nc = p.nc
    _es = ExitStack()

    def _sbl(name, shape, dt=F32):
        return _es.enter_context(
            nc.sbuf_tensor(name, shape, dt, allow_name_mangling=True))

    Qf = io["Qf"]
    KN_b = io["KN_b"]
    KR_b = io["KR_b"]
    Vt_b = io["Vt_b"]
    ident_bf_d = io["ident_bf"]
    attn = io["attn"]

    ident_bf = _sbl("b2_ident", [128, 128], BF16)
    KR_sb = _sbl("b2_KR", [RD, S], BF16)
    attn_sb = _sbl("b2_attn", [128, NT * NH * VD], BF16)
    qhi = _sbl("b2_qhi", [128, 2 * TOK], BF16)
    qlo = _sbl("b2_qlo", [64, 2 * TOK], BF16)
    KN_h = _sbl("b2_KN", [128, 2 * S], BF16)
    V_h = _sbl("b2_V", [128, 2 * S], BF16)
    p_sb = _sbl("b2_p", [128, S], BF16)
    pT_sb = _sbl("b2_pT", [128, S], BF16)
    m8 = _sbl("b2_m8", [128, 8])
    mcols = _sbl("b2_m", [128, 4])
    wcol = _sbl("b2_w", [128, 2])
    bias2 = _sbl("b2_bias", [128, 2])
    s8 = _sbl("b2_s8", [128, 8])
    fcol = _sbl("b2_f", [128, 2])
    dcol = _sbl("b2_d", [128, 2])
    scol = _sbl("b2_s", [128, 2])
    Sc = _sbl("b2_S", [128, 2])
    negM = _sbl("b2_negM", [128, 1])
    tmpO = _sbl("b2_t0", [128, VD])
    tmpO2 = _sbl("b2_t1", [128, VD])

    c_ld = p.sem()
    qld = [p.sem(), p.sem()]
    mmq = p.sem()
    dvr = p.sem()
    dvt = p.sem()
    ex = p.sem()
    fx = p.sem()
    tpb = p.sem()
    evp = p.sem()
    mm_av = p.sem()
    dn = p.sem()
    st_at = p.sem()

    ps4 = _es.enter_context(nc.psum_tensor(
        "b2_ps", [128, 2048], F32, allow_name_mangling=True))
    ptr2 = _es.enter_context(nc.psum_tensor(
        "b2_ptr", [128, 2048], BF16, allow_name_mangling=True))
    poA = _es.enter_context(nc.psum_tensor(
        "b2_poA", [128, VD], F32, allow_name_mangling=True))
    poB = _es.enter_context(nc.psum_tensor(
        "b2_poB", [128, VD], F32, allow_name_mangling=True))
    po = [poA, poB]

    NIT = NH * NT
    NRD = NIT * 2

    with nc.Block() as blk:
        @blk.sync
        def _(s):
            c_ld.inc(s.dma_start(out=ident_bf[:], in_=ident_bf_d), 16)
            c_ld.inc(s.dma_start(out=KR_sb[:], in_=KR_b), 16)

        @blk.gpsimd
        def _(g):
            for h in range(NH):
                par = h % 2
                if h >= 2:
                    g.wait_ge(mm_av.h, (h - 1) * NT * 2)
                qld[par].inc(g.dma_start(
                    out=qhi[:, par * TOK:(par + 1) * TOK],
                    in_=Qf[h * QK:h * QK + ND, :]), 16)
                qld[par].inc(g.dma_start(
                    out=qlo[:, par * TOK:(par + 1) * TOK],
                    in_=Qf[h * QK + ND:(h + 1) * QK, :]), 16)
                qld[par].inc(g.dma_start(
                    out=KN_h[:, par * S:(par + 1) * S],
                    in_=KN_b[h * ND:(h + 1) * ND, :]), 16)
                vsrc = Vt_b.rearrange("(kt p) c -> p kt c", p=128)
                for j in range(8):
                    qld[par].inc(g.dma_start(
                        out=V_h[:, par * S + j * 512:par * S + (j + 1) * 512]
                            .rearrange("p (kt v) -> p kt v", kt=4),
                        in_=vsrc[:, 4 * j:4 * (j + 1), h * VD:(h + 1) * VD]), 16)

        def emit_T(t, rd):
            it, r = divmod(rd, 2)
            for b in range(2):
                gb = 2 * rd + b
                if gb >= 2:
                    t.wait_ge(evp.h, gb - 1)
                t.wait_ge(ex.h, 4 * rd + 2 * (b + 1))
                po_off = (gb % 2) * 1024
                last = None
                for c8 in range(8):
                    kt = b * 8 + c8
                    last = t.transpose(
                        ptr2[:, po_off + c8 * 128:po_off + (c8 + 1) * 128],
                        p_sb[:, r * 2048 + kt * 128:r * 2048 + (kt + 1) * 128],
                        ident_bf[:])
                tpb.inc(last)

        def emit_AV(t, rd):
            it, r = divmod(rd, 2)
            h, qt = divmod(it, NT)
            par = h % 2
            t.wait_ge(dn.h, 10 * it)
            last = None
            for b in range(2):
                t.wait_ge(evp.h, 2 * rd + b + 1)
                for c8 in range(8):
                    kt = b * 8 + c8
                    last = t.matmul(
                        po[r][:],
                        pT_sb[:, r * 2048 + kt * 128:r * 2048 + (kt + 1) * 128],
                        V_h[:, par * S + r * 2048 + kt * 128:
                            par * S + r * 2048 + (kt + 1) * 128],
                        start=(kt == 0), stop=(kt == 15))
            mm_av.inc(last)

        @blk.tensor
        def _(t):
            t.wait_ge(c_ld.h, 32)
            for rd in range(NRD):
                it, r = divmod(rd, 2)
                h, qt = divmod(it, NT)
                par = h % 2
                if r == 0 and qt == 0:
                    t.wait_ge(qld[par].h, 16 * 11 * (h // 2 + 1))
                for c4 in range(4):
                    if rd >= 1:
                        t.wait_ge(ex.h, 4 * (rd - 1) + c4 + 1)
                    ko = r * 2048 + c4 * 512
                    t.matmul(ps4[:, c4 * 512:(c4 + 1) * 512],
                             qhi[:, par * TOK + qt * 128:par * TOK + (qt + 1) * 128],
                             KN_h[:, par * S + ko:par * S + ko + 512],
                             start=True, stop=False)
                    last = t.matmul(ps4[:, c4 * 512:(c4 + 1) * 512],
                                    qlo[:, par * TOK + qt * 128:par * TOK + (qt + 1) * 128],
                                    KR_sb[:, ko:ko + 512],
                                    start=False, stop=True)
                    mmq.inc(last)
                if rd >= 1:
                    emit_T(t, rd - 1)
                    emit_AV(t, rd - 1)
            emit_T(t, NRD - 1)
            emit_AV(t, NRD - 1)

        @blk.vector
        def _(v):
            def emit_evacs(prd):
                it1, r1 = divmod(prd, 2)
                for b in range(2):
                    gb = 2 * prd + b
                    v.wait_ge(tpb.h, gb + 1)
                    if prd >= 2:
                        v.wait_ge(mm_av.h, prd - 1)
                    po_off = (gb % 2) * 1024
                    evp.inc(v.tensor_copy(
                        pT_sb[:, r1 * 2048 + b * 1024:r1 * 2048 + (b + 1) * 1024],
                        ptr2[:, po_off:po_off + 1024]))

            def part_negM(it):
                sl = (it % 2) * 2
                v.wait_ge(dvt.h, 4 * it + 4)
                if it >= 1:
                    v.wait_ge(dn.h, 10 * (it - 1) + 2)
                dn.inc(v.tensor_tensor(negM[:], mcols[:, sl:sl + 1],
                                       mcols[:, sl + 1:sl + 2], ALU.min))
                v.wait_ge(dn.h, 10 * it + 1)
                v.wait_ge(fx.h, it)
                dn.inc(v.tensor_scalar(dcol[:], mcols[:, sl:sl + 2],
                                       negM[:, 0:1], None, ALU.subtract))

            def part_scols(q):
                v.wait_ge(dn.h, 10 * q)
                v.wait_ge(ex.h, 8 * (q + 1))
                dn.inc(v.tensor_reduce(scol[:, 0:1], s8[:, 0:4], AX.X, ALU.add))
                v.wait_ge(dn.h, 10 * q + 3)
                dn.inc(v.tensor_reduce(scol[:, 1:2], s8[:, 4:8], AX.X, ALU.add))

            def part_wSrS(q):
                v.wait_ge(dn.h, 10 * q + 4)
                v.wait_ge(fx.h, q + 1)
                dn.inc(v.tensor_mul(wcol[:], fcol[:], scol[:]))
                v.wait_ge(dn.h, 10 * q + 5)
                dn.inc(v.tensor_reduce(Sc[:, 0:1], wcol[:], AX.X, ALU.add))
                v.wait_ge(dn.h, 10 * q + 6)
                dn.inc(v.reciprocal(Sc[:, 1:2], Sc[:, 0:1]))
                v.wait_ge(dn.h, 10 * q + 7)
                v.wait_ge(mm_av.h, 2 * q + 1)
                dn.inc(v.tensor_scalar_mul(tmpO[:], poA[:, :], fcol[:, 0:1]))

            def part_t1out(q):
                v.wait_ge(dn.h, 10 * q + 8)
                v.wait_ge(mm_av.h, 2 * q + 2)
                dn.inc(v.scalar_tensor_tensor(tmpO2[:], poB[:, :], fcol[:, 1:2],
                                              tmpO[:], ALU.mult, ALU.add))
                v.wait_ge(dn.h, 10 * q + 9)
                h, qt = divmod(q, NT)
                dn.inc(v.tensor_scalar_mul(
                    attn_sb[:, qt * NH * VD + h * VD:qt * NH * VD + (h + 1) * VD],
                    tmpO2[:], Sc[:, 1:2]))

            for rd in range(NRD):
                it, r = divmod(rd, 2)
                if r == 0 and it >= 1:
                    part_scols(it - 1)
                if r == 1 and it >= 1:
                    part_t1out(it - 1)
                for c4 in range(4):
                    v.wait_ge(mmq.h, 4 * rd + c4 + 1)
                    if rd >= 2:
                        v.wait_ge(dvt.h, 2 * (rd - 1))
                    dvr.inc(v.tensor_reduce(m8[:, r * 4 + c4:r * 4 + c4 + 1],
                                            ps4[:, c4 * 512:(c4 + 1) * 512],
                                            AX.X, ALU.max, negate=True))
                sl = (it % 2) * 2
                v.wait_ge(dvr.h, 4 * (rd + 1))
                if it >= 2:
                    v.wait_ge(dn.h, 10 * (it - 2) + 2)
                dvt.inc(v.tensor_reduce(mcols[:, sl + r:sl + r + 1],
                                        m8[:, r * 4:r * 4 + 4],
                                        AX.X, ALU.min))
                v.wait_ge(dvt.h, 2 * rd + 1)
                if rd >= 2:
                    v.wait_ge(ex.h, 4 * (rd - 1))
                dvt.inc(v.tensor_scalar_mul(bias2[:, r:r + 1],
                                            mcols[:, sl + r:sl + r + 1],
                                            SM_SCALE))
                if r == 1:
                    part_negM(it)
                if rd >= 1:
                    emit_evacs(rd - 1)
                if r == 0 and it >= 1:
                    part_wSrS(it - 1)
            emit_evacs(NRD - 1)
            part_scols(NIT - 1)
            part_wSrS(NIT - 1)
            part_t1out(NIT - 1)

        @blk.scalar
        def _(s):
            for rd in range(NRD):
                it, r = divmod(rd, 2)
                s.wait_ge(dvt.h, 2 * rd + 2)
                if rd >= 2:
                    s.wait_ge(tpb.h, 2 * (rd - 1))
                if it >= 1 and r == 0:
                    s.wait_ge(dn.h, 10 * (it - 1) + 4)
                for c4 in range(4):
                    ex.inc(s.activation(
                        p_sb[:, r * 2048 + c4 * 512:r * 2048 + (c4 + 1) * 512],
                        ps4[:, c4 * 512:(c4 + 1) * 512],
                        AF.Exp, bias=bias2[:, r:r + 1], scale=SM_SCALE,
                        accum_out=s8[:, r * 4 + c4:r * 4 + c4 + 1]))
                if r == 1:
                    it_ = it
                    s.wait_ge(dn.h, 10 * it_ + 2)
                    fx.inc(s.activation(fcol[:], dcol[:], AF.Exp,
                                        scale=-SM_SCALE))

        @blk.scalar
        def _(s):
            s.wait_ge(dn.h, 10 * NIT)
            for qt in range(NT):
                st_at.inc(s.dma_start(
                    out=attn[qt * 128:(qt + 1) * 128, :],
                    in_=attn_sb[:, qt * NH * VD:(qt + 1) * NH * VD]), 16)

    _es.close()
    return [st_at]


def build_launch_B():
    p = Prog()
    io = dict(
        Qf=p.inp("Qf", [NH * QK, TOK], BF16),
        KN_b=p.inp("KN_b", [NH * ND, S], BF16),
        KR_b=p.inp("KR_b", [RD, S], BF16),
        Vt_b=p.inp("Vt_b", [S, NH * VD], BF16),
        ident_bf=p.inp("ident_bf", [128, 128], BF16),
        attn=p.outp("attn", [TOK, NH * VD], BF16),
    )
    sts = emit_launch_B(p, io)
    with p.nc.Block() as blk:
        @blk.sync
        def _(s):
            for cs in sts:
                s.wait_ge(cs.h, cs.v)
    return p


def run_launch_B(resA, consts=None):
    consts = consts or host_consts()
    in_maps = []
    for c in range(NCORES):
        b = c // 4
        grp = range(b * 4, b * 4 + 4)
        KN_full = np.concatenate([resA[g]["KN"] for g in grp], axis=1)
        KR_full = np.concatenate([resA[g]["KR"] for g in grp], axis=1)
        Vt_full = np.concatenate([resA[g]["Vt"] for g in grp], axis=0)
        in_maps.append(dict(
            Qf=resA[c]["Qf"], KN_b=KN_full, KR_b=KR_full, Vt_b=Vt_full,
            ident_bf=consts["ident_bf"],
        ))
    pB = build_launch_B()
    res = run_bass_kernel_spmd(pB.nc, in_maps, core_ids=list(range(NCORES)))
    return res.results


def emit_loadT(p, nc, src_dram, width, dstT, ident_bf, cast, entry_waits=(),
               tag="lt"):
    ld = [p.sem(), p.sem()]
    tp = p.sem()
    ev = p.sem()
    nch = width // 128
    nb = (nch + 7) // 8
    with nc.sbuf_tensor(f"{tag}_buf", [128, 2 * width], BF16, allow_name_mangling=True) as buf, \
         nc.psum_tensor(f"{tag}_ptr", [128, 2048], BF16, allow_name_mangling=True) as ptr, \
         nc.Block() as blk:
        @blk.gpsimd
        def _(g):
            for ctr, val in entry_waits:
                g.wait_ge(ctr.h, val)
            for tt in range(NT):
                if tt >= 2:
                    g.wait_ge(tp.h, (tt - 1) * nb)
                ld[tt % 2].inc(g.dma_start(
                    out=buf[:, (tt % 2) * width:(tt % 2 + 1) * width],
                    in_=src_dram[tt * 128:(tt + 1) * 128, :]), 16)

        @blk.tensor
        def _(t):
            for ctr, val in entry_waits:
                t.wait_ge(ctr.h, val)
            for tt in range(NT):
                t.wait_ge(ld[tt % 2].h, 16 * (tt // 2 + 1))
                for b in range(nb):
                    n_in_b = min(8, nch - b * 8)
                    idx = tt * nb + b
                    if idx >= 2:
                        t.wait_ge(ev.h, idx - 1)
                    po = (idx % 2) * 1024
                    last = None
                    for c8 in range(n_in_b):
                        c = b * 8 + c8
                        last = t.transpose(
                            ptr[:, po + c8 * 128:po + (c8 + 1) * 128],
                            buf[:, (tt % 2) * width + c * 128:(tt % 2) * width + (c + 1) * 128],
                            ident_bf[:])
                    tp.inc(last)

        @blk.vector
        def _(v):
            for idx in range(NT * nb):
                tt, b = divmod(idx, nb)
                n_in_b = min(8, nch - b * 8)
                v.wait_ge(tp.h, idx + 1)
                po = (idx % 2) * 1024
                ev.inc(v.tensor_copy(chunks_ap(dstT, TOK, b * 8, tt, n_in_b),
                                     ptr[:, po:po + n_in_b * 128]))
    return ld, tp, ev


def emit_ln(p, nc, y_dram, ysum_dram, out_f32_dram, dstT, ident_bf,
            entry_waits=(), tag="ln"):
    yld = [p.sem(), p.sem()]
    sld = p.sem()
    sq_v = p.sem()
    sq_s = p.sem()
    dvm = p.sem()
    dvr = p.sem()
    evn = p.sem()
    evc = p.sem()
    tp = p.sem()
    evT = p.sem()
    st_f = [p.sem(), p.sem()]
    nch = D // 128
    nb = nch // 8
    with nc.sbuf_tensor(f"{tag}_y", [128, 2 * D], F32, allow_name_mangling=True) as ybuf, \
         nc.sbuf_tensor(f"{tag}_scr", [128, D], F32, allow_name_mangling=True) as scr, \
         nc.sbuf_tensor(f"{tag}_x", [128, D], F32, allow_name_mangling=True) as xbuf, \
         nc.sbuf_tensor(f"{tag}_xb", [128, D], BF16, allow_name_mangling=True) as xbbuf, \
         nc.sbuf_tensor(f"{tag}_ys", [128, 8], F32, allow_name_mangling=True) as ys, \
         nc.sbuf_tensor(f"{tag}_st", [128, 4 * 8], F32, allow_name_mangling=True) as stt, \
         nc.sbuf_tensor(f"{tag}_eps", [128, 1], F32, allow_name_mangling=True) as eps_col, \
         nc.psum_tensor(f"{tag}_ptr", [128, 2048], BF16, allow_name_mangling=True) as ptr, \
         nc.Block() as blk:
        @blk.sync
        def _(s):
            for ctr, val in entry_waits:
                s.wait_ge(ctr.h, val)
            sld.inc(s.dma_start(out=ys[:, :], in_=ysum_dram), 16)
            for tt in range(NT):
                if tt >= 2:
                    s.wait_ge(evn.h, tt - 1)
                yld[tt % 2].inc(s.dma_start(
                    out=ybuf[:, (tt % 2) * D:(tt % 2 + 1) * D],
                    in_=y_dram[tt * 128:(tt + 1) * 128, :]), 16)

        @blk.vector
        def _(v):
            v.wait_ge(sld.h, 16)
            dvm.inc(v.memset(eps_col[:, :], EPS_LN))
            for tt in range(NT):
                dvm.inc(v.tensor_scalar_mul(stt[:, tt * 4:tt * 4 + 1],
                                            ys[:, tt:tt + 1], -1.0 / D))

        @blk.scalar
        def _(s):
            def emit_store(tt):
                s.wait_ge(evn.h, tt + 1)
                if dstT is not None:
                    s.wait_ge(evc.h, tt + 1)
                st_f[tt % 2].inc(s.dma_start(
                    out=out_f32_dram[tt * 128:(tt + 1) * 128, :],
                    in_=xbuf[:, :]), 16)

            for tt in range(NT):
                s.wait_ge(dvm.h, tt + 2)
                s.wait_ge(yld[tt % 2].h, 16 * (tt // 2 + 1))
                if tt >= 1:
                    s.wait_ge(sq_v.h, tt)
                sq_v.inc(s.activation(scr[:, :],
                                      ybuf[:, (tt % 2) * D:(tt % 2 + 1) * D],
                                      AF.Square, bias=stt[:, tt * 4:tt * 4 + 1],
                                      accum_out=stt[:, tt * 4 + 1:tt * 4 + 2]))
                s.wait_ge(sq_v.h, tt + 1)
                sq_s.inc(s.activation(stt[:, tt * 4 + 2:tt * 4 + 3],
                                      stt[:, tt * 4 + 1:tt * 4 + 2],
                                      AF.Sqrt, bias=eps_col[:, :], scale=1.0 / D))
                if out_f32_dram is not None and tt >= 1:
                    emit_store(tt - 1)
            if out_f32_dram is not None:
                emit_store(NT - 1)

        @blk.vector
        def _(v):
            def emit_evacT(tt):
                for b in range(nb):
                    idx = tt * nb + b
                    v.wait_ge(tp.h, idx + 1)
                    po = (idx % 2) * 1024
                    evT.inc(v.tensor_copy(chunks_ap(dstT, TOK, b * 8, tt, 8),
                                          ptr[:, po:po + 1024]))

            for tt in range(NT):
                v.wait_ge(sq_s.h, tt + 1)
                dvr.inc(v.reciprocal(stt[:, tt * 4 + 3:tt * 4 + 4],
                                     stt[:, tt * 4 + 2:tt * 4 + 3]))
                v.wait_ge(dvr.h, tt + 1)
                if tt >= 1:
                    if out_f32_dram is not None:
                        v.wait_ge(st_f[(tt - 1) % 2].h,
                                  16 * ((tt - 1) // 2 + 1))
                    if dstT is not None:
                        v.wait_ge(evc.h, tt)
                evn.inc(v.tensor_scalar(
                    xbuf[:, :],
                    ybuf[:, (tt % 2) * D:(tt % 2 + 1) * D],
                    stt[:, tt * 4:tt * 4 + 1], stt[:, tt * 4 + 3:tt * 4 + 4],
                    ALU.add, ALU.mult))
                if dstT is not None:
                    if tt >= 1:
                        emit_evacT(tt - 1)
                    v.wait_ge(evn.h, tt + 1)
                    if tt >= 1:
                        v.wait_ge(tp.h, tt * nb)
                    evc.inc(v.tensor_copy(xbbuf[:, :], xbuf[:, :]))
            if dstT is not None:
                emit_evacT(NT - 1)

        if dstT is not None:
            @blk.tensor
            def _(t):
                for ctr, val in entry_waits:
                    t.wait_ge(ctr.h, val)
                for tt in range(NT):
                    t.wait_ge(evc.h, tt + 1)
                    for b in range(nb):
                        idx = tt * nb + b
                        if idx >= 2:
                            t.wait_ge(evT.h, idx - 1)
                        po = (idx % 2) * 1024
                        last = None
                        for c8 in range(8):
                            c = b * 8 + c8
                            last = t.transpose(
                                ptr[:, po + c8 * 128:po + (c8 + 1) * 128],
                                xbbuf[:, c * 128:(c + 1) * 128],
                                ident_bf[:])
                        tp.inc(last)

    return evn, evT, st_f, tp


def emit_mm_res(p, nc, lhsT_cache, nkc, w_dram, dout, xres_dram, y_dram,
                ysum_dram, ones_bf, bias_sb=None, entry_waits=(),
                extra_mm=None, extra_waits=(), mask_fn=None, psum_ref=None,
                tag="mm"):
    w_ld = [p.sem() for _ in range(4)]
    x_ld = [p.sem() for _ in range(4)]
    w_fr = p.sem()
    msk = p.sem()
    mme = p.sem()
    eve = p.sem()
    dvs = p.sem()
    sty = [p.sem() for _ in range(4)]
    sts = p.sem()
    ndg = dout // 512
    has_tail = (bias_sb is not None) or (extra_mm is not None)
    with nc.sbuf_tensor(f"{tag}_w", [128, 4 * 512], BF16, allow_name_mangling=True) as wbuf, \
         nc.sbuf_tensor(f"{tag}_wr", [128, 4 * 512], F32, allow_name_mangling=True) as wraw, \
         nc.sbuf_tensor(f"{tag}_xr", [128, 4 * 512], F32, allow_name_mangling=True) as xrbuf, \
         nc.sbuf_tensor(f"{tag}_yb", [128, 4 * 512], F32, allow_name_mangling=True) as ybuf, \
         nc.sbuf_tensor(f"{tag}_sp", [128, NT * 8], F32, allow_name_mangling=True) as spart, \
         nc.sbuf_tensor(f"{tag}_ys", [128, 8], F32, allow_name_mangling=True) as ysb, \
         nc.psum_tensor(f"{tag}_pq", [128, 4096], F32, allow_name_mangling=True) as pq, \
         nc.Block() as blk:
        if psum_ref is not None:
            psum_ref[0] = pq

        @blk.sync
        def _(s):
            for ctr, val in entry_waits:
                s.wait_ge(ctr.h, val)
            for dg in range(ndg):
                for kc in range(nkc):
                    idx = dg * nkc + kc
                    if idx >= 4:
                        s.wait_ge(msk.h, idx - 3)
                    w_ld[idx % 4].inc(s.dma_start(
                        out=wraw[:, (idx % 4) * 512:(idx % 4 + 1) * 512],
                        in_=w_dram[kc * 128:(kc + 1) * 128,
                                   dg * 512:(dg + 1) * 512]), 16)


        @blk.tensor
        def _(t):
            for ctr, val in entry_waits:
                t.wait_ge(ctr.h, val)
            for ctr, val in extra_waits:
                t.wait_ge(ctr.h, val)
            for dg in range(ndg):
                if dg > 0:
                    t.wait_ge(eve.h, dg * NT)
                for kc in range(nkc):
                    idx = dg * nkc + kc
                    t.wait_ge(msk.h, idx + 1)
                    last = None
                    for tt in range(NT):
                        last = t.matmul(
                            pq[:, tt * 512:(tt + 1) * 512],
                            chunk_ap(lhsT_cache, TOK, kc, tt * 128, 128),
                            wbuf[:, (idx % 4) * 512:(idx % 4 + 1) * 512],
                            start=(kc == 0),
                            stop=(kc == nkc - 1 and not has_tail))
                    w_fr.inc(last)
                if has_tail:
                    last = None
                    for tt in range(NT):
                        if bias_sb is not None:
                            last = t.matmul(
                                pq[:, tt * 512:(tt + 1) * 512],
                                ones_bf[0:1, 0:128],
                                bias_sb[0:1, dg * 512:(dg + 1) * 512],
                                start=False, stop=(extra_mm is None))
                        if extra_mm is not None:
                            last = extra_mm(t, dg, tt)
                    mme.inc(last)

        @blk.vector
        def _(v):
            def emit_mask(dg, kc):
                idx = dg * nkc + kc
                v.wait_ge(w_ld[idx % 4].h, 16 * (idx // 4 + 1))
                if idx >= 4:
                    v.wait_ge(w_fr.h, idx - 3)
                if mask_fn is not None:
                    msk.inc(v.tensor_tensor(
                        wbuf[:, (idx % 4) * 512:(idx % 4 + 1) * 512],
                        wraw[:, (idx % 4) * 512:(idx % 4 + 1) * 512],
                        mask_fn(dg, kc), ALU.mult))
                else:
                    msk.inc(v.tensor_copy(
                        wbuf[:, (idx % 4) * 512:(idx % 4 + 1) * 512],
                        wraw[:, (idx % 4) * 512:(idx % 4 + 1) * 512]))

            def emit_evac(dg):
                for tt in range(NT):
                    idx = dg * NT + tt
                    v.wait_ge(w_fr.h, (dg + 1) * nkc)
                    if has_tail:
                        v.wait_ge(mme.h, dg + 1)
                    v.wait_ge(x_ld[idx % 4].h, 16 * (idx // 4 + 1))
                    if idx >= 4:
                        v.wait_ge(sty[idx % 4].h, 16 * (idx // 4))
                    eve.inc(v.scalar_tensor_tensor(
                        ybuf[:, (idx % 4) * 512:(idx % 4 + 1) * 512],
                        pq[:, tt * 512:(tt + 1) * 512], 1.0,
                        xrbuf[:, (idx % 4) * 512:(idx % 4 + 1) * 512],
                        ALU.mult, ALU.add,
                        accum_out=spart[:, tt * 8 + dg:tt * 8 + dg + 1]))

            for dg in range(ndg):
                emit_mask(dg, 0)
                emit_mask(dg, 1)
                if dg >= 1:
                    emit_evac(dg - 1)
                for kc in range(2, nkc):
                    emit_mask(dg, kc)
            emit_evac(ndg - 1)
            for tt in range(NT):
                v.wait_ge(eve.h, (ndg - 1) * NT + tt + 1)
                dvs.inc(v.tensor_reduce(ysb[:, tt:tt + 1],
                                        spart[:, tt * 8:tt * 8 + ndg],
                                        AX.X, ALU.add))

        @blk.scalar
        def _(s):
            def xload(idx):
                dg, tt = divmod(idx, NT)
                x_ld[idx % 4].inc(s.dma_start(
                    out=xrbuf[:, (idx % 4) * 512:(idx % 4 + 1) * 512],
                    in_=xres_dram[tt * 128:(tt + 1) * 128,
                                  dg * 512:(dg + 1) * 512]), 16)
            for ctr, val in entry_waits:
                s.wait_ge(ctr.h, val)
            for j in range(4):
                xload(j)
            for dg in range(ndg):
                for tt in range(NT):
                    idx = dg * NT + tt
                    s.wait_ge(eve.h, idx + 1)
                    sty[idx % 4].inc(s.dma_start(
                        out=y_dram[tt * 128:(tt + 1) * 128,
                                   dg * 512:(dg + 1) * 512],
                        in_=ybuf[:, (idx % 4) * 512:(idx % 4 + 1) * 512]), 16)
                    if idx + 4 < ndg * NT:
                        xload(idx + 4)
            s.wait_ge(dvs.h, NT)
            sts.inc(s.dma_start(out=ysum_dram, in_=ysb[:, :]), 16)
    return w_fr, mme, eve, sty, sts


def run_launch_A(inputs, consts=None):
    consts = consts or host_consts()
    x_full = inputs["L0_attn_hidden_states"]
    mask = inputs["L1_attn_mask"]
    in_maps = []
    for c in range(NCORES):
        b, off = divmod(c, 4)
        pos0 = off * TOK
        in_maps.append(dict(
            xT=np.ascontiguousarray(x_full[b, pos0:pos0 + TOK, :].T),
            w_qa=inputs["L0_attn_w_qa"],
            g_qa=inputs["L0_attn_g_qa"].reshape(1, -1),
            w_qb=inputs["L0_attn_w_qb"],
            w_kva=inputs["L0_attn_w_kva"],
            g_kva=inputs["L0_attn_g_kva"].reshape(1, -1),
            w_kvb=inputs["L0_attn_w_kvb"],
            cos_own=np.ascontiguousarray(consts["cos_fm"][:, pos0:pos0 + TOK]),
            sin_own=np.ascontiguousarray(consts["sin_fm"][:, pos0:pos0 + TOK]),
            mask_rows=np.ascontiguousarray(mask[c * 512:(c + 1) * 512, :]),
            Gmat=consts["Gmat"],
            ident_bf=consts["ident_bf"],
            ident_f32=consts["ident_f32"],
            ones_f32=consts["ones_f32"],
        ))
    pA = build_launch_A()
    res = run_bass_kernel_spmd(pA.nc, in_maps, core_ids=list(range(NCORES)))
    return res.results



def emit_launch_C1(p, io):
    nc = p.nc
    _es = ExitStack()
    def _sbl(name, shape, dt=F32):
        return _es.enter_context(
            nc.sbuf_tensor(name, shape, dt, allow_name_mangling=True))
    attn_d = io["attn"]
    x_d = io["x"]
    w_o = io["w_o"]
    ident_bf_d = io["ident_bf"]
    y1 = io["y1"]
    ysum1 = io["ysum1"]

    ident_bf = _sbl("c1_ident", [128, 128], BF16)
    attnT = _sbl("c1_attnT", [128, 32 * TOK], BF16)
    c_ld = p.sem()
    with nc.Block() as blk:
        @blk.sync
        def _(s):
            c_ld.inc(s.dma_start(out=ident_bf[:], in_=ident_bf_d), 16)

    ldA, tpA, evA = emit_loadT(p, nc, attn_d, NH * VD, attnT, ident_bf, False,
                               entry_waits=[(c_ld, 16)], tag="c1lt")
    w_fr, mme, eve, sty, sts = emit_mm_res(
        p, nc, attnT, 32, w_o, D, x_d, y1, ysum1, None,
        entry_waits=[(evA, NT * 4)], tag="c1mm")
    _es.close()
    return sty + [sts]


def build_launch_C1():
    p = Prog()
    io = dict(
        attn=p.inp("attn", [TOK, NH * VD], BF16),
        x=p.inp("x", [TOK, D], F32),
        w_o=p.inp("w_o", [D, D], F32),
        ident_bf=p.inp("ident_bf", [128, 128], BF16),
        y1=p.outp("y1", [TOK, D], F32),
        ysum1=p.outp("ysum1", [128, NT], F32),
    )
    sts = emit_launch_C1(p, io)
    with p.nc.Block() as blk:
        @blk.sync
        def _(s):
            for cs in sts:
                s.wait_ge(cs.h, cs.v)
    return p


def run_launch_C1(inputs, resB, consts=None):
    consts = consts or host_consts()
    x_full = inputs["L0_attn_hidden_states"]
    in_maps = []
    for c in range(NCORES):
        b, off = divmod(c, 4)
        pos0 = off * TOK
        in_maps.append(dict(
            attn=resB[c]["attn"],
            x=np.ascontiguousarray(x_full[b, pos0:pos0 + TOK, :]),
            w_o=inputs["L0_attn_w_o"],
            ident_bf=consts["ident_bf"],
        ))
    pC = build_launch_C1()
    res = run_bass_kernel_spmd(pC.nc, in_maps, core_ids=list(range(NCORES)))
    return res.results


def emit_launch_C5(p, io):
    nc = p.nc
    evn, evT, st_f, tp = emit_ln(p, nc, io["y4"], io["ysum4"], io["out"],
                                 None, None, tag="c5ln")
    return st_f


def build_launch_C5():
    p = Prog()
    io = dict(
        y4=p.inp("y4", [TOK, D], F32),
        ysum4=p.inp("ysum4", [128, NT], F32),
        out=p.outp("out", [TOK, D], F32),
    )
    sts = emit_launch_C5(p, io)
    with p.nc.Block() as blk:
        @blk.sync
        def _(s):
            for cs in sts:
                if cs.v:
                    s.wait_ge(cs.h, cs.v)
    return p


def run_launch_C5(resC4):
    in_maps = [dict(y4=resC4[c]["y4"], ysum4=resC4[c]["ysum4"])
               for c in range(NCORES)]
    pC = build_launch_C5()
    res = run_bass_kernel_spmd(pC.nc, in_maps, core_ids=list(range(NCORES)))
    return res.results


def emit_launch_C2(p, io):
    nc = p.nc
    _es = ExitStack()
    def _sbl(name, shape, dt=F32):
        return _es.enter_context(
            nc.sbuf_tensor(name, shape, dt, allow_name_mangling=True))
    y1 = io["y1"]
    ysum1 = io["ysum1"]
    U1 = io["U1"]
    V1 = io["V1"]
    U2 = io["U2"]
    V2 = io["V2"]
    b1 = io["b1"]
    b2 = io["b2"]
    ident_bf_d = io["ident_bf"]
    ones_bf_d = io["ones_bf"]
    y2 = io["y2"]
    ysum2 = io["ysum2"]
    x1i = nc.dram_tensor("x1i", [TOK, D], F32, kind="Internal",
                         unsafe_allow_realloc=True).ap()

    ident_bf = _sbl("c2_ident", [128, 128], BF16)
    ones_bf = _sbl("c2_ones", [1, 512], BF16)
    b2_sb = _sbl("c2_b2", [1, D], BF16)
    PT = _sbl("c2_PT", [128, 8 * TOK], BF16)
    ZT = _sbl("c2_ZT", [128, 8 * TOK], BF16)
    c_ld = p.sem()
    c_ld2 = p.sem()
    with nc.Block() as blk:
        @blk.sync
        def _(s):
            c_ld.inc(s.dma_start(out=ident_bf[:], in_=ident_bf_d), 16)
            c_ld.inc(s.dma_start(out=ones_bf[:], in_=ones_bf_d), 16)

        @blk.gpsimd
        def _(g):
            c_ld2.inc(g.dma_start(out=b2_sb[:], in_=b2), 16)

    u_ld = [p.sem() for _ in range(4)]
    u_fr = p.sem()
    evP = p.sem()
    v_ld = [p.sem() for _ in range(4)]
    v_fr = p.sem()
    mmgu = p.sem()
    sgq = p.sem()
    eva = p.sem()
    u2_ld = [p.sem(), p.sem()]
    u2_fr = p.sem()
    evZ = p.sem()

    with nc.sbuf_tensor("c2_x1T", [128, 32 * TOK], BF16, allow_name_mangling=True) as x1T:
        evn1, evT1, stf1, _tp = emit_ln(p, nc, y1, ysum1, x1i, x1T, ident_bf,
                                        entry_waits=[(c_ld, 16 * 2), (c_ld2, 16)], tag="c2ln")

        u_cst = p.sem()
        with nc.sbuf_tensor("c2_u1w", [128, 4 * 512], BF16, allow_name_mangling=True) as u1w, \
             nc.sbuf_tensor("c2_u1r", [128, 4 * 512], F32, allow_name_mangling=True) as u1r, \
             nc.psum_tensor("c2_pp", [128, 4096], F32, allow_name_mangling=True) as pp, \
             nc.Block() as blk:
            @blk.sync
            def _(s):
                for rg in range(2):
                    for dc in range(32):
                        idx = rg * 32 + dc
                        if idx >= 4:
                            s.wait_ge(u_cst.h, idx - 3)
                        u_ld[idx % 4].inc(s.dma_start(
                            out=u1r[:, (idx % 4) * 512:(idx % 4 + 1) * 512],
                            in_=U1[dc * 128:(dc + 1) * 128,
                                   rg * 512:(rg + 1) * 512]), 16)

            @blk.tensor
            def _(t):
                t.wait_ge(evT1.h, evT1.v)
                for rg in range(2):
                    if rg > 0:
                        t.wait_ge(evP.h, 8)
                    for dc in range(32):
                        idx = rg * 32 + dc
                        t.wait_ge(u_cst.h, idx + 1)
                        last = None
                        for sub in range(8):
                            rc4, th = divmod(sub, 2)
                            last = t.matmul(
                                pp[:, sub * 512:(sub + 1) * 512],
                                u1w[:, (idx % 4) * 512 + rc4 * 128:
                                    (idx % 4) * 512 + (rc4 + 1) * 128],
                                chunk_ap(x1T, TOK, dc, th * 512, 512),
                                start=(dc == 0), stop=(dc == 31))
                        u_fr.inc(last)

            @blk.vector
            def _(v):
                def ucast(idx):
                    v.wait_ge(u_ld[idx % 4].h, 16 * (idx // 4 + 1))
                    if idx >= 4:
                        v.wait_ge(u_fr.h, idx - 3)
                    u_cst.inc(v.tensor_copy(
                        u1w[:, (idx % 4) * 512:(idx % 4 + 1) * 512],
                        u1r[:, (idx % 4) * 512:(idx % 4 + 1) * 512]))
                for idx in range(36):
                    ucast(idx)
                v.wait_ge(u_fr.h, 32)
                for sub in range(8):
                    rc4, th = divmod(sub, 2)
                    evP.inc(v.tensor_copy(
                        chunk_ap(PT, TOK, 0 * 4 + rc4, th * 512, 512),
                        pp[:, sub * 512:(sub + 1) * 512]))
                for idx in range(36, 64):
                    ucast(idx)
                v.wait_ge(u_fr.h, 64)
                for sub in range(8):
                    rc4, th = divmod(sub, 2)
                    evP.inc(v.tensor_copy(
                        chunk_ap(PT, TOK, 1 * 4 + rc4, th * 512, 512),
                        pp[:, sub * 512:(sub + 1) * 512]))

    JF = 22
    fgs = [(j * 512, min(512, INTER - j * 512)) for j in range(JF)]
    nf4s = [fw // 128 for (fo, fw) in fgs]
    FC_HALF = sum(nf4s)

    b1_ld = p.sem()
    vcst = p.sem()
    u2cst = p.sem()
    with nc.sbuf_tensor("c2_actT", [128, 86 * 512], BF16, allow_name_mangling=True) as actT, \
         nc.sbuf_tensor("c2_b1", [1, 2 * INTER], BF16, allow_name_mangling=True) as b1_sb, \
         nc.sbuf_tensor("c2_v1w", [128, 2 * 1024], BF16, allow_name_mangling=True) as v1w, \
         nc.sbuf_tensor("c2_v1r", [128, 4 * 1024], F32, allow_name_mangling=True) as v1r, \
         nc.sbuf_tensor("c2_sg", [128, 2 * 512], F32, allow_name_mangling=True) as sgbuf, \
         nc.sbuf_tensor("c2_u2w", [128, 2 * 1024], BF16, allow_name_mangling=True) as u2w, \
         nc.sbuf_tensor("c2_u2r", [128, 2 * 1024], F32, allow_name_mangling=True) as u2r, \
         nc.psum_tensor("c2_pgu", [128, 4096], F32, allow_name_mangling=True) as pgu, \
         nc.Block() as blk:
        @blk.gpsimd
        def _(g):
            g.wait_ge(evP.h, 16)
            for cs in stf1:
                g.wait_ge(cs.h, cs.v)
            b1_ld.inc(g.dma_start(out=b1_sb[:], in_=b1), 16)

        @blk.sync
        def _(s):
            s.wait_ge(evP.h, 16)
            for cs in stf1:
                s.wait_ge(cs.h, cs.v)
            for half in range(2):
                for j, (fo, fw) in enumerate(fgs):
                    for rc in range(8):
                        idx = (half * JF + j) * 8 + rc
                        if idx >= 4:
                            s.wait_ge(vcst.h, idx - 3)
                        g_src = V1[rc * 128:(rc + 1) * 128, :].rearrange(
                            "p (gu f) -> p gu f", gu=2)[:, :, fo:fo + fw]
                        v_ld[idx % 4].inc(s.dma_start(
                            out=v1r[:, (idx % 4) * 1024:(idx % 4) * 1024 + 2 * fw]
                                .rearrange("p (gu f) -> p gu f", gu=2),
                            in_=g_src), 16)
                for fc in range(86):
                    idx = half * 86 + fc
                    if idx >= 2:
                        s.wait_ge(u2cst.h, idx - 1)
                    u2_ld[idx % 2].inc(s.dma_start(
                        out=u2r[:, (idx % 2) * 1024:(idx % 2 + 1) * 1024],
                        in_=U2[fc * 128:(fc + 1) * 128, :]), 16)

        @blk.tensor
        def _(t):
            t.wait_ge(evP.h, 16)
            t.wait_ge(b1_ld.h, 16)
            for half in range(2):
                if half == 1:
                    t.wait_ge(evZ.h, 8)
                for j, (fo, fw) in enumerate(fgs):
                    nf4 = fw // 128
                    git = half * JF + j
                    done_fc4 = half * FC_HALF + sum(nf4s[:j])
                    if git > 0:
                        t.wait_ge(eva.h, done_fc4)
                        t.wait_ge(sgq.h, done_fc4)
                    for rc in range(8):
                        idx = git * 8 + rc
                        t.wait_ge(vcst.h, idx + 1)
                        last = None
                        for fc4 in range(nf4):
                            for gu in range(2):
                                last = t.matmul(
                                    pgu[:, (fc4 * 2 + gu) * 512:(fc4 * 2 + gu + 1) * 512],
                                    v1w[:, (idx % 2) * 1024 + gu * fw + fc4 * 128:
                                        (idx % 2) * 1024 + gu * fw + (fc4 + 1) * 128],
                                    chunk_ap(PT, TOK, rc, half * 512, 512),
                                    start=(rc == 0), stop=False)
                        v_fr.inc(last)
                    last = None
                    for fc4 in range(nf4):
                        t.matmul(
                            pgu[:, (fc4 * 2) * 512:(fc4 * 2) * 512 + 512],
                            b1_sb[0:1, fo + fc4 * 128:fo + (fc4 + 1) * 128],
                            ones_bf[0:1, 0:512], start=False, stop=True)
                        last = t.matmul(
                            pgu[:, (fc4 * 2 + 1) * 512:(fc4 * 2 + 1) * 512 + 512],
                            b1_sb[0:1, INTER + fo + fc4 * 128:INTER + fo + (fc4 + 1) * 128],
                            ones_bf[0:1, 0:512], start=False, stop=True)
                    mmgu.inc(last)
                t.wait_ge(eva.h, (half + 1) * FC_HALF)
                t.wait_ge(sgq.h, (half + 1) * FC_HALF)
                if half == 1:
                    pass
                for fc in range(86):
                    idx = half * 86 + fc
                    t.wait_ge(u2cst.h, idx + 1)
                    last = None
                    for r2c in range(8):
                        last = t.matmul(
                            pgu[:, r2c * 512:(r2c + 1) * 512],
                            u2w[:, (idx % 2) * 1024 + r2c * 128:
                                (idx % 2) * 1024 + (r2c + 1) * 128],
                            actT[:, fc * 512:(fc + 1) * 512],
                            start=(fc == 0), stop=(fc == 85))
                    u2_fr.inc(last)

        @blk.scalar
        def _(s):
            cnt = 0
            for half in range(2):
                for j, (fo, fw) in enumerate(fgs):
                    git = half * JF + j
                    nf4 = fw // 128
                    s.wait_ge(mmgu.h, git + 1)
                    for fc4 in range(nf4):
                        if cnt >= 2:
                            s.wait_ge(eva.h, cnt - 1)
                        sgq.inc(s.activation(
                            sgbuf[:, (cnt % 2) * 512:(cnt % 2 + 1) * 512],
                            pgu[:, (fc4 * 2) * 512:(fc4 * 2) * 512 + 512],
                            AF.Silu))
                        cnt += 1

        @blk.vector
        def _(v):
            cnt = [0]

            def vcast(idx):
                half_, rem = divmod(idx, JF * 8)
                j_, rc_ = divmod(rem, 8)
                fw_ = fgs[j_][1]
                v.wait_ge(v_ld[idx % 4].h, 16 * (idx // 4 + 1))
                if idx >= 2:
                    v.wait_ge(v_fr.h, idx - 1)
                vcst.inc(v.tensor_copy(
                    v1w[:, (idx % 2) * 1024:(idx % 2) * 1024 + 2 * fw_],
                    v1r[:, (idx % 4) * 1024:(idx % 4) * 1024 + 2 * fw_]))

            def u2cast(idx):
                v.wait_ge(u2_ld[idx % 2].h, 16 * (idx // 2 + 1))
                if idx >= 2:
                    v.wait_ge(u2_fr.h, idx - 1)
                u2cst.inc(v.tensor_copy(
                    u2w[:, (idx % 2) * 1024:(idx % 2 + 1) * 1024],
                    u2r[:, (idx % 2) * 1024:(idx % 2 + 1) * 1024]))

            def emit_evas(git):
                half_, j_ = divmod(git, JF)
                nf4 = nf4s[j_]
                nfc0 = sum(nf4s[:j_])
                for fc4 in range(nf4):
                    v.wait_ge(sgq.h, cnt[0] + 1)
                    eva.inc(v.tensor_tensor(
                        actT[:, (nfc0 + fc4) * 512:(nfc0 + fc4 + 1) * 512],
                        sgbuf[:, (cnt[0] % 2) * 512:(cnt[0] % 2 + 1) * 512],
                        pgu[:, (fc4 * 2 + 1) * 512:(fc4 * 2 + 1) * 512 + 512],
                        ALU.mult))
                    cnt[0] += 1

            for half in range(2):
                if half == 1:
                    v.wait_ge(u2_fr.h, 86)
                for j in range(JF):
                    git = half * JF + j
                    for rc in range(8):
                        vcast(git * 8 + rc)
                        if rc == 0 and j > 0:
                            emit_evas(git - 1)
                emit_evas(half * JF + JF - 1)
                for fc in range(86):
                    u2cast(half * 86 + fc)
                v.wait_ge(u2_fr.h, (half + 1) * 86)
                for r2c in range(8):
                    evZ.inc(v.tensor_copy(
                        chunk_ap(ZT, TOK, r2c, half * 512, 512),
                        pgu[:, r2c * 512:(r2c + 1) * 512]))

    w_fr2, mme2, eve2, sty2, sts2 = emit_mm_res(
        p, nc, ZT, 8, V2, D, x1i, y2, ysum2, ones_bf, bias_sb=b2_sb,
        entry_waits=[(evZ, 16), (stf1[0], stf1[0].v), (stf1[1], stf1[1].v)],
        tag="c2mm")
    _es.close()
    return sty2 + [sts2]


def build_launch_C2():
    p = Prog()
    io = dict(
        y1=p.inp("y1", [TOK, D], F32),
        ysum1=p.inp("ysum1", [128, NT], F32),
        U1=p.inp("U1", [D, RANK], F32),
        V1=p.inp("V1", [RANK, 2 * INTER], F32),
        U2=p.inp("U2", [INTER, RANK], F32),
        V2=p.inp("V2", [RANK, D], F32),
        b1=p.inp("b1", [1, 2 * INTER], F32),
        b2=p.inp("b2", [1, D], F32),
        ident_bf=p.inp("ident_bf", [128, 128], BF16),
        ones_bf=p.inp("ones_bf", [1, 512], BF16),
        y2=p.outp("y2", [TOK, D], F32),
        ysum2=p.outp("ysum2", [128, NT], F32),
    )
    sts = emit_launch_C2(p, io)
    with p.nc.Block() as blk:
        @blk.sync
        def _(s):
            for cs in sts:
                s.wait_ge(cs.h, cs.v)
    return p


def run_launch_C2(inputs, resC1, consts=None):
    consts = consts or host_consts()
    ones512 = np.ones((1, 512), dtype=BF)
    in_maps = []
    for c in range(NCORES):
        in_maps.append(dict(
            y1=resC1[c]["y1"], ysum1=resC1[c]["ysum1"],
            U1=inputs["L0_ffn_U1"], V1=inputs["L0_ffn_V1"],
            U2=inputs["L0_ffn_U2"], V2=inputs["L0_ffn_V2"],
            b1=inputs["L0_ffn_b1"].reshape(1, -1),
            b2=inputs["L0_ffn_b2"].reshape(1, -1),
            ident_bf=consts["ident_bf"], ones_bf=ones512,
        ))
    pC = build_launch_C2()
    res = run_bass_kernel_spmd(pC.nc, in_maps, core_ids=list(range(NCORES)))
    return res.results


def emit_launch_C3(p, io):
    nc = p.nc
    _es = ExitStack()
    def _sbl(name, shape, dt=F32):
        return _es.enter_context(
            nc.sbuf_tensor(name, shape, dt, allow_name_mangling=True))
    y2 = io["y2"]
    ysum2 = io["ysum2"]
    wT = io["wT"]
    bias = io["bias"]
    bmT = io["bmT"]
    Emat_d = io["Emat"]
    ident_bf_d = io["ident_bf"]
    ones_bf_d = io["ones_bf"]
    y3 = io["y3"]
    ysum3 = io["ysum3"]
    x2i = nc.dram_tensor("x2i", [TOK, D], F32, kind="Internal",
                         unsafe_allow_realloc=True).ap()

    ident_bf = _sbl("c3_ident", [128, 128], BF16)
    ones_bf = _sbl("c3_ones", [1, 512], BF16)
    bias_sb = _sbl("c3_bias", [1, D], BF16)
    Emat_s = _sbl("c3_E", [4, 128], F32)
    bmT4 = _sbl("c3_bmT4", [4, 32 * 128], F32)
    flg4 = _sbl("c3_flg4", [4, 32 * 128], F32)
    bmET = _sbl("c3_bmET", [128, 32 * 128], F32)
    x2T = _sbl("c3_x2T", [128, 32 * TOK], BF16)
    c_ld = p.sem()
    c_ld2 = p.sem()
    dvf = p.sem()
    mmE = p.sem()
    evEx = p.sem()
    with nc.psum_tensor("c3_pE", [128, 128], F32, allow_name_mangling=True) as pE, \
         nc.Block() as blk:
        @blk.sync
        def _(s):
            c_ld.inc(s.dma_start(out=ident_bf[:], in_=ident_bf_d), 16)
            c_ld.inc(s.dma_start(out=ones_bf[:], in_=ones_bf_d), 16)
            c_ld.inc(s.dma_start(out=Emat_s[:], in_=Emat_d), 16)
            c_ld.inc(s.dma_start(
                out=bmT4[:, :].rearrange("p (g nb) -> p g nb", g=32),
                in_=bmT.rearrange("(g p) nb -> p g nb", p=4)), 16)

        @blk.gpsimd
        def _(g):
            c_ld2.inc(g.dma_start(out=bias_sb[:], in_=bias), 16)

        @blk.vector
        def _(v):
            v.wait_ge(c_ld.h, 16 * 4)
            v.wait_ge(c_ld2.h, 16)
            dvf.inc(v.tensor_single_scalar(flg4[:, :], bmT4[:, :], 0.0, ALU.is_gt))

        @blk.tensor
        def _(t):
            t.wait_ge(dvf.h, 1)
            for kc in range(32):
                if kc >= 1:
                    t.wait_ge(evEx.h, kc)
                mmE.inc(t.matmul(pE[:], Emat_s[:, :],
                                 flg4[:, kc * 128:(kc + 1) * 128],
                                 start=True, stop=True))

        @blk.vector
        def _(v):
            for kc in range(32):
                v.wait_ge(mmE.h, kc + 1)
                evEx.inc(v.tensor_copy(bmET[:, kc * 128:(kc + 1) * 128], pE[:]))

    evn2, evT2, stf2, _tp = emit_ln(p, nc, y2, ysum2, x2i, x2T, ident_bf,
                                    entry_waits=[(evEx, 32)], tag="c3ln")

    def mask_view(dg, kc):
        return bmET[:, kc * 128 + dg * 16:kc * 128 + (dg + 1) * 16] \
            .unsqueeze(2).broadcast_to([128, 16, 32])

    w_fr3, mme3, eve3, sty3, sts3 = emit_mm_res(
        p, nc, x2T, 32, wT, D, x2i, y3, ysum3, ones_bf, bias_sb=bias_sb,
        entry_waits=[(evT2, evT2.v), (stf2[0], stf2[0].v), (stf2[1], stf2[1].v)],
        mask_fn=mask_view, tag="c3mm")
    _es.close()
    return sty3 + [sts3]


def build_launch_C3():
    p = Prog()
    io = dict(
        y2=p.inp("y2", [TOK, D], F32),
        ysum2=p.inp("ysum2", [128, NT], F32),
        wT=p.inp("wT", [D, D], F32),
        bias=p.inp("bias", [1, D], F32),
        bmT=p.inp("bmT", [128, 128], F32),
        Emat=p.inp("Emat", [4, 128], F32),
        ident_bf=p.inp("ident_bf", [128, 128], BF16),
        ones_bf=p.inp("ones_bf", [1, 512], BF16),
        y3=p.outp("y3", [TOK, D], F32),
        ysum3=p.outp("ysum3", [128, NT], F32),
    )
    sts = emit_launch_C3(p, io)
    with p.nc.Block() as blk:
        @blk.sync
        def _(s):
            for cs in sts:
                s.wait_ge(cs.h, cs.v)
    return p


def run_launch_C3(inputs, resC2, bm_full, consts=None):
    consts = consts or host_consts()
    ones512 = np.ones((1, 512), dtype=BF)
    Emat = np.zeros((4, 128), dtype=np.float32)
    for pp_ in range(128):
        Emat[pp_ // 32, pp_] = 1.0
    wTh = np.ascontiguousarray(inputs["L1_attn_weight"].T)
    bmT = np.ascontiguousarray(bm_full.T)
    in_maps = []
    for c in range(NCORES):
        in_maps.append(dict(
            y2=resC2[c]["y2"], ysum2=resC2[c]["ysum2"],
            wT=wTh, bias=inputs["L1_attn_bias"].reshape(1, -1),
            bmT=bmT, Emat=Emat,
            ident_bf=consts["ident_bf"], ones_bf=ones512,
        ))
    pC = build_launch_C3()
    res = run_bass_kernel_spmd(pC.nc, in_maps, core_ids=list(range(NCORES)))
    return res.results


SORT8_STAGES = [
    [(0, 1), (2, 3), (4, 5), (6, 7)],
    [(0, 2), (1, 3), (4, 6), (5, 7)],
    [(1, 2), (5, 6), (0, 4), (3, 7)],
    [(1, 5), (2, 6)],
    [(1, 4), (3, 6)],
    [(2, 4), (3, 5)],
    [(3, 4)],
]


def emit_launch_C4(p, io):
    nc = p.nc
    _es = ExitStack()
    def _sbl(name, shape, dt=F32):
        return _es.enter_context(
            nc.sbuf_tensor(name, shape, dt, allow_name_mangling=True))
    y3 = io["y3"]
    ysum3 = io["ysum3"]
    WbT = io["WbT"]
    b_base = io["b_base"]
    WrT = io["WrT"]
    b_router = io["b_router"]
    laA2 = io["laA2"]
    laB2 = io["laB2"]
    kvec_d = io["kvec"]
    R16_d = io["R16"]
    ident_f32_d = io["ident_f32"]
    ident_bf_d = io["ident_bf"]
    ones_bf_d = io["ones_bf"]
    y4 = io["y4"]
    ysum4 = io["ysum4"]
    x3i = nc.dram_tensor("x3i", [TOK, D], F32, kind="Internal",
                         unsafe_allow_realloc=True).ap()

    ident_bf = _sbl("c4_identb", [128, 128], BF16)
    ident_f32 = _sbl("c4_identf", [128, 128], F32)
    ones_bf = _sbl("c4_ones", [1, 512], BF16)
    bb_sb = _sbl("c4_bb", [1, D], BF16)
    br_sb = _sbl("c4_br", [1, E_EXP], BF16)
    kvec = _sbl("c4_kvec", [128, 8], F32)
    R16 = _sbl("c4_R16", [8, 128], F32)
    WrT_sb = _sbl("c4_WrT", [128, 32 * 8], BF16)
    laA_sb = _sbl("c4_laA", [128, 32 * 128], BF16)
    laB_sb = _sbl("c4_laB", [128, D], BF16)
    x3T = _sbl("c4_x3T", [128, 32 * TOK], BF16)
    hwT_sb = _sbl("c4_hwT", [128, NT * 128], BF16)
    c_ld = p.sem()
    c_ld2 = p.sem()
    with nc.Block() as blk:
        @blk.sync
        def _(s):
            c_ld.inc(s.dma_start(out=ident_bf[:], in_=ident_bf_d), 16)
            c_ld.inc(s.dma_start(out=ident_f32[:], in_=ident_f32_d), 16)
            c_ld.inc(s.dma_start(out=ones_bf[:], in_=ones_bf_d), 16)
            c_ld.inc(s.dma_start(out=kvec[:], in_=kvec_d), 16)
            c_ld.inc(s.dma_start(out=R16[:], in_=R16_d), 16)

        @blk.gpsimd
        def _(g):
            c_ld2.inc(g.dma_start(out=bb_sb[:], in_=b_base), 16)
            c_ld2.inc(g.dma_start(out=br_sb[:], in_=b_router), 16)
            c_ld2.inc(g.dma_start(
                out=WrT_sb[:, :].rearrange("p (c e) -> p c e", c=32),
                in_=WrT.rearrange("(c p) e -> p c e", p=128)), 16)
            c_ld2.inc(g.dma_start(out=laB_sb[:], in_=laB2), 16)
            for j in range(4):
                c_ld2.inc(g.dma_start(
                    out=laA_sb[:, j * 8 * 128:(j + 1) * 8 * 128]
                        .rearrange("p (c e) -> p c e", c=8),
                    in_=laA2.rearrange("(c p) e -> p c e", p=128)[
                        :, j * 8:(j + 1) * 8, :]), 16)

    evn3, evT3, stf3, _tp = emit_ln(p, nc, y3, ysum3, x3i, x3T, ident_bf,
                                    entry_waits=[(c_ld, c_ld.v), (c_ld2, c_ld2.v)],
                                    tag="c4ln")

    mml = p.sem()
    evl = p.sem()
    dsp = p.sem()
    mmh = p.sem()
    mmt = p.sem()
    evt = p.sem()
    mmx = p.sem()
    evx = p.sem()
    evw = p.sem()
    with nc.sbuf_tensor("c4_lg", [128, NT * 8], F32, allow_name_mangling=True) as lg, \
         nc.sbuf_tensor("c4_sa", [128, 8], F32, allow_name_mangling=True) as sa, \
         nc.sbuf_tensor("c4_sb2", [128, 8], F32, allow_name_mangling=True) as sb2, \
         nc.sbuf_tensor("c4_sc", [128, 16], F32, allow_name_mangling=True) as scx, \
         nc.sbuf_tensor("c4_tm8", [128, 8], F32, allow_name_mangling=True) as tm8, \
         nc.sbuf_tensor("c4_wt", [8, 128], F32, allow_name_mangling=True) as wt_sb, \
         nc.sbuf_tensor("c4_wfm", [128, 128], F32, allow_name_mangling=True) as wfm_sb, \
         nc.sbuf_tensor("c4_wtok", [128, NT * 8], F32, allow_name_mangling=True) as wtok, \
         nc.psum_tensor("c4_pq", [128, 4096], F32, allow_name_mangling=True) as pq, \
         nc.Block() as blk:
        @blk.tensor
        def _(t):
            t.wait_ge(evT3.h, evT3.v)
            for tt in range(NT):
                last = None
                for kc in range(32):
                    last = t.matmul(pq[:, tt * 512:tt * 512 + 8],
                                    chunk_ap(x3T, TOK, kc, tt * 128, 128),
                                    WrT_sb[:, kc * 8:(kc + 1) * 8],
                                    start=(kc == 0), stop=False)
                last = t.matmul(pq[:, tt * 512:tt * 512 + 8],
                                ones_bf[0:1, 0:128], br_sb[0:1, :],
                                start=False, stop=True)
                mml.inc(last)
                last = None
                for kc in range(32):
                    last = t.matmul(pq[:, tt * 512 + 128:tt * 512 + 256],
                                    laA_sb[:, kc * 128:(kc + 1) * 128],
                                    chunk_ap(x3T, TOK, kc, tt * 128, 128),
                                    start=(kc == 0), stop=(kc == 31))
                mmh.inc(last)
            for tt in range(NT):
                t.wait_ge(dsp.h, (tt + 1) * 100)
                t.wait_ge(evl.h, tt + 1)
                if tt >= 1:
                    t.wait_ge(evt.h, tt)
                mmt.inc(t.transpose(pq[0:8, tt * 512 + 256:tt * 512 + 384],
                                    wtok[:, tt * 8:(tt + 1) * 8],
                                    ident_f32[:]))
                t.wait_ge(evt.h, tt + 1)
                if tt >= 1:
                    t.wait_ge(evx.h, tt)
                mmx.inc(t.matmul(pq[:, tt * 512 + 384:tt * 512 + 512],
                                 R16[:, :], wt_sb[:, :], start=True, stop=True))

        @blk.vector
        def _(v):
            def ch(instr, n=1):
                dsp.inc(instr, n)
                v.wait_ge(dsp.h, dsp.v)
                return instr

            for tt in range(NT):
                base = dsp.v
                v.wait_ge(mml.h, tt + 1)
                v.wait_ge(mmh.h, tt + 1)
                evl.inc(v.tensor_copy(lg[:, tt * 8:(tt + 1) * 8],
                                      pq[:, tt * 512:tt * 512 + 8]))
                v.wait_ge(evl.h, evl.v)
                ch(v.tensor_copy(sa[:, :], lg[:, tt * 8:(tt + 1) * 8]))
                src, dst = sa, sb2
                for stage in SORT8_STAGES:
                    in_pairs = set()
                    last = None
                    for (i, j) in stage:
                        last = v.tensor_max(dst[:, i:i + 1], src[:, i:i + 1],
                                            src[:, j:j + 1])
                        last = v.tensor_tensor(dst[:, j:j + 1], src[:, i:i + 1],
                                               src[:, j:j + 1], ALU.min)
                        in_pairs.add(i)
                        in_pairs.add(j)
                    for k in range(8):
                        if k not in in_pairs:
                            last = v.tensor_copy(dst[:, k:k + 1], src[:, k:k + 1])
                    ch(last)
                    src, dst = dst, src
                ch(v.tensor_copy(scx[:, 0:1], src[:, 0:1]))
                for jj in range(1, 8):
                    ch(v.tensor_add(scx[:, jj:jj + 1], scx[:, jj - 1:jj],
                                    src[:, jj:jj + 1]))
                ch(v.tensor_tensor(tm8[:, :], src[:, :], kvec[:, :], ALU.mult))
                ch(v.tensor_scalar_add(src[:, :], tm8[:, :], 1.0))
                ch(v.tensor_tensor(dst[:, :], src[:, :], scx[:, 0:8], ALU.is_gt))
                ch(v.tensor_reduce(scx[:, 8:9], dst[:, :], AX.X, ALU.add))
                ch(v.tensor_sub(tm8[:, 0:7], dst[:, 0:7], dst[:, 1:8]))
                ch(v.tensor_copy(tm8[:, 7:8], dst[:, 7:8]))
                ch(v.tensor_tensor(dst[:, :], scx[:, 0:8], tm8[:, :], ALU.mult))
                ch(v.tensor_reduce(scx[:, 9:10], dst[:, :], AX.X, ALU.add))
                ch(v.reciprocal(scx[:, 10:11], scx[:, 8:9]))
                ch(v.tensor_scalar_add(scx[:, 11:12], scx[:, 9:10], -1.0))
                ch(v.tensor_mul(scx[:, 12:13], scx[:, 11:12], scx[:, 10:11]))
                ch(v.tensor_scalar(wtok[:, tt * 8:(tt + 1) * 8],
                                   lg[:, tt * 8:(tt + 1) * 8],
                                   scx[:, 12:13], 0.0, ALU.subtract, ALU.max))
                pad = base + 100 - dsp.v
                assert pad >= 0, dsp.v
                if pad:
                    dsp.inc(v.engine_nop(), pad)
            for tt in range(NT):
                v.wait_ge(mmt.h, tt + 1)
                evt.inc(v.tensor_copy(wt_sb[:, :],
                                      pq[0:8, tt * 512 + 256:tt * 512 + 384]))
                v.wait_ge(mmx.h, tt + 1)
                v.wait_ge(evt.h, tt + 1)
                if tt >= 1:
                    v.wait_ge(evw.h, tt)
                evx.inc(v.tensor_copy(wfm_sb[:, :],
                                      pq[:, tt * 512 + 384:tt * 512 + 512]))
                v.wait_ge(evx.h, tt + 1)
                v.wait_ge(mmh.h, tt + 1)
                evw.inc(v.tensor_tensor(hwT_sb[:, tt * 128:(tt + 1) * 128],
                                        pq[:, tt * 512 + 128:tt * 512 + 256],
                                        wfm_sb[:, :], ALU.mult))

    _c4pq = [None]

    def lora_mm(t, dg, tt):
        return t.matmul(
            _c4pq[0][:, tt * 512:(tt + 1) * 512],
            hwT_sb[:, tt * 128:(tt + 1) * 128],
            laB_sb[:, dg * 512:(dg + 1) * 512],
            start=False, stop=True)

    w_fr4, mme4, eve4, sty4, sts4 = emit_mm_res(
        p, nc, x3T, 32, WbT, D, x3i, y4, ysum4, ones_bf, bias_sb=bb_sb,
        entry_waits=[(stf3[0], stf3[0].v), (stf3[1], stf3[1].v)],
        extra_mm=lora_mm, extra_waits=[(evw, NT)], psum_ref=_c4pq, tag="c4mm")
    _es.close()
    return sty4 + [sts4]


def build_launch_C4():
    p = Prog()
    io = dict(
        y3=p.inp("y3", [TOK, D], F32),
        ysum3=p.inp("ysum3", [128, NT], F32),
        WbT=p.inp("WbT", [D, D], F32),
        b_base=p.inp("b_base", [1, D], F32),
        WrT=p.inp("WrT", [D, E_EXP], F32),
        b_router=p.inp("b_router", [1, E_EXP], F32),
        laA2=p.inp("laA2", [D, 128], F32),
        laB2=p.inp("laB2", [128, D], F32),
        kvec=p.inp("kvec", [128, 8], F32),
        R16=p.inp("R16", [8, 128], F32),
        ident_f32=p.inp("ident_f32", [128, 128], F32),
        ident_bf=p.inp("ident_bf", [128, 128], BF16),
        ones_bf=p.inp("ones_bf", [1, 512], BF16),
        y4=p.outp("y4", [TOK, D], F32),
        ysum4=p.outp("ysum4", [128, NT], F32),
    )
    sts = emit_launch_C4(p, io)
    with p.nc.Block() as blk:
        @blk.sync
        def _(s):
            for cs in sts:
                s.wait_ge(cs.h, cs.v)
    return p


def run_launch_C4(inputs, resC3, consts=None):
    consts = consts or host_consts()
    ones512 = np.ones((1, 512), dtype=BF)
    kvec = np.broadcast_to(np.arange(1, 9, dtype=np.float32), (128, 8)).copy()
    R16 = np.zeros((8, 128), dtype=np.float32)
    for pp_ in range(128):
        R16[pp_ // 16, pp_] = 1.0
    WbT = np.ascontiguousarray(inputs["L1_ffn_W_base"].T)
    WrT = np.ascontiguousarray(inputs["L1_ffn_W_router"].T)
    laA2 = np.ascontiguousarray(
        inputs["L1_ffn_lora_A"].transpose(1, 0, 2).reshape(D, 128))
    laB2 = np.ascontiguousarray(inputs["L1_ffn_lora_B"].reshape(128, D))
    in_maps = []
    for c in range(NCORES):
        in_maps.append(dict(
            y3=resC3[c]["y3"], ysum3=resC3[c]["ysum3"],
            WbT=WbT, b_base=inputs["L1_ffn_b_base"].reshape(1, -1),
            WrT=WrT, b_router=inputs["L1_ffn_b_router"].reshape(1, -1),
            laA2=laA2, laB2=laB2, kvec=kvec, R16=R16,
            ident_f32=consts["ident_f32"], ident_bf=consts["ident_bf"],
            ones_bf=ones512,
        ))
    pC = build_launch_C4()
    res = run_bass_kernel_spmd(pC.nc, in_maps, core_ids=list(range(NCORES)))
    return res.results




def phase_reset(p, store_ctrs=()):
    nc = p.nc
    live = [cs for cs in store_ctrs if cs is not None and cs.v]
    if live:
        with nc.Block() as blk:
            @blk.sync
            def _(s):
                for cs in live:
                    s.wait_ge(cs.h, cs.v)
    nc.all_engine_barrier()
    nc.clear_and_free_semaphores(nc.allocated_since(p.snap))
    nc.all_engine_barrier()
    nc.restore_sems(p.snap)


def build_fused():
    p = Prog()
    nc = p.nc
    i_ = p.inp
    x = i_("x", [TOK, D])
    xT_own = i_("xT_own", [D, TOK])
    xbT = i_("xbT", [D, S])
    mask = i_("mask", [S, D], I32)
    w_qa = i_("w_qa", [D, QR]); g_qa = i_("g_qa", [1, QR])
    w_qb = i_("w_qb", [QR, NH * QK])
    w_kva = i_("w_kva", [D, KVR + RD]); g_kva = i_("g_kva", [1, KVR])
    w_kvb = i_("w_kvb", [KVR, NH * (ND + VD)])
    w_o = i_("w_o", [D, D])
    U1 = i_("U1", [D, RANK]); V1 = i_("V1", [RANK, 2 * INTER])
    U2 = i_("U2", [INTER, RANK]); V2 = i_("V2", [RANK, D])
    b1 = i_("b1", [1, 2 * INTER]); b2 = i_("b2", [1, D])
    wT = i_("wT", [D, D]); bias1 = i_("bias1", [1, D])
    WbT = i_("WbT", [D, D]); b_base = i_("b_base", [1, D])
    WrT = i_("WrT", [D, E_EXP]); b_router = i_("b_router", [1, E_EXP])
    laA2 = i_("laA2", [D, 128]); laB2 = i_("laB2", [128, D])
    cos_own = i_("cos_own", [RD, TOK]); sin_own = i_("sin_own", [RD, TOK])
    cos_full = i_("cos_full", [RD, S]); sin_full = i_("sin_full", [RD, S])
    ident_bf = i_("ident_bf", [128, 128], BF16)
    ident_f32 = i_("ident_f32", [128, 128])
    ones_f32 = i_("ones_f32", [1, 128])
    ones_bf = i_("ones_bf", [1, 512], BF16)
    Gmat = i_("Gmat", [128, 4]); Emat = i_("Emat", [4, 128])
    kvec = i_("kvec", [128, 8]); R16 = i_("R16", [8, 128])
    out = p.outp("out", [TOK, D], F32)

    def internal(name, shape, dt=F32):
        return nc.dram_tensor(name, list(shape), dt, kind="Internal").ap()

    Qf_i = internal("Qf_i", [NH * QK, TOK], BF16)
    KN_i = internal("KN_i", [NH * ND, S], BF16)
    KR_i = internal("KR_i", [RD, S], BF16)
    Vt_i = internal("Vt_i", [S, NH * VD], BF16)
    attn_i = internal("attn_i", [TOK, NH * VD], BF16)
    bm_i = internal("bm_i", [128, 128])
    bmT_i = internal("bmT_i", [128, 128])
    ys = {k: internal(f"ysum{k}_i", [128, NT]) for k in (1, 2, 3, 4)}
    yv = {k: internal(f"y{k}_i", [TOK, D]) for k in (1, 2, 3, 4)}

    p.snap = nc.snapshot_sems()

    st = emit_launch_A(p, dict(
        xT=xT_own, w_qa=w_qa, g_qa=g_qa, w_qb=w_qb, w_kva=w_kva, g_kva=g_kva,
        w_kvb=w_kvb, cos_own=cos_own, sin_own=sin_own,
        ident_bf=ident_bf, ident_f32=ident_f32, ones_f32=ones_f32,
        Qf=Qf_i, KN=KN_i, KR=KR_i, Vt=Vt_i), do_q=True, do_kv=False,
        do_bm=False)
    phase_reset(p, st)

    for sl in range(4):
        st = emit_launch_A(p, dict(
            xT=xbT[:, sl * TOK:(sl + 1) * TOK],
            w_kva=w_kva, g_kva=g_kva, w_kvb=w_kvb,
            cos_own=cos_full[:, sl * TOK:(sl + 1) * TOK],
            sin_own=sin_full[:, sl * TOK:(sl + 1) * TOK],
            ident_bf=ident_bf, ident_f32=ident_f32, ones_f32=ones_f32,
            KN=KN_i[:, sl * TOK:(sl + 1) * TOK],
            KR=KR_i[:, sl * TOK:(sl + 1) * TOK],
            Vt=Vt_i[sl * TOK:(sl + 1) * TOK, :]),
            do_q=False, do_kv=True, do_bm=False)
        phase_reset(p, st)

    st = emit_bm(p, nc, mask, Gmat, bm_i, 32)
    phase_reset(p, st)
    stT = p.sem()
    ld_t = p.sem()
    mm_t = p.sem()
    ev_t = p.sem()
    with nc.sbuf_tensor("fb_bm", [128, 128], F32, allow_name_mangling=True) as bm_sb, \
         nc.sbuf_tensor("fb_id", [128, 128], F32, allow_name_mangling=True) as idf, \
         nc.sbuf_tensor("fb_bmT", [128, 128], F32, allow_name_mangling=True) as bmT_sb, \
         nc.psum_tensor("fb_pt", [128, 128], F32, allow_name_mangling=True) as pt, \
         nc.Block() as blk:
        @blk.sync
        def _(s):
            ld_t.inc(s.dma_start(out=bm_sb[:], in_=bm_i), 16)
            ld_t.inc(s.dma_start(out=idf[:], in_=ident_f32), 16)

        @blk.tensor
        def _(t):
            t.wait_ge(ld_t.h, 32)
            mm_t.inc(t.transpose(pt[:], bm_sb[:], idf[:]))

        @blk.vector
        def _(v):
            v.wait_ge(mm_t.h, 1)
            ev_t.inc(v.tensor_copy(bmT_sb[:], pt[:]))

        @blk.scalar
        def _(s):
            s.wait_ge(ev_t.h, 1)
            stT.inc(s.dma_start(out=bmT_i, in_=bmT_sb[:]), 16)
    phase_reset(p, [stT])

    st = emit_launch_B2(p, dict(Qf=Qf_i, KN_b=KN_i, KR_b=KR_i, Vt_b=Vt_i,
                                ident_bf=ident_bf, attn=attn_i))
    phase_reset(p, st)

    st = emit_launch_C1(p, dict(attn=attn_i, x=x, w_o=w_o, ident_bf=ident_bf,
                                y1=yv[1], ysum1=ys[1]))
    phase_reset(p, st)
    st = emit_launch_C2(p, dict(y1=yv[1], ysum1=ys[1], U1=U1, V1=V1, U2=U2,
                                V2=V2, b1=b1, b2=b2, ident_bf=ident_bf,
                                ones_bf=ones_bf, y2=yv[2], ysum2=ys[2]))
    phase_reset(p, st)
    st = emit_launch_C3(p, dict(y2=yv[2], ysum2=ys[2], wT=wT, bias=bias1,
                                bmT=bmT_i, Emat=Emat, ident_bf=ident_bf,
                                ones_bf=ones_bf, y3=yv[3], ysum3=ys[3]))
    phase_reset(p, st)
    st = emit_launch_C4(p, dict(y3=yv[3], ysum3=ys[3], WbT=WbT, b_base=b_base,
                                WrT=WrT, b_router=b_router, laA2=laA2,
                                laB2=laB2, kvec=kvec, R16=R16,
                                ident_f32=ident_f32, ident_bf=ident_bf,
                                ones_bf=ones_bf, y4=yv[4], ysum4=ys[4]))
    phase_reset(p, st)
    st = emit_launch_C5(p, dict(y4=yv[4], ysum4=ys[4], out=out))
    with nc.Block() as blk:
        @blk.sync
        def _(s):
            for cs in st:
                if cs.v:
                    s.wait_ge(cs.h, cs.v)
    return p


def kernel_fused(inputs):
    consts = host_consts()
    ones512 = np.ones((1, 512), dtype=BF)
    kvec = np.broadcast_to(np.arange(1, 9, dtype=np.float32), (128, 8)).copy()
    R16 = np.zeros((8, 128), dtype=np.float32)
    for pp_ in range(128):
        R16[pp_ // 16, pp_] = 1.0
    Emat = np.zeros((4, 128), dtype=np.float32)
    for pp_ in range(128):
        Emat[pp_ // 32, pp_] = 1.0
    x_full = inputs["L0_attn_hidden_states"]
    wTh = np.ascontiguousarray(inputs["L1_attn_weight"].T)
    WbT = np.ascontiguousarray(inputs["L1_ffn_W_base"].T)
    WrT = np.ascontiguousarray(inputs["L1_ffn_W_router"].T)
    laA2 = np.ascontiguousarray(
        inputs["L1_ffn_lora_A"].transpose(1, 0, 2).reshape(D, 128))
    laB2 = np.ascontiguousarray(inputs["L1_ffn_lora_B"].reshape(128, D))
    in_maps = []
    for c in range(NCORES):
        b, off = divmod(c, 4)
        pos0 = off * TOK
        in_maps.append(dict(
            x=np.ascontiguousarray(x_full[b, pos0:pos0 + TOK, :]),
            xT_own=np.ascontiguousarray(x_full[b, pos0:pos0 + TOK, :].T),
            xbT=np.ascontiguousarray(x_full[b].T),
            mask=inputs["L1_attn_mask"],
            w_qa=inputs["L0_attn_w_qa"],
            g_qa=inputs["L0_attn_g_qa"].reshape(1, -1),
            w_qb=inputs["L0_attn_w_qb"],
            w_kva=inputs["L0_attn_w_kva"],
            g_kva=inputs["L0_attn_g_kva"].reshape(1, -1),
            w_kvb=inputs["L0_attn_w_kvb"],
            w_o=inputs["L0_attn_w_o"],
            U1=inputs["L0_ffn_U1"], V1=inputs["L0_ffn_V1"],
            U2=inputs["L0_ffn_U2"], V2=inputs["L0_ffn_V2"],
            b1=inputs["L0_ffn_b1"].reshape(1, -1),
            b2=inputs["L0_ffn_b2"].reshape(1, -1),
            wT=wTh, bias1=inputs["L1_attn_bias"].reshape(1, -1),
            WbT=WbT, b_base=inputs["L1_ffn_b_base"].reshape(1, -1),
            WrT=WrT, b_router=inputs["L1_ffn_b_router"].reshape(1, -1),
            laA2=laA2, laB2=laB2,
            cos_own=np.ascontiguousarray(consts["cos_fm"][:, pos0:pos0 + TOK]),
            sin_own=np.ascontiguousarray(consts["sin_fm"][:, pos0:pos0 + TOK]),
            cos_full=consts["cos_fm"], sin_full=consts["sin_fm"],
            ident_bf=consts["ident_bf"], ident_f32=consts["ident_f32"],
            ones_f32=consts["ones_f32"], ones_bf=ones512,
            Gmat=consts["Gmat"], Emat=Emat, kvec=kvec, R16=R16,
        ))
    pF = build_fused()
    res = run_bass_kernel_spmd(pF.nc, in_maps, core_ids=list(range(NCORES)))
    outv = np.zeros((B, S, D), np.float32)
    for c in range(NCORES):
        b, off = divmod(c, 4)
        outv[b, off * TOK:(off + 1) * TOK] = res.results[c]["out"]
    return outv


FUSED = True

def kernel(**inputs):
    inputs = {k: np.ascontiguousarray(v) for k, v in inputs.items()}
    if FUSED:
        return kernel_fused(inputs)
    consts = host_consts()
    resA = kernel_resA = run_launch_A(inputs, consts)
    resB = run_launch_B(resA, consts)
    resC1 = run_launch_C1(inputs, resB, consts)
    resC2 = run_launch_C2(inputs, resC1, consts)
    bm_full = np.concatenate([resA[c]["bmp"] for c in range(NCORES)], axis=0)
    resC3 = run_launch_C3(inputs, resC2, bm_full, consts)
    resC4 = run_launch_C4(inputs, resC3, consts)
    resC5 = run_launch_C5(resC4)
    out = np.zeros((B, S, D), np.float32)
    for c in range(NCORES):
        b, off = divmod(c, 4)
        out[b, off * TOK:(off + 1) * TOK] = resC5[c]["out"]
    return out


# version 06
# speedup vs baseline: 1.1701x; 1.1701x over previous
import numpy as np
import ml_dtypes
import concourse.bass as bass
import concourse.mybir as mybir
from concourse.bass_utils import run_bass_kernel_spmd
from contextlib import ExitStack

F32 = mybir.dt.float32
BF16 = mybir.dt.bfloat16
I32 = mybir.dt.int32
AF = mybir.ActivationFunctionType
ALU = mybir.AluOpType
AX = mybir.AxisListType

B, S, D = 2, 4096, 4096
NCORES = 8
TOK = 1024
NT = TOK // 128
NH, QR, KVR, RD, ND, VD = 32, 1536, 512, 64, 128, 128
QK = ND + RD
EPS_RMS = 1e-6
EPS_LN = 1e-5
SM_SCALE = QK ** (-0.5)
INTER = 11008
RANK = 1024
E_EXP, R_LORA = 8, 16

BF = ml_dtypes.bfloat16


def host_consts():
    inv_freq = 10000.0 ** (-(np.arange(0, RD, 2, dtype=np.float64) / RD))
    t = np.arange(S, dtype=np.float64)
    freqs = np.outer(t, inv_freq)
    emb = np.concatenate([freqs, freqs], -1)
    Gmat = np.zeros((128, 4), dtype=np.float32)
    for r in range(128):
        Gmat[r, r // 32] = 1.0
    return dict(
        ident_bf=np.eye(128, dtype=BF),
        ident_f32=np.eye(128, dtype=np.float32),
        ones_f32=np.ones((1, 128), dtype=np.float32),
        ones_bf=np.ones((1, 128), dtype=BF),
        cos_fm=np.ascontiguousarray(np.cos(emb).T.astype(np.float32)),
        sin_fm=np.ascontiguousarray(np.sin(emb).T.astype(np.float32)),
        Gmat=Gmat,
    )


class Ctr:

    def __init__(self, h):
        self.h = h
        self.v = 0

    def inc(self, instr, n=1):
        instr.then_inc(self.h, n)
        self.v += n


class DStream:

    def __init__(self, p, nslots=2):
        self.sems = [p.sem() for _ in range(nslots)]
        self.n = nslots
        self.i = 0

    def issue(self, instr):
        s = self.sems[self.i % self.n]
        instr.then_inc(s.h, 16)
        s.v += 16
        self.i += 1
        return self.i - 1

    def wait(self, eng, idx):
        eng.wait_ge(self.sems[idx % self.n].h, 16 * (idx // self.n + 1))

    def wait_all(self, eng):
        for s in self.sems:
            if s.v:
                eng.wait_ge(s.h, s.v)


class Prog:
    def __init__(self):
        self.nc = bass.Bass("TRN2", target_bir_lowering=False, debug=False)
        self.ctx = ExitStack()
        self._sem_n = 0

    def sb(self, name, shape, dt=F32):
        return self.ctx.enter_context(
            self.nc.sbuf_tensor(name, shape, dt, allow_name_mangling=True))

    def ps(self, name, shape, dt=F32):
        return self.ctx.enter_context(
            self.nc.psum_tensor(name, shape, dt, allow_name_mangling=True))

    def sem(self, name=None):
        self._sem_n += 1
        name = name or f"s{self._sem_n}"
        return Ctr(self.ctx.enter_context(self.nc.semaphore(name)))

    def inp(self, name, shape, dt=F32):
        return self.nc.dram_tensor(name, list(shape), dt, kind="ExternalInput").ap()

    def outp(self, name, shape, dt=F32):
        return self.nc.dram_tensor(name, list(shape), dt, kind="ExternalOutput").ap()


def chunks_ap(tensor_sb, tok, c0, tt, nch):
    nch_total = tensor_sb.shape[1] // tok
    return tensor_sb[:, :].rearrange("p (c t) -> p c t", c=nch_total)[
        :, c0:c0 + nch, tt * 128:(tt + 1) * 128]


def chunk_ap(tensor_sb, tok, c, lo, w):
    return tensor_sb[:, c * tok + lo:c * tok + lo + w]


def emit_rope_fm(v, out_bf, src, cos, sin, tmp, tmp2, w, chain, prev_done=None):
    if chain.v:
        v.wait_ge(chain.h, chain.v)
    if prev_done is not None:
        v.wait_ge(prev_done[0].h, prev_done[1])
    v.tensor_mul(tmp[0:32, 0:w], src[32:64, 0:w], sin[0:32, 0:w])
    v.tensor_mul(tmp[32:64, 0:w], src[0:32, 0:w], sin[32:64, 0:w])
    v.tensor_mul(tmp2[0:32, 0:w], src[0:32, 0:w], cos[0:32, 0:w])
    chain.inc(v.tensor_mul(tmp2[32:64, 0:w], src[32:64, 0:w], cos[32:64, 0:w]))
    v.wait_ge(chain.h, chain.v)
    chain.inc(v.tensor_sub(out_bf[0:32, 0:w], tmp2[0:32, 0:w], tmp[0:32, 0:w]))
    return v.tensor_add(out_bf[32:64, 0:w], tmp2[32:64, 0:w], tmp[32:64, 0:w])


def emit_bm(p, nc, mask_rows, Gmat_d, bmp, bm_nch, entry_waits=()):
    g_ld = p.sem()
    m_ld = [p.sem(), p.sem()]
    ev6 = p.sem()
    mm6 = p.sem()
    ev6b = p.sem()
    ch6 = p.sem()
    stBM = p.sem()
    with nc.sbuf_tensor("bm_G", [128, 4], F32, allow_name_mangling=True) as Gmat_s, \
         nc.sbuf_tensor("mbuf", [128, 2 * D], I32, allow_name_mangling=True) as mbuf, \
         nc.sbuf_tensor("mf32", [128, 2 * D], F32, allow_name_mangling=True) as mf32, \
         nc.sbuf_tensor("mred", [128, 128], F32, allow_name_mangling=True) as mred, \
         nc.sbuf_tensor("bms", [4, bm_nch * 128], F32, allow_name_mangling=True) as bms, \
         nc.psum_tensor("pbm", [4, 128], F32, allow_name_mangling=True) as pbm, \
         nc.Block() as blk:
        @blk.sync
        def _(s):
            for ctr, val in entry_waits:
                s.wait_ge(ctr.h, val)
            g_ld.inc(s.dma_start(out=Gmat_s[:], in_=Gmat_d), 16)
            for ch in range(bm_nch):
                if ch >= 2:
                    s.wait_ge(ev6.h, ch - 1)
                m_ld[ch % 2].inc(s.dma_start(
                    out=mbuf[:, (ch % 2) * D:(ch % 2 + 1) * D],
                    in_=mask_rows[ch * 128:(ch + 1) * 128, :]), 16)

        @blk.vector
        def _(v):
            for step in range(bm_nch + 2):
                if step < bm_nch:
                    v.wait_ge(m_ld[step % 2].h, 16 * (step // 2 + 1))
                    if step >= 2:
                        v.wait_ge(ev6.h, step - 1)
                    ch6.inc(v.tensor_copy(mf32[:, (step % 2) * D:(step % 2 + 1) * D],
                                          mbuf[:, (step % 2) * D:(step % 2 + 1) * D]))
                if 1 <= step <= bm_nch:
                    rch = step - 1
                    v.wait_ge(ch6.h, rch + 1)
                    if rch >= 1:
                        v.wait_ge(mm6.h, rch)
                    ev6.inc(v.tensor_reduce(
                        mred[:, :],
                        mf32[:, (rch % 2) * D:(rch % 2 + 1) * D].rearrange(
                            "p (a b) -> p a b", b=32),
                        AX.X, ALU.add))
                if step >= 2:
                    cch = step - 2
                    v.wait_ge(mm6.h, cch + 1)
                    ev6b.inc(v.tensor_copy(bms[:, cch * 128:(cch + 1) * 128], pbm[:]))

        @blk.tensor
        def _(t):
            for ctr, val in entry_waits:
                t.wait_ge(ctr.h, val)
            t.wait_ge(g_ld.h, 16)
            for ch in range(bm_nch):
                t.wait_ge(ev6.h, ch + 1)
                if ch >= 1:
                    t.wait_ge(ev6b.h, ch)
                mm6.inc(t.matmul(pbm[:], Gmat_s[:, :], mred[:, :],
                                 start=True, stop=True))

        @blk.scalar
        def _(s):
            s.wait_ge(ev6b.h, bm_nch)
            stBM.inc(s.dma_start(
                out=bmp.rearrange("(c p) w -> p c w", p=4),
                in_=bms[:, :].rearrange("p (c w) -> p c w", c=bm_nch)), 16)
    return [stBM]


def emit_launch_A(p, io, do_q=True, do_kv=True, do_bm=True, bm_nch=4):
    nc = p.nc
    _es = ExitStack()
    def _sbl(name, shape, dt=F32):
        return _es.enter_context(
            nc.sbuf_tensor(name, shape, dt, allow_name_mangling=True))
    xT_src = io["xT"]
    w_qa = io.get("w_qa")
    g_qa = io.get("g_qa")
    w_qb = io.get("w_qb")
    w_kva = io["w_kva"]
    g_kva = io["g_kva"]
    w_kvb = io["w_kvb"]
    cos_own = io["cos_own"]
    sin_own = io["sin_own"]
    mask_rows = io.get("mask_rows")
    Gmat_d = io.get("Gmat")
    ident_bf_d = io["ident_bf"]
    ident_f32_d = io["ident_f32"]
    ones_f32_d = io["ones_f32"]
    Qf = io.get("Qf")
    KN = io["KN"]
    KR = io["KR"]
    Vt = io["Vt"]
    bmp = io.get("bmp")

    ident_bf = _sbl("ident_bf_s", [128, 128], BF16)
    ident_f32 = _sbl("ident_f32_s", [128, 128])
    ones_f32 = _sbl("ones_f32_s", [1, 128])
    cos_sb = _sbl("cos_sb", [RD, TOK])
    sin_sb = _sbl("sin_sb", [RD, TOK])
    g_qa_b = _sbl("g_qa_b", [128, QR])
    g_kva_b = _sbl("g_kva_b", [128, KVR])
    g_row = _sbl("g_row", [1, QR])
    g_row2 = _sbl("g_row2", [1, KVR])
    qcT = _sbl("qcT", [128, 12 * TOK], BF16)
    kcT = _sbl("kcT", [128, 4 * TOK], BF16)
    kr_f32 = _sbl("kr_f32", [128, NT * RD])
    rope_tmp = _sbl("rope_tmp", [64, TOK])
    rope_tmp2 = _sbl("rope_tmp2", [64, TOK])
    rope_chain = p.sem()

    all_stores = []

    def new_store(nslots=1):
        cs = [p.sem() for _ in range(nslots)]
        all_stores.extend(cs)
        return cs

    c_ld = p.sem()
    mm0 = p.sem()
    ev0 = p.sem()
    with nc.psum_tensor("pg", [128, QR], allow_name_mangling=True) as pg, \
         nc.psum_tensor("pg2", [128, KVR], allow_name_mangling=True) as pg2, \
         nc.Block() as blk:
        @blk.sync
        def _(s):
            pairs = [(ident_bf[:], ident_bf_d), (ident_f32[:], ident_f32_d),
                     (ones_f32[:], ones_f32_d),
                     (cos_sb[:], cos_own), (sin_sb[:], sin_own)]
            if do_kv:
                pairs.append((g_row2[:], g_kva))
            if do_q:
                pairs.append((g_row[:], g_qa))
            for dst, srcd in pairs:
                c_ld.inc(s.dma_start(out=dst, in_=srcd), 16)

        @blk.tensor
        def _(t):
            t.wait_ge(c_ld.h, c_ld.v)
            last = None
            if do_q:
                for j in range(3):
                    last = t.matmul(pg[:, j * 512:(j + 1) * 512], ones_f32[:],
                                    g_row[0:1, j * 512:(j + 1) * 512],
                                    start=True, stop=True)
            if do_kv:
                last = t.matmul(pg2[:], ones_f32[:], g_row2[0:1, :],
                                start=True, stop=True)
            mm0.inc(last)

        @blk.vector
        def _(v):
            v.wait_ge(mm0.h, 1)
            last = None
            if do_q:
                last = v.tensor_copy(g_qa_b[:], pg[:])
            if do_kv:
                last = v.tensor_copy(g_kva_b[:], pg2[:])
            ev0.inc(last)

    with nc.sbuf_tensor("xT", [128, 32 * TOK], BF16, allow_name_mangling=True) as xT, \
         nc.sbuf_tensor("qc_bf", [128, NT * QR], BF16, allow_name_mangling=True) as qc_bf, \
         nc.sbuf_tensor("kc_bf", [128, NT * KVR], BF16, allow_name_mangling=True) as kc_bf, \
         nc.sbuf_tensor("ss_q", [128, NT * 3], F32, allow_name_mangling=True) as ss_q, \
         nc.sbuf_tensor("ss_k", [128, NT], F32, allow_name_mangling=True) as ss_k:

        x_ld = [p.sem(), p.sem()]
        with nc.Block() as blk:
            @blk.gpsimd
            def _(g):
                for c in range(32):
                    x_ld[c % 2].inc(g.dma_start(
                        out=chunk_ap(xT, TOK, c, 0, TOK),
                        in_=xT_src[c * 128:(c + 1) * 128, :]), 16)
        xT_ready = [(x_ld[0], x_ld[0].v), (x_ld[1], x_ld[1].v), (ev0, ev0.v)]

        def proj(w_dram, nkc, dout, out_bf, ss, tag, prev_psum_free,
                 prev_sbuf_free=(), extra_cols=None, extra_dst=None):
            w_ld = [p.sem() for _ in range(4)]
            w_cst = p.sem()
            w_fr = p.sem()
            ev = p.sem()
            sq = p.sem()
            ndg = dout // 512
            ngrp = ndg + (1 if extra_cols else 0)
            with nc.sbuf_tensor(f"wbuf_{tag}", [128, 4 * 512], BF16, allow_name_mangling=True) as wbuf, \
                 nc.sbuf_tensor(f"wraw_{tag}", [128, 4 * 512], F32, allow_name_mangling=True) as wraw, \
                 nc.sbuf_tensor(f"scr_{tag}", [128, 2 * 512], F32, allow_name_mangling=True) as scr, \
                 nc.psum_tensor(f"pq_{tag}", [128, 4096], allow_name_mangling=True) as pq, \
                 nc.Block() as blk:
                def _gw(dg):
                    return 512 if dg < ndg else extra_cols[1] - extra_cols[0]

                @blk.sync
                def _(s):
                    for ctr, val in xT_ready:
                        s.wait_ge(ctr.h, val)
                    for ctr, val in prev_sbuf_free:
                        s.wait_ge(ctr.h, val)
                    for dg in range(ngrp):
                        for kc in range(nkc):
                            idx = dg * nkc + kc
                            if idx >= 4:
                                s.wait_ge(w_cst.h, idx - 3)
                            if dg < ndg:
                                srcd = w_dram[kc * 128:(kc + 1) * 128,
                                              dg * 512:(dg + 1) * 512]
                            else:
                                srcd = w_dram[kc * 128:(kc + 1) * 128,
                                              extra_cols[0]:extra_cols[1]]
                            w = _gw(dg)
                            w_ld[idx % 4].inc(s.dma_start(
                                out=wraw[:, (idx % 4) * 512:(idx % 4) * 512 + w],
                                in_=srcd), 16)

                @blk.tensor
                def _(t):
                    for ctr, val in prev_psum_free:
                        t.wait_ge(ctr.h, val)
                    for dg in range(ngrp):
                        w = _gw(dg)
                        if dg > 0:
                            t.wait_ge(ev.h, dg * NT)
                            t.wait_ge(sq.h, min(dg, ndg) * NT)
                        for kc in range(nkc):
                            idx = dg * nkc + kc
                            t.wait_ge(w_cst.h, idx + 1)
                            last = None
                            for tt in range(NT):
                                last = t.matmul(
                                    pq[:, tt * 512:tt * 512 + w],
                                    chunk_ap(xT, TOK, kc, tt * 128, 128),
                                    wbuf[:, (idx % 4) * 512:(idx % 4) * 512 + w],
                                    start=(kc == 0), stop=(kc == nkc - 1))
                            w_fr.inc(last)

                @blk.vector
                def _(v):
                    def wcast(idx):
                        w = _gw(idx // nkc)
                        v.wait_ge(w_ld[idx % 4].h, 16 * (idx // 4 + 1))
                        if idx >= 4:
                            v.wait_ge(w_fr.h, idx - 3)
                        w_cst.inc(v.tensor_copy(
                            wbuf[:, (idx % 4) * 512:(idx % 4) * 512 + w],
                            wraw[:, (idx % 4) * 512:(idx % 4) * 512 + w]))

                    def emit_ev(dg):
                        w = _gw(dg)
                        v.wait_ge(w_fr.h, (dg + 1) * nkc)
                        for tt in range(NT):
                            if dg < ndg:
                                dst = out_bf[:, tt * dout + dg * 512:
                                             tt * dout + dg * 512 + w]
                            else:
                                dst = extra_dst[:, tt * w:(tt + 1) * w]
                            ev.inc(v.tensor_copy(dst, pq[:, tt * 512:tt * 512 + w]))

                    for dg in range(ngrp):
                        for kc in range(nkc):
                            wcast(dg * nkc + kc)
                            if kc == 0 and dg > 0:
                                emit_ev(dg - 1)
                    emit_ev(ngrp - 1)

                @blk.scalar
                def _(s):
                    for dg in range(ndg):
                        for tt in range(NT):
                            s.wait_ge(ev.h, dg * NT + tt + 1)
                            idx = dg * NT + tt
                            if idx >= 2:
                                s.wait_ge(sq.h, idx - 1)
                            sq.inc(s.activation(
                                scr[:, (idx % 2) * 512:(idx % 2 + 1) * 512],
                                pq[:, tt * 512:(tt + 1) * 512],
                                AF.Square,
                                accum_out=ss[:, tt * ndg + dg:tt * ndg + dg + 1]))
            return w_fr, ev, sq

        def rmsnorm_T(src_bf, ss, g_bcast, dim, dstT, sq_prev, tag,
                      psum_free_waits):
            ev_a = p.sem()
            sq_b = p.sem()
            ev_c = p.sem()
            tp_d = p.sem()
            ev_e = p.sem()
            ch1 = p.sem()
            ch2 = p.sem()
            ch3 = p.sem()
            ndg = dim // 512
            nch = dim // 128
            nb = (nch + 7) // 8
            sq_prev_base = sq_prev.v - ndg * NT
            with nc.sbuf_tensor(f"ssum_{tag}", [128, NT], F32, allow_name_mangling=True) as ssum, \
                 nc.sbuf_tensor(f"rms_{tag}", [128, NT], F32, allow_name_mangling=True) as rms, \
                 nc.sbuf_tensor(f"rcp_{tag}", [128, NT], F32, allow_name_mangling=True) as rcp, \
                 nc.sbuf_tensor(f"rcp2_{tag}", [128, NT], F32, allow_name_mangling=True) as rcp2, \
                 nc.sbuf_tensor(f"ybf_{tag}", [128, 2 * dim], BF16, allow_name_mangling=True) as ybf, \
                 nc.psum_tensor(f"ptr_{tag}", [128, 2048], BF16, allow_name_mangling=True) as ptr, \
                 nc.Block() as blk:
                @blk.vector
                def _(v):
                    if ndg == 1:
                        for tt in range(NT):
                            v.wait_ge(sq_prev.h, sq_prev_base + tt + 1)
                            ev_a.inc(v.tensor_copy(ssum[:, tt:tt + 1], ss[:, tt:tt + 1]))
                    else:
                        for tt in range(NT):
                            v.wait_ge(sq_prev.h, sq_prev_base + (ndg - 1) * NT + tt + 1)
                            ch1.inc(v.tensor_add(ssum[:, tt:tt + 1],
                                                 ss[:, tt * ndg:tt * ndg + 1],
                                                 ss[:, tt * ndg + 1:tt * ndg + 2]))
                        for tt in range(NT):
                            v.wait_ge(ch1.h, tt + 1)
                            ev_a.inc(v.tensor_add(ssum[:, tt:tt + 1], ssum[:, tt:tt + 1],
                                                  ss[:, tt * ndg + 2:tt * ndg + 3]))

                @blk.scalar
                def _(s):
                    for tt in range(NT):
                        s.wait_ge(ev_a.h, tt + 1)
                        sq_b.inc(s.activation(rms[:, tt:tt + 1], ssum[:, tt:tt + 1],
                                              AF.Sqrt, scale=1.0 / dim))

                @blk.vector
                def _(v):
                    for tt in range(NT):
                        v.wait_ge(sq_b.h, tt + 1)
                        ch2.inc(v.tensor_scalar_add(rcp[:, tt:tt + 1],
                                                    rms[:, tt:tt + 1], EPS_RMS))
                    for tt in range(NT):
                        v.wait_ge(ch2.h, tt + 1)
                        ch3.inc(v.reciprocal(rcp2[:, tt:tt + 1], rcp[:, tt:tt + 1]))
                    def emit_evac(v, tt):
                        for b in range(nb):
                            n_in_b = min(8, nch - b * 8)
                            idx = tt * nb + b
                            v.wait_ge(tp_d.h, idx + 1)
                            po = (idx % 2) * 1024
                            ev_e.inc(v.tensor_copy(
                                chunks_ap(dstT, TOK, b * 8, tt, n_in_b),
                                ptr[:, po:po + n_in_b * 128]))

                    for tt in range(NT):
                        v.wait_ge(ch3.h, tt + 1)
                        if tt >= 2:
                            v.wait_ge(tp_d.h, (tt - 1) * nb)
                        last = None
                        for dg in range(ndg):
                            last = v.scalar_tensor_tensor(
                                ybf[:, (tt % 2) * dim + dg * 512:(tt % 2) * dim + (dg + 1) * 512],
                                src_bf[:, tt * dim + dg * 512:tt * dim + (dg + 1) * 512],
                                rcp2[:, tt:tt + 1],
                                g_bcast[:, dg * 512:(dg + 1) * 512],
                                ALU.mult, ALU.mult)
                        ev_c.inc(last)
                        if tt >= 1:
                            emit_evac(v, tt - 1)
                    emit_evac(v, NT - 1)

                @blk.tensor
                def _(t):
                    for ctr, val in psum_free_waits:
                        t.wait_ge(ctr.h, val)
                    for tt in range(NT):
                        t.wait_ge(ev_c.h, tt + 1)
                        for b in range(nb):
                            n_in_b = min(8, nch - b * 8)
                            idx = tt * nb + b
                            if idx >= 2:
                                t.wait_ge(ev_e.h, idx - 1)
                            po = (idx % 2) * 1024
                            last = None
                            for c8 in range(n_in_b):
                                c = b * 8 + c8
                                last = t.transpose(
                                    ptr[:, po + c8 * 128:po + (c8 + 1) * 128],
                                    ybf[:, (tt % 2) * dim + c * 128:(tt % 2) * dim + (c + 1) * 128],
                                    ident_bf[:])
                            tp_d.inc(last)
            return ev_c, ev_e, tp_d

        evT_q = evT_k = ev3 = None
        if do_q:
            wfr2, ev2, sq2 = proj(w_qa, 32, QR, qc_bf, ss_q, "qc",
                                  prev_psum_free=list(xT_ready))
            evc_q, evT_q, tpd_q = rmsnorm_T(qc_bf, ss_q, g_qa_b, QR, qcT,
                                            sq2, "qn",
                                            psum_free_waits=[(ev2, 3 * NT),
                                                             (sq2, 3 * NT)])
        if do_kv:
            if do_q:
                wfr3, ev3, sq3 = proj(w_kva, 32, KVR, kc_bf, ss_k, "kc",
                                      prev_psum_free=[(ev2, 3 * NT), (sq2, 3 * NT),
                                                      (evT_q, evT_q.v)],
                                      prev_sbuf_free=[(wfr2, 96)],
                                      extra_cols=(KVR, KVR + RD), extra_dst=kr_f32)
            else:
                wfr3, ev3, sq3 = proj(w_kva, 32, KVR, kc_bf, ss_k, "kc",
                                      prev_psum_free=list(xT_ready),
                                      extra_cols=(KVR, KVR + RD), extra_dst=kr_f32)
            evc_k, evT_k, tpd_k = rmsnorm_T(kc_bf, ss_k, g_kva_b, KVR, kcT,
                                            sq3, "kn",
                                            psum_free_waits=[(ev3, 2 * NT),
                                                             (sq3, NT)])

    if do_kv:
        tp3c = p.sem()
        ev3c = p.sem()
        stKR = new_store(1)[0]
        with nc.sbuf_tensor("kr_bf", [RD, TOK], BF16, allow_name_mangling=True) as kr_bf, \
             nc.psum_tensor("prope", [RD, TOK], allow_name_mangling=True) as prope, \
             nc.Block() as blk:
            @blk.tensor
            def _(t):
                t.wait_ge(evT_k.h, evT_k.v)
                t.wait_ge(ev3.h, 2 * NT)
                last = None
                for tt in range(NT):
                    last = t.transpose(prope[:, tt * 128:(tt + 1) * 128],
                                       kr_f32[:, tt * RD:(tt + 1) * RD],
                                       ident_f32[:])
                tp3c.inc(last)

            @blk.vector
            def _(v):
                v.wait_ge(tp3c.h, 1)
                ev3c.inc(emit_rope_fm(v, kr_bf, prope, cos_sb, sin_sb,
                                      rope_tmp, rope_tmp2, TOK, rope_chain))

            @blk.scalar
            def _(s):
                s.wait_ge(ev3c.h, 1)
                stKR.inc(s.dma_start(out=KR, in_=kr_bf[:]), 16)

    if do_q:
        w_ld4 = [p.sem() for _ in range(4)]
        w_cst4 = p.sem()
        mm4 = p.sem()
        ev4 = p.sem()
        stQ = new_store(2)
        with nc.sbuf_tensor("wqb_sb", [128, 2 * 12 * 768], BF16, allow_name_mangling=True) as wqb_sb, \
             nc.sbuf_tensor("wqb_raw", [128, 4 * 768], F32, allow_name_mangling=True) as wqb_raw, \
             nc.sbuf_tensor("qhi_bf", [128, 2 * TOK], BF16, allow_name_mangling=True) as qhi_bf, \
             nc.sbuf_tensor("qlo_bf", [64, 2 * TOK], BF16, allow_name_mangling=True) as qlo_bf, \
             nc.psum_tensor("phi", [128, 2 * TOK], allow_name_mangling=True) as phi, \
             nc.psum_tensor("plo", [64, 2 * TOK], allow_name_mangling=True) as plo, \
             nc.Block() as blk:
            @blk.sync
            def _(s):
                s.wait_ge(evT_q.h, evT_q.v)
                if do_kv:
                    s.wait_ge(evT_k.h, evT_k.v)
                    s.wait_ge(stKR.h, 16)
                for hg in range(8):
                    for c in range(12):
                        idx = hg * 12 + c
                        if idx >= 4:
                            s.wait_ge(w_cst4.h, idx - 3)
                        w_ld4[idx % 4].inc(s.dma_start(
                            out=wqb_raw[:, (idx % 4) * 768:(idx % 4 + 1) * 768],
                            in_=w_qb[c * 128:(c + 1) * 128, hg * 768:(hg + 1) * 768]), 16)

            @blk.tensor
            def _(t):
                if do_kv:
                    t.wait_ge(ev3c.h, 1)
                for h in range(NH):
                    hg, hh = divmod(h, 4)
                    par = h % 2
                    t.wait_ge(w_cst4.h, 12 * (hg + 1))
                    if h >= 2:
                        t.wait_ge(ev4.h, h - 1)
                    last = None
                    for c in range(12):
                        wb = (hg % 2) * 9216 + c * 768 + hh * 192
                        for half in range(2):
                            t.matmul(phi[:, par * TOK + half * 512:par * TOK + half * 512 + 512],
                                     wqb_sb[:, wb:wb + 128],
                                     chunk_ap(qcT, TOK, c, half * 512, 512),
                                     start=(c == 0), stop=(c == 11))
                            last = t.matmul(
                                plo[:, par * TOK + half * 512:par * TOK + half * 512 + 512],
                                wqb_sb[:, wb + 128:wb + 192],
                                chunk_ap(qcT, TOK, c, half * 512, 512),
                                start=(c == 0), stop=(c == 11))
                    mm4.inc(last)

            @blk.vector
            def _(v):
                def qcast(idx):
                    hg_ = idx // 12
                    v.wait_ge(w_ld4[idx % 4].h, 16 * (idx // 4 + 1))
                    if idx % 12 == 0 and hg_ >= 2:
                        v.wait_ge(mm4.h, 4 * (hg_ - 1))
                    w_cst4.inc(v.tensor_copy(
                        wqb_sb[:, (hg_ % 2) * 9216 + (idx % 12) * 768:
                               (hg_ % 2) * 9216 + (idx % 12 + 1) * 768],
                        wqb_raw[:, (idx % 4) * 768:(idx % 4 + 1) * 768]))

                def qrope(h):
                    par = h % 2
                    v.wait_ge(mm4.h, h + 1)
                    if h >= 2:
                        v.wait_ge(stQ[h % 2].h, 32 * (h // 2))
                    v.tensor_copy(qhi_bf[:, par * TOK:(par + 1) * TOK],
                                  phi[:, par * TOK:(par + 1) * TOK])
                    pd = ((ev3c, 1) if do_kv else None) if h == 0 else (ev4, h)
                    ev4.inc(emit_rope_fm(
                        v, qlo_bf[:, par * TOK:(par + 1) * TOK],
                        plo[:, par * TOK:(par + 1) * TOK],
                        cos_sb, sin_sb, rope_tmp, rope_tmp2, TOK, rope_chain,
                        prev_done=pd))

                for hg in range(8):
                    for c in range(12):
                        qcast(hg * 12 + c)
                    if hg >= 1:
                        for hh in range(4):
                            qrope((hg - 1) * 4 + hh)
                for hh in range(4):
                    qrope(7 * 4 + hh)

            @blk.scalar
            def _(s):
                for h in range(NH):
                    par = h % 2
                    s.wait_ge(ev4.h, h + 1)
                    stQ[h % 2].inc(s.dma_start(out=Qf[h * QK:h * QK + ND, :],
                                               in_=qhi_bf[:, par * TOK:(par + 1) * TOK]), 16)
                    stQ[h % 2].inc(s.dma_start(out=Qf[h * QK + ND:(h + 1) * QK, :],
                                               in_=qlo_bf[:, par * TOK:(par + 1) * TOK]), 16)

    if do_kv:
        kv_ld = [p.sem(), p.sem()]
        kv_cst = p.sem()
        mm5 = p.sem()
        ev5 = p.sem()
        mm5v = p.sem()
        ev5v = p.sem()
        stKN = new_store(2)
        stV = new_store(2)
        with nc.sbuf_tensor("wkvb_sb", [128, 4 * 8192], BF16, allow_name_mangling=True) as wkvb_sb, \
             nc.sbuf_tensor("wkvb_raw", [128, 2 * 4096], F32, allow_name_mangling=True) as wkvb_raw, \
             nc.sbuf_tensor("kn_bf", [128, 2 * TOK], BF16, allow_name_mangling=True) as kn_bf, \
             nc.sbuf_tensor("v_bf", [128, 2 * NH * VD], BF16, allow_name_mangling=True) as v_bf, \
             nc.psum_tensor("pkn", [128, 2 * TOK], allow_name_mangling=True) as pkn, \
             nc.psum_tensor("pv", [128, 2 * 512], allow_name_mangling=True) as pv, \
             nc.Block() as blk:
            @blk.sync
            def _(s):
                if do_q:
                    s.wait_ge(mm4.h, NH)
                    s.wait_ge(stQ[0].h, stQ[0].v)
                    s.wait_ge(stQ[1].h, stQ[1].v)
                else:
                    s.wait_ge(evT_k.h, evT_k.v)
                    s.wait_ge(stKR.h, 16)
                for idx in range(8):
                    r, hf = divmod(idx, 2)
                    if idx >= 2:
                        s.wait_ge(kv_cst.h, idx - 1)
                    kv_ld[idx % 2].inc(s.dma_start(
                        out=wkvb_raw[:, (idx % 2) * 4096:(idx % 2 + 1) * 4096],
                        in_=w_kvb[r * 128:(r + 1) * 128,
                                  hf * 4096:(hf + 1) * 4096]), 16)

            @blk.tensor
            def _(t):
                t.wait_ge(kv_cst.h, 8)
                if do_q:
                    t.wait_ge(ev4.h, NH)
                else:
                    t.wait_ge(ev3c.h, 1)
                for h in range(NH):
                    par = h % 2
                    if h >= 2:
                        t.wait_ge(ev5.h, h - 1)
                    last = None
                    for r in range(4):
                        for half in range(2):
                            last = t.matmul(
                                pkn[:, par * TOK + half * 512:par * TOK + (half + 1) * 512],
                                wkvb_sb[:, r * 8192 + h * 256:r * 8192 + h * 256 + 128],
                                chunk_ap(kcT, TOK, r, half * 512, 512),
                                start=(r == 0), stop=(r == 3))
                    mm5.inc(last)
                for tt in range(NT):
                    for hg in range(8):
                        idx = tt * 8 + hg
                        par = idx % 2
                        if idx >= 2:
                            t.wait_ge(ev5v.h, idx - 1)
                        last = None
                        for r in range(4):
                            rhs = wkvb_sb[:, :].rearrange(
                                "p (r h c) -> p r h c", r=4, h=NH)[
                                :, r, hg * 4:(hg + 1) * 4, 128:256]
                            last = t.matmul(
                                pv[:, par * 512:(par + 1) * 512],
                                chunk_ap(kcT, TOK, r, tt * 128, 128),
                                rhs, start=(r == 0), stop=(r == 3))
                        mm5v.inc(last)

            @blk.vector
            def _(v):
                for idx in range(8):
                    r, hf = divmod(idx, 2)
                    v.wait_ge(kv_ld[idx % 2].h, 16 * (idx // 2 + 1))
                    kv_cst.inc(v.tensor_copy(
                        wkvb_sb[:, r * 8192 + hf * 4096:
                                r * 8192 + (hf + 1) * 4096],
                        wkvb_raw[:, (idx % 2) * 4096:(idx % 2 + 1) * 4096]))
                for h in range(NH):
                    par = h % 2
                    v.wait_ge(mm5.h, h + 1)
                    if h >= 2:
                        v.wait_ge(stKN[h % 2].h, 16 * (h // 2))
                    ev5.inc(v.tensor_copy(kn_bf[:, par * TOK:(par + 1) * TOK],
                                          pkn[:, par * TOK:(par + 1) * TOK]))
                for tt in range(NT):
                    for hg in range(8):
                        idx = tt * 8 + hg
                        par = idx % 2
                        v.wait_ge(mm5v.h, idx + 1)
                        if hg == 0 and tt >= 2:
                            v.wait_ge(stV[tt % 2].h, 16 * (tt // 2))
                        ev5v.inc(v.tensor_copy(
                            v_bf[:, (tt % 2) * NH * VD + hg * 512:(tt % 2) * NH * VD + (hg + 1) * 512],
                            pv[:, par * 512:(par + 1) * 512]))

            @blk.scalar
            def _(s):
                for h in range(NH):
                    par = h % 2
                    s.wait_ge(ev5.h, h + 1)
                    stKN[h % 2].inc(s.dma_start(out=KN[h * ND:(h + 1) * ND, :],
                                                in_=kn_bf[:, par * TOK:(par + 1) * TOK]), 16)
                for tt in range(NT):
                    s.wait_ge(ev5v.h, (tt + 1) * 8)
                    stV[tt % 2].inc(s.dma_start(out=Vt[tt * 128:(tt + 1) * 128, :],
                                                in_=v_bf[:, (tt % 2) * NH * VD:(tt % 2 + 1) * NH * VD]), 16)

    if do_bm:
        ew = []
        if do_kv:
            ew = [(mm5v, NT * 8), (ev5v, NT * 8)] + \
                [(cs, cs.v) for cs in (stKN + stV)]
        all_stores.extend(emit_bm(p, nc, mask_rows, Gmat_d, bmp, bm_nch,
                                  entry_waits=ew))

    _es.close()
    return all_stores


def build_launch_A():
    p = Prog()
    io = dict(
        xT=p.inp("xT", [D, TOK]),
        w_qa=p.inp("w_qa", [D, QR]),
        g_qa=p.inp("g_qa", [1, QR]),
        w_qb=p.inp("w_qb", [QR, NH * QK]),
        w_kva=p.inp("w_kva", [D, KVR + RD]),
        g_kva=p.inp("g_kva", [1, KVR]),
        w_kvb=p.inp("w_kvb", [KVR, NH * (ND + VD)]),
        cos_own=p.inp("cos_own", [RD, TOK]),
        sin_own=p.inp("sin_own", [RD, TOK]),
        mask_rows=p.inp("mask_rows", [512, D], I32),
        Gmat=p.inp("Gmat", [128, 4]),
        ident_bf=p.inp("ident_bf", [128, 128], BF16),
        ident_f32=p.inp("ident_f32", [128, 128]),
        ones_f32=p.inp("ones_f32", [1, 128]),
        Qf=p.outp("Qf", [NH * QK, TOK], BF16),
        KN=p.outp("KN", [NH * ND, TOK], BF16),
        KR=p.outp("KR", [RD, TOK], BF16),
        Vt=p.outp("Vt", [TOK, NH * VD], BF16),
        bmp=p.outp("bmp", [16, 128]),
    )
    all_stores = emit_launch_A(p, io)
    with p.nc.Block() as blk:
        @blk.sync
        def _(s):
            for cs in all_stores:
                if cs.v:
                    s.wait_ge(cs.h, cs.v)
    return p


def emit_launch_B(p, io):
    nc = p.nc
    _es = ExitStack()
    def _sbl(name, shape, dt=F32):
        return _es.enter_context(
            nc.sbuf_tensor(name, shape, dt, allow_name_mangling=True))
    Qf = io["Qf"]
    KN_b = io["KN_b"]
    KR_b = io["KR_b"]
    Vt_b = io["Vt_b"]
    ident_bf_d = io["ident_bf"]
    attn = io["attn"]

    ident_bf = _sbl("b_ident", [128, 128], BF16)
    KR_sb = _sbl("b_KR", [RD, S], BF16)
    attn_sb = _sbl("b_attn", [128, NT * NH * VD], BF16)
    qhi = _sbl("b_qhi", [128, 2 * TOK], BF16)
    qlo = _sbl("b_qlo", [64, 2 * TOK], BF16)
    KN_h = _sbl("b_KN", [128, 2 * S], BF16)
    V_h = _sbl("b_V", [128, 2 * S], BF16)
    p_sb = _sbl("b_p", [128, S], BF16)
    pT_sb = _sbl("b_pT", [128, S], BF16)
    m_col = _sbl("b_m", [128, 1])
    bias_col = _sbl("b_bias", [128, 1])
    s8 = _sbl("b_s8", [128, 8])
    S_col = _sbl("b_S", [128, 2])
    tmpO = _sbl("b_tmpO", [128, VD])

    c_ld = p.sem()
    qld = [p.sem(), p.sem()]
    mmq = p.sem()
    dv = p.sem()
    ex = p.sem()
    tp_b = p.sem()
    evp = p.sem()
    mm_av = p.sem()
    dn = p.sem()
    st_at = p.sem()

    ps = _es.enter_context(nc.psum_tensor(
        "b_ps", [128, 4096], F32, allow_name_mangling=True))
    ptr_v = ps[:, 512:1536].bitcast(BF16)
    po = ps[:, 0:VD]

    NIT = NH * NT

    with nc.Block() as blk:
        @blk.sync
        def _(s):
            c_ld.inc(s.dma_start(out=ident_bf[:], in_=ident_bf_d), 16)
            c_ld.inc(s.dma_start(out=KR_sb[:], in_=KR_b), 16)

        @blk.gpsimd
        def _(g):
            for h in range(NH):
                par = h % 2
                if h >= 2:
                    g.wait_ge(mm_av.h, (h - 1) * NT)
                qld[par].inc(g.dma_start(
                    out=qhi[:, par * TOK:(par + 1) * TOK],
                    in_=Qf[h * QK:h * QK + ND, :]), 16)
                qld[par].inc(g.dma_start(
                    out=qlo[:, par * TOK:(par + 1) * TOK],
                    in_=Qf[h * QK + ND:(h + 1) * QK, :]), 16)
                qld[par].inc(g.dma_start(
                    out=KN_h[:, par * S:(par + 1) * S],
                    in_=KN_b[h * ND:(h + 1) * ND, :]), 16)
                vsrc = Vt_b.rearrange("(kt p) c -> p kt c", p=128)
                for j in range(8):
                    qld[par].inc(g.dma_start(
                        out=V_h[:, par * S + j * 512:par * S + (j + 1) * 512]
                            .rearrange("p (kt v) -> p kt v", kt=4),
                        in_=vsrc[:, 4 * j:4 * (j + 1), h * VD:(h + 1) * VD]), 16)

        @blk.tensor
        def _(t):
            t.wait_ge(c_ld.h, 32)
            for it in range(NIT):
                h, qt = divmod(it, NT)
                par = h % 2
                if qt == 0:
                    t.wait_ge(qld[par].h, 16 * 11 * (h // 2 + 1))
                if it >= 1:
                    t.wait_ge(ex.h, 8 * it)
                    t.wait_ge(evp.h, 4 * it)
                    t.wait_ge(dn.h, 3 * it)
                for c4 in range(8):
                    t.matmul(ps[:, c4 * 512:(c4 + 1) * 512],
                             qhi[:, par * TOK + qt * 128:par * TOK + (qt + 1) * 128],
                             KN_h[:, par * S + c4 * 512:par * S + (c4 + 1) * 512],
                             start=True, stop=False)
                    last = t.matmul(ps[:, c4 * 512:(c4 + 1) * 512],
                                    qlo[:, par * TOK + qt * 128:par * TOK + (qt + 1) * 128],
                                    KR_sb[:, c4 * 512:(c4 + 1) * 512],
                                    start=False, stop=True)
                mmq.inc(last)
                t.wait_ge(ex.h, 8 * (it + 1))
                for b in range(4):
                    if b >= 2:
                        t.wait_ge(evp.h, 4 * it + b - 1)
                    po_off = (b % 2) * 1024
                    for c8 in range(8):
                        kt = b * 8 + c8
                        last = t.transpose(
                            ptr_v[:, po_off + c8 * 128:po_off + (c8 + 1) * 128],
                            p_sb[:, kt * 128:(kt + 1) * 128],
                            ident_bf[:])
                    tp_b.inc(last)
                for b in range(4):
                    t.wait_ge(evp.h, 4 * it + b + 1)
                    for c8 in range(8):
                        kt = b * 8 + c8
                        last = t.matmul(
                            po, pT_sb[:, kt * 128:(kt + 1) * 128],
                            V_h[:, par * S + kt * 128:par * S + (kt + 1) * 128],
                            start=(kt == 0), stop=(kt == 31))
                mm_av.inc(last)

        @blk.vector
        def _(v):
            for it in range(NIT):
                h, qt = divmod(it, NT)
                v.wait_ge(mmq.h, it + 1)
                if it >= 1:
                    v.wait_ge(dv.h, 2 * it)
                dv.inc(v.tensor_reduce(m_col[:], ps[:, :], AX.X, ALU.max,
                                       negate=True))
                v.wait_ge(dv.h, 2 * it + 1)
                dv.inc(v.tensor_scalar_mul(bias_col[:], m_col[:], SM_SCALE))
                if it >= 1:
                    v.wait_ge(mm_av.h, it)
                for b in range(4):
                    v.wait_ge(tp_b.h, 4 * it + b + 1)
                    po_off = (b % 2) * 1024
                    evp.inc(v.tensor_copy(
                        pT_sb[:, b * 1024:(b + 1) * 1024],
                        ptr_v[:, po_off:po_off + 1024]))
                v.wait_ge(ex.h, 8 * (it + 1))
                if it >= 1:
                    v.wait_ge(dn.h, 3 * it - 1)
                dn.inc(v.tensor_reduce(S_col[:, 0:1], s8[:], AX.X, ALU.add))
                v.wait_ge(dn.h, 3 * it + 1)
                dn.inc(v.reciprocal(S_col[:, 1:2], S_col[:, 0:1]))
                v.wait_ge(dn.h, 3 * it + 2)
                v.wait_ge(mm_av.h, it + 1)
                dn.inc(v.tensor_scalar_mul(
                    attn_sb[:, qt * NH * VD + h * VD:qt * NH * VD + (h + 1) * VD],
                    po, S_col[:, 1:2]))

        @blk.scalar
        def _(s):
            for it in range(NIT):
                h, qt = divmod(it, NT)
                s.wait_ge(dv.h, 2 * (it + 1))
                if it >= 1:
                    s.wait_ge(tp_b.h, 4 * it)
                    s.wait_ge(dn.h, 3 * it - 2)
                for c4 in range(8):
                    ex.inc(s.activation(
                        p_sb[:, c4 * 512:(c4 + 1) * 512],
                        ps[:, c4 * 512:(c4 + 1) * 512],
                        AF.Exp, bias=bias_col[:], scale=SM_SCALE,
                        accum_out=s8[:, c4:c4 + 1]))

        @blk.vector
        def _(v):
            v.wait_ge(dn.h, 3 * NIT)
            v.engine_nop()

        @blk.scalar
        def _(s):
            s.wait_ge(dn.h, 3 * NIT)
            for qt in range(NT):
                st_at.inc(s.dma_start(
                    out=attn[qt * 128:(qt + 1) * 128, :],
                    in_=attn_sb[:, qt * NH * VD:(qt + 1) * NH * VD]), 16)

    _es.close()
    return [st_at]


def emit_launch_B2(p, io):
    nc = p.nc
    _es = ExitStack()

    def _sbl(name, shape, dt=F32):
        return _es.enter_context(
            nc.sbuf_tensor(name, shape, dt, allow_name_mangling=True))

    Qf = io["Qf"]
    KN_b = io["KN_b"]
    KR_b = io["KR_b"]
    Vt_b = io["Vt_b"]
    ident_bf_d = io["ident_bf"]
    attn = io["attn"]

    ident_bf = _sbl("b2_ident", [128, 128], BF16)
    KR_sb = _sbl("b2_KR", [RD, S], BF16)
    attn_sb = _sbl("b2_attn", [128, NT * NH * VD], BF16)
    qhi = _sbl("b2_qhi", [128, 2 * TOK], BF16)
    qlo = _sbl("b2_qlo", [64, 2 * TOK], BF16)
    KN_h = _sbl("b2_KN", [128, 2 * S], BF16)
    V_h = _sbl("b2_V", [128, 2 * S], BF16)
    p_sb = _sbl("b2_p", [128, S], BF16)
    pT_sb = _sbl("b2_pT", [128, S], BF16)
    m8 = _sbl("b2_m8", [128, 8])
    mcols = _sbl("b2_m", [128, 4])
    wcol = _sbl("b2_w", [128, 2])
    bias2 = _sbl("b2_bias", [128, 2])
    s8 = _sbl("b2_s8", [128, 8])
    fcol = _sbl("b2_f", [128, 2])
    dcol = _sbl("b2_d", [128, 2])
    scol = _sbl("b2_s", [128, 2])
    Sc = _sbl("b2_S", [128, 2])
    negM = _sbl("b2_negM", [128, 1])
    tmpO = _sbl("b2_t0", [128, VD])
    tmpO2 = _sbl("b2_t1", [128, VD])

    c_ld = p.sem()
    qld = [p.sem(), p.sem()]
    mmq = p.sem()
    dvr = p.sem()
    dvt = p.sem()
    ex = p.sem()
    fx = p.sem()
    tpb = p.sem()
    evp = p.sem()
    mm_av = p.sem()
    dn = p.sem()
    st_at = p.sem()

    ps4 = _es.enter_context(nc.psum_tensor(
        "b2_ps", [128, 2048], F32, allow_name_mangling=True))
    ptr2 = _es.enter_context(nc.psum_tensor(
        "b2_ptr", [128, 2048], BF16, allow_name_mangling=True))
    poA = _es.enter_context(nc.psum_tensor(
        "b2_poA", [128, VD], F32, allow_name_mangling=True))
    poB = _es.enter_context(nc.psum_tensor(
        "b2_poB", [128, VD], F32, allow_name_mangling=True))
    po = [poA, poB]

    NIT = NH * NT
    NRD = NIT * 2

    with nc.Block() as blk:
        @blk.sync
        def _(s):
            c_ld.inc(s.dma_start(out=ident_bf[:], in_=ident_bf_d), 16)
            c_ld.inc(s.dma_start(out=KR_sb[:], in_=KR_b), 16)

        @blk.gpsimd
        def _(g):
            for h in range(NH):
                par = h % 2
                if h >= 2:
                    g.wait_ge(mm_av.h, (h - 1) * NT * 2)
                qld[par].inc(g.dma_start(
                    out=qhi[:, par * TOK:(par + 1) * TOK],
                    in_=Qf[h * QK:h * QK + ND, :]), 16)
                qld[par].inc(g.dma_start(
                    out=qlo[:, par * TOK:(par + 1) * TOK],
                    in_=Qf[h * QK + ND:(h + 1) * QK, :]), 16)
                qld[par].inc(g.dma_start(
                    out=KN_h[:, par * S:(par + 1) * S],
                    in_=KN_b[h * ND:(h + 1) * ND, :]), 16)
                vsrc = Vt_b.rearrange("(kt p) c -> p kt c", p=128)
                for j in range(8):
                    qld[par].inc(g.dma_start(
                        out=V_h[:, par * S + j * 512:par * S + (j + 1) * 512]
                            .rearrange("p (kt v) -> p kt v", kt=4),
                        in_=vsrc[:, 4 * j:4 * (j + 1), h * VD:(h + 1) * VD]), 16)

        def emit_T(t, rd):
            it, r = divmod(rd, 2)
            for b in range(2):
                gb = 2 * rd + b
                if gb >= 2:
                    t.wait_ge(evp.h, gb - 1)
                t.wait_ge(ex.h, 4 * rd + 2 * (b + 1))
                po_off = (gb % 2) * 1024
                last = None
                for c8 in range(8):
                    kt = b * 8 + c8
                    last = t.transpose(
                        ptr2[:, po_off + c8 * 128:po_off + (c8 + 1) * 128],
                        p_sb[:, r * 2048 + kt * 128:r * 2048 + (kt + 1) * 128],
                        ident_bf[:])
                tpb.inc(last)

        def emit_AV(t, rd):
            it, r = divmod(rd, 2)
            h, qt = divmod(it, NT)
            par = h % 2
            t.wait_ge(dn.h, 10 * it)
            last = None
            for b in range(2):
                t.wait_ge(evp.h, 2 * rd + b + 1)
                for c8 in range(8):
                    kt = b * 8 + c8
                    last = t.matmul(
                        po[r][:],
                        pT_sb[:, r * 2048 + kt * 128:r * 2048 + (kt + 1) * 128],
                        V_h[:, par * S + r * 2048 + kt * 128:
                            par * S + r * 2048 + (kt + 1) * 128],
                        start=(kt == 0), stop=(kt == 15))
            mm_av.inc(last)

        @blk.tensor
        def _(t):
            t.wait_ge(c_ld.h, 32)
            for rd in range(NRD):
                it, r = divmod(rd, 2)
                h, qt = divmod(it, NT)
                par = h % 2
                if r == 0 and qt == 0:
                    t.wait_ge(qld[par].h, 16 * 11 * (h // 2 + 1))
                for c4 in range(4):
                    if rd >= 1:
                        t.wait_ge(ex.h, 4 * (rd - 1) + c4 + 1)
                    ko = r * 2048 + c4 * 512
                    t.matmul(ps4[:, c4 * 512:(c4 + 1) * 512],
                             qhi[:, par * TOK + qt * 128:par * TOK + (qt + 1) * 128],
                             KN_h[:, par * S + ko:par * S + ko + 512],
                             start=True, stop=False)
                    last = t.matmul(ps4[:, c4 * 512:(c4 + 1) * 512],
                                    qlo[:, par * TOK + qt * 128:par * TOK + (qt + 1) * 128],
                                    KR_sb[:, ko:ko + 512],
                                    start=False, stop=True)
                    mmq.inc(last)
                if rd >= 1:
                    emit_T(t, rd - 1)
                    emit_AV(t, rd - 1)
            emit_T(t, NRD - 1)
            emit_AV(t, NRD - 1)

        @blk.vector
        def _(v):
            def emit_evacs(prd):
                it1, r1 = divmod(prd, 2)
                for b in range(2):
                    gb = 2 * prd + b
                    v.wait_ge(tpb.h, gb + 1)
                    if prd >= 2:
                        v.wait_ge(mm_av.h, prd - 1)
                    po_off = (gb % 2) * 1024
                    evp.inc(v.tensor_copy(
                        pT_sb[:, r1 * 2048 + b * 1024:r1 * 2048 + (b + 1) * 1024],
                        ptr2[:, po_off:po_off + 1024]))

            def part_negM(it):
                sl = (it % 2) * 2
                v.wait_ge(dvt.h, 4 * it + 4)
                if it >= 1:
                    v.wait_ge(dn.h, 10 * (it - 1) + 2)
                dn.inc(v.tensor_tensor(negM[:], mcols[:, sl:sl + 1],
                                       mcols[:, sl + 1:sl + 2], ALU.min))
                v.wait_ge(dn.h, 10 * it + 1)
                v.wait_ge(fx.h, it)
                dn.inc(v.tensor_scalar(dcol[:], mcols[:, sl:sl + 2],
                                       negM[:, 0:1], None, ALU.subtract))

            def part_scols(q):
                v.wait_ge(dn.h, 10 * q)
                v.wait_ge(ex.h, 8 * (q + 1))
                dn.inc(v.tensor_reduce(scol[:, 0:1], s8[:, 0:4], AX.X, ALU.add))
                v.wait_ge(dn.h, 10 * q + 3)
                dn.inc(v.tensor_reduce(scol[:, 1:2], s8[:, 4:8], AX.X, ALU.add))

            def part_wSrS(q):
                v.wait_ge(dn.h, 10 * q + 4)
                v.wait_ge(fx.h, q + 1)
                dn.inc(v.tensor_mul(wcol[:], fcol[:], scol[:]))
                v.wait_ge(dn.h, 10 * q + 5)
                dn.inc(v.tensor_reduce(Sc[:, 0:1], wcol[:], AX.X, ALU.add))
                v.wait_ge(dn.h, 10 * q + 6)
                dn.inc(v.reciprocal(Sc[:, 1:2], Sc[:, 0:1]))
                v.wait_ge(dn.h, 10 * q + 7)
                v.wait_ge(mm_av.h, 2 * q + 1)
                dn.inc(v.tensor_scalar_mul(tmpO[:], poA[:, :], fcol[:, 0:1]))

            def part_t1out(q):
                v.wait_ge(dn.h, 10 * q + 8)
                v.wait_ge(mm_av.h, 2 * q + 2)
                dn.inc(v.scalar_tensor_tensor(tmpO2[:], poB[:, :], fcol[:, 1:2],
                                              tmpO[:], ALU.mult, ALU.add))
                v.wait_ge(dn.h, 10 * q + 9)
                h, qt = divmod(q, NT)
                dn.inc(v.tensor_scalar_mul(
                    attn_sb[:, qt * NH * VD + h * VD:qt * NH * VD + (h + 1) * VD],
                    tmpO2[:], Sc[:, 1:2]))

            for rd in range(NRD):
                it, r = divmod(rd, 2)
                if r == 0 and it >= 1:
                    part_scols(it - 1)
                if r == 1 and it >= 1:
                    part_t1out(it - 1)
                for c4 in range(4):
                    v.wait_ge(mmq.h, 4 * rd + c4 + 1)
                    if rd >= 2:
                        v.wait_ge(dvt.h, 2 * (rd - 1))
                    dvr.inc(v.tensor_reduce(m8[:, r * 4 + c4:r * 4 + c4 + 1],
                                            ps4[:, c4 * 512:(c4 + 1) * 512],
                                            AX.X, ALU.max, negate=True))
                sl = (it % 2) * 2
                v.wait_ge(dvr.h, 4 * (rd + 1))
                if it >= 2:
                    v.wait_ge(dn.h, 10 * (it - 2) + 2)
                dvt.inc(v.tensor_reduce(mcols[:, sl + r:sl + r + 1],
                                        m8[:, r * 4:r * 4 + 4],
                                        AX.X, ALU.min))
                v.wait_ge(dvt.h, 2 * rd + 1)
                if rd >= 2:
                    v.wait_ge(ex.h, 4 * (rd - 1))
                dvt.inc(v.tensor_scalar_mul(bias2[:, r:r + 1],
                                            mcols[:, sl + r:sl + r + 1],
                                            SM_SCALE))
                if r == 1:
                    part_negM(it)
                if rd >= 1:
                    emit_evacs(rd - 1)
                if r == 0 and it >= 1:
                    part_wSrS(it - 1)
            emit_evacs(NRD - 1)
            part_scols(NIT - 1)
            part_wSrS(NIT - 1)
            part_t1out(NIT - 1)

        @blk.scalar
        def _(s):
            for rd in range(NRD):
                it, r = divmod(rd, 2)
                s.wait_ge(dvt.h, 2 * rd + 2)
                if rd >= 2:
                    s.wait_ge(tpb.h, 2 * (rd - 1))
                if it >= 1 and r == 0:
                    s.wait_ge(dn.h, 10 * (it - 1) + 4)
                for c4 in range(4):
                    ex.inc(s.activation(
                        p_sb[:, r * 2048 + c4 * 512:r * 2048 + (c4 + 1) * 512],
                        ps4[:, c4 * 512:(c4 + 1) * 512],
                        AF.Exp, bias=bias2[:, r:r + 1], scale=SM_SCALE,
                        accum_out=s8[:, r * 4 + c4:r * 4 + c4 + 1]))
                if r == 1:
                    it_ = it
                    s.wait_ge(dn.h, 10 * it_ + 2)
                    fx.inc(s.activation(fcol[:], dcol[:], AF.Exp,
                                        scale=-SM_SCALE))

        @blk.scalar
        def _(s):
            s.wait_ge(dn.h, 10 * NIT)
            for qt in range(NT):
                st_at.inc(s.dma_start(
                    out=attn[qt * 128:(qt + 1) * 128, :],
                    in_=attn_sb[:, qt * NH * VD:(qt + 1) * NH * VD]), 16)

    _es.close()
    return [st_at]


def build_launch_B():
    p = Prog()
    io = dict(
        Qf=p.inp("Qf", [NH * QK, TOK], BF16),
        KN_b=p.inp("KN_b", [NH * ND, S], BF16),
        KR_b=p.inp("KR_b", [RD, S], BF16),
        Vt_b=p.inp("Vt_b", [S, NH * VD], BF16),
        ident_bf=p.inp("ident_bf", [128, 128], BF16),
        attn=p.outp("attn", [TOK, NH * VD], BF16),
    )
    sts = emit_launch_B(p, io)
    with p.nc.Block() as blk:
        @blk.sync
        def _(s):
            for cs in sts:
                s.wait_ge(cs.h, cs.v)
    return p


def run_launch_B(resA, consts=None):
    consts = consts or host_consts()
    in_maps = []
    for c in range(NCORES):
        b = c // 4
        grp = range(b * 4, b * 4 + 4)
        KN_full = np.concatenate([resA[g]["KN"] for g in grp], axis=1)
        KR_full = np.concatenate([resA[g]["KR"] for g in grp], axis=1)
        Vt_full = np.concatenate([resA[g]["Vt"] for g in grp], axis=0)
        in_maps.append(dict(
            Qf=resA[c]["Qf"], KN_b=KN_full, KR_b=KR_full, Vt_b=Vt_full,
            ident_bf=consts["ident_bf"],
        ))
    pB = build_launch_B()
    res = run_bass_kernel_spmd(pB.nc, in_maps, core_ids=list(range(NCORES)))
    return res.results


def emit_loadT(p, nc, src_dram, width, dstT, ident_bf, cast, entry_waits=(),
               tag="lt"):
    ld = [p.sem(), p.sem()]
    tp = p.sem()
    ev = p.sem()
    nch = width // 128
    nb = (nch + 7) // 8
    with nc.sbuf_tensor(f"{tag}_buf", [128, 2 * width], BF16, allow_name_mangling=True) as buf, \
         nc.psum_tensor(f"{tag}_ptr", [128, 2048], BF16, allow_name_mangling=True) as ptr, \
         nc.Block() as blk:
        @blk.gpsimd
        def _(g):
            for ctr, val in entry_waits:
                g.wait_ge(ctr.h, val)
            for tt in range(NT):
                if tt >= 2:
                    g.wait_ge(tp.h, (tt - 1) * nb)
                ld[tt % 2].inc(g.dma_start(
                    out=buf[:, (tt % 2) * width:(tt % 2 + 1) * width],
                    in_=src_dram[tt * 128:(tt + 1) * 128, :]), 16)

        @blk.tensor
        def _(t):
            for ctr, val in entry_waits:
                t.wait_ge(ctr.h, val)
            for tt in range(NT):
                t.wait_ge(ld[tt % 2].h, 16 * (tt // 2 + 1))
                for b in range(nb):
                    n_in_b = min(8, nch - b * 8)
                    idx = tt * nb + b
                    if idx >= 2:
                        t.wait_ge(ev.h, idx - 1)
                    po = (idx % 2) * 1024
                    last = None
                    for c8 in range(n_in_b):
                        c = b * 8 + c8
                        last = t.transpose(
                            ptr[:, po + c8 * 128:po + (c8 + 1) * 128],
                            buf[:, (tt % 2) * width + c * 128:(tt % 2) * width + (c + 1) * 128],
                            ident_bf[:])
                    tp.inc(last)

        @blk.vector
        def _(v):
            for idx in range(NT * nb):
                tt, b = divmod(idx, nb)
                n_in_b = min(8, nch - b * 8)
                v.wait_ge(tp.h, idx + 1)
                po = (idx % 2) * 1024
                ev.inc(v.tensor_copy(chunks_ap(dstT, TOK, b * 8, tt, n_in_b),
                                     ptr[:, po:po + n_in_b * 128]))
    return ld, tp, ev


def emit_ln(p, nc, y_dram, ysum_dram, out_f32_dram, dstT, ident_bf,
            entry_waits=(), tag="ln"):
    yld = [p.sem(), p.sem()]
    sld = p.sem()
    sq_v = p.sem()
    sq_s = p.sem()
    dvm = p.sem()
    dvr = p.sem()
    evn = p.sem()
    evc = p.sem()
    tp = p.sem()
    evT = p.sem()
    st_f = [p.sem(), p.sem()]
    nch = D // 128
    nb = nch // 8
    with nc.sbuf_tensor(f"{tag}_y", [128, 2 * D], F32, allow_name_mangling=True) as ybuf, \
         nc.sbuf_tensor(f"{tag}_scr", [128, D], F32, allow_name_mangling=True) as scr, \
         nc.sbuf_tensor(f"{tag}_x", [128, D], F32, allow_name_mangling=True) as xbuf, \
         nc.sbuf_tensor(f"{tag}_xb", [128, D], BF16, allow_name_mangling=True) as xbbuf, \
         nc.sbuf_tensor(f"{tag}_ys", [128, 8], F32, allow_name_mangling=True) as ys, \
         nc.sbuf_tensor(f"{tag}_st", [128, 4 * 8], F32, allow_name_mangling=True) as stt, \
         nc.sbuf_tensor(f"{tag}_eps", [128, 1], F32, allow_name_mangling=True) as eps_col, \
         nc.psum_tensor(f"{tag}_ptr", [128, 2048], BF16, allow_name_mangling=True) as ptr, \
         nc.Block() as blk:
        @blk.sync
        def _(s):
            for ctr, val in entry_waits:
                s.wait_ge(ctr.h, val)
            sld.inc(s.dma_start(out=ys[:, :], in_=ysum_dram), 16)
            for tt in range(NT):
                if tt >= 2:
                    s.wait_ge(evn.h, tt - 1)
                yld[tt % 2].inc(s.dma_start(
                    out=ybuf[:, (tt % 2) * D:(tt % 2 + 1) * D],
                    in_=y_dram[tt * 128:(tt + 1) * 128, :]), 16)

        @blk.vector
        def _(v):
            v.wait_ge(sld.h, 16)
            dvm.inc(v.memset(eps_col[:, :], EPS_LN))
            for tt in range(NT):
                dvm.inc(v.tensor_scalar_mul(stt[:, tt * 4:tt * 4 + 1],
                                            ys[:, tt:tt + 1], -1.0 / D))

        @blk.scalar
        def _(s):
            def emit_store(tt):
                s.wait_ge(evn.h, tt + 1)
                if dstT is not None:
                    s.wait_ge(evc.h, tt + 1)
                st_f[tt % 2].inc(s.dma_start(
                    out=out_f32_dram[tt * 128:(tt + 1) * 128, :],
                    in_=xbuf[:, :]), 16)

            for tt in range(NT):
                s.wait_ge(dvm.h, tt + 2)
                s.wait_ge(yld[tt % 2].h, 16 * (tt // 2 + 1))
                if tt >= 1:
                    s.wait_ge(sq_v.h, tt)
                sq_v.inc(s.activation(scr[:, :],
                                      ybuf[:, (tt % 2) * D:(tt % 2 + 1) * D],
                                      AF.Square, bias=stt[:, tt * 4:tt * 4 + 1],
                                      accum_out=stt[:, tt * 4 + 1:tt * 4 + 2]))
                s.wait_ge(sq_v.h, tt + 1)
                sq_s.inc(s.activation(stt[:, tt * 4 + 2:tt * 4 + 3],
                                      stt[:, tt * 4 + 1:tt * 4 + 2],
                                      AF.Sqrt, bias=eps_col[:, :], scale=1.0 / D))
                if out_f32_dram is not None and tt >= 1:
                    emit_store(tt - 1)
            if out_f32_dram is not None:
                emit_store(NT - 1)

        @blk.vector
        def _(v):
            def emit_evacT(tt):
                for b in range(nb):
                    idx = tt * nb + b
                    v.wait_ge(tp.h, idx + 1)
                    po = (idx % 2) * 1024
                    evT.inc(v.tensor_copy(chunks_ap(dstT, TOK, b * 8, tt, 8),
                                          ptr[:, po:po + 1024]))

            for tt in range(NT):
                v.wait_ge(sq_s.h, tt + 1)
                dvr.inc(v.reciprocal(stt[:, tt * 4 + 3:tt * 4 + 4],
                                     stt[:, tt * 4 + 2:tt * 4 + 3]))
                v.wait_ge(dvr.h, tt + 1)
                if tt >= 1:
                    if out_f32_dram is not None:
                        v.wait_ge(st_f[(tt - 1) % 2].h,
                                  16 * ((tt - 1) // 2 + 1))
                    if dstT is not None:
                        v.wait_ge(evc.h, tt)
                evn.inc(v.tensor_scalar(
                    xbuf[:, :],
                    ybuf[:, (tt % 2) * D:(tt % 2 + 1) * D],
                    stt[:, tt * 4:tt * 4 + 1], stt[:, tt * 4 + 3:tt * 4 + 4],
                    ALU.add, ALU.mult))
                if dstT is not None:
                    if tt >= 1:
                        emit_evacT(tt - 1)
                    v.wait_ge(evn.h, tt + 1)
                    if tt >= 1:
                        v.wait_ge(tp.h, tt * nb)
                    evc.inc(v.tensor_copy(xbbuf[:, :], xbuf[:, :]))
            if dstT is not None:
                emit_evacT(NT - 1)

        if dstT is not None:
            @blk.tensor
            def _(t):
                for ctr, val in entry_waits:
                    t.wait_ge(ctr.h, val)
                for tt in range(NT):
                    t.wait_ge(evc.h, tt + 1)
                    for b in range(nb):
                        idx = tt * nb + b
                        if idx >= 2:
                            t.wait_ge(evT.h, idx - 1)
                        po = (idx % 2) * 1024
                        last = None
                        for c8 in range(8):
                            c = b * 8 + c8
                            last = t.transpose(
                                ptr[:, po + c8 * 128:po + (c8 + 1) * 128],
                                xbbuf[:, c * 128:(c + 1) * 128],
                                ident_bf[:])
                        tp.inc(last)

    return evn, evT, st_f, tp


def emit_mm_res(p, nc, lhsT_cache, nkc, w_dram, dout, xres_dram, y_dram,
                ysum_dram, ones_bf, bias_sb=None, entry_waits=(),
                extra_mm=None, extra_waits=(), mask_fn=None, psum_ref=None,
                tag="mm"):
    w_ld = [p.sem() for _ in range(4)]
    x_ld = [p.sem() for _ in range(4)]
    w_fr = p.sem()
    msk = p.sem()
    mme = p.sem()
    eve = p.sem()
    dvs = p.sem()
    sty = [p.sem() for _ in range(4)]
    sts = p.sem()
    ndg = dout // 512
    has_tail = (bias_sb is not None) or (extra_mm is not None)
    with nc.sbuf_tensor(f"{tag}_w", [128, 4 * 512], BF16, allow_name_mangling=True) as wbuf, \
         nc.sbuf_tensor(f"{tag}_wr", [128, 4 * 512], F32, allow_name_mangling=True) as wraw, \
         nc.sbuf_tensor(f"{tag}_xr", [128, 4 * 512], F32, allow_name_mangling=True) as xrbuf, \
         nc.sbuf_tensor(f"{tag}_yb", [128, 4 * 512], F32, allow_name_mangling=True) as ybuf, \
         nc.sbuf_tensor(f"{tag}_sp", [128, NT * 8], F32, allow_name_mangling=True) as spart, \
         nc.sbuf_tensor(f"{tag}_ys", [128, 8], F32, allow_name_mangling=True) as ysb, \
         nc.psum_tensor(f"{tag}_pq", [128, 4096], F32, allow_name_mangling=True) as pq, \
         nc.Block() as blk:
        if psum_ref is not None:
            psum_ref[0] = pq

        @blk.sync
        def _(s):
            for ctr, val in entry_waits:
                s.wait_ge(ctr.h, val)
            for dg in range(ndg):
                for kc in range(nkc):
                    idx = dg * nkc + kc
                    if idx >= 4:
                        s.wait_ge(msk.h, idx - 3)
                    w_ld[idx % 4].inc(s.dma_start(
                        out=wraw[:, (idx % 4) * 512:(idx % 4 + 1) * 512],
                        in_=w_dram[kc * 128:(kc + 1) * 128,
                                   dg * 512:(dg + 1) * 512]), 16)


        @blk.tensor
        def _(t):
            for ctr, val in entry_waits:
                t.wait_ge(ctr.h, val)
            for ctr, val in extra_waits:
                t.wait_ge(ctr.h, val)
            for dg in range(ndg):
                if dg > 0:
                    t.wait_ge(eve.h, dg * NT)
                for kc in range(nkc):
                    idx = dg * nkc + kc
                    t.wait_ge(msk.h, idx + 1)
                    last = None
                    for tt in range(NT):
                        last = t.matmul(
                            pq[:, tt * 512:(tt + 1) * 512],
                            chunk_ap(lhsT_cache, TOK, kc, tt * 128, 128),
                            wbuf[:, (idx % 4) * 512:(idx % 4 + 1) * 512],
                            start=(kc == 0),
                            stop=(kc == nkc - 1 and not has_tail))
                    w_fr.inc(last)
                if has_tail:
                    last = None
                    for tt in range(NT):
                        if bias_sb is not None:
                            last = t.matmul(
                                pq[:, tt * 512:(tt + 1) * 512],
                                ones_bf[0:1, 0:128],
                                bias_sb[0:1, dg * 512:(dg + 1) * 512],
                                start=False, stop=(extra_mm is None))
                        if extra_mm is not None:
                            last = extra_mm(t, dg, tt)
                    mme.inc(last)

        @blk.vector
        def _(v):
            def emit_mask(dg, kc):
                idx = dg * nkc + kc
                v.wait_ge(w_ld[idx % 4].h, 16 * (idx // 4 + 1))
                if idx >= 4:
                    v.wait_ge(w_fr.h, idx - 3)
                if mask_fn is not None:
                    msk.inc(v.tensor_tensor(
                        wbuf[:, (idx % 4) * 512:(idx % 4 + 1) * 512],
                        wraw[:, (idx % 4) * 512:(idx % 4 + 1) * 512],
                        mask_fn(dg, kc), ALU.mult))
                else:
                    msk.inc(v.tensor_copy(
                        wbuf[:, (idx % 4) * 512:(idx % 4 + 1) * 512],
                        wraw[:, (idx % 4) * 512:(idx % 4 + 1) * 512]))

            def emit_evac(dg):
                for tt in range(NT):
                    idx = dg * NT + tt
                    v.wait_ge(w_fr.h, (dg + 1) * nkc)
                    if has_tail:
                        v.wait_ge(mme.h, dg + 1)
                    v.wait_ge(x_ld[idx % 4].h, 16 * (idx // 4 + 1))
                    if idx >= 4:
                        v.wait_ge(sty[idx % 4].h, 16 * (idx // 4))
                    eve.inc(v.scalar_tensor_tensor(
                        ybuf[:, (idx % 4) * 512:(idx % 4 + 1) * 512],
                        pq[:, tt * 512:(tt + 1) * 512], 1.0,
                        xrbuf[:, (idx % 4) * 512:(idx % 4 + 1) * 512],
                        ALU.mult, ALU.add,
                        accum_out=spart[:, tt * 8 + dg:tt * 8 + dg + 1]))

            for dg in range(ndg):
                emit_mask(dg, 0)
                emit_mask(dg, 1)
                if dg >= 1:
                    emit_evac(dg - 1)
                for kc in range(2, nkc):
                    emit_mask(dg, kc)
            emit_evac(ndg - 1)
            for tt in range(NT):
                v.wait_ge(eve.h, (ndg - 1) * NT + tt + 1)
                dvs.inc(v.tensor_reduce(ysb[:, tt:tt + 1],
                                        spart[:, tt * 8:tt * 8 + ndg],
                                        AX.X, ALU.add))

        @blk.scalar
        def _(s):
            def xload(idx):
                dg, tt = divmod(idx, NT)
                x_ld[idx % 4].inc(s.dma_start(
                    out=xrbuf[:, (idx % 4) * 512:(idx % 4 + 1) * 512],
                    in_=xres_dram[tt * 128:(tt + 1) * 128,
                                  dg * 512:(dg + 1) * 512]), 16)
            for ctr, val in entry_waits:
                s.wait_ge(ctr.h, val)
            for j in range(4):
                xload(j)
            for dg in range(ndg):
                for tt in range(NT):
                    idx = dg * NT + tt
                    s.wait_ge(eve.h, idx + 1)
                    sty[idx % 4].inc(s.dma_start(
                        out=y_dram[tt * 128:(tt + 1) * 128,
                                   dg * 512:(dg + 1) * 512],
                        in_=ybuf[:, (idx % 4) * 512:(idx % 4 + 1) * 512]), 16)
                    if idx + 4 < ndg * NT:
                        xload(idx + 4)
            s.wait_ge(dvs.h, NT)
            sts.inc(s.dma_start(out=ysum_dram, in_=ysb[:, :]), 16)
    return w_fr, mme, eve, sty, sts


def run_launch_A(inputs, consts=None):
    consts = consts or host_consts()
    x_full = inputs["L0_attn_hidden_states"]
    mask = inputs["L1_attn_mask"]
    in_maps = []
    for c in range(NCORES):
        b, off = divmod(c, 4)
        pos0 = off * TOK
        in_maps.append(dict(
            xT=np.ascontiguousarray(x_full[b, pos0:pos0 + TOK, :].T),
            w_qa=inputs["L0_attn_w_qa"],
            g_qa=inputs["L0_attn_g_qa"].reshape(1, -1),
            w_qb=inputs["L0_attn_w_qb"],
            w_kva=inputs["L0_attn_w_kva"],
            g_kva=inputs["L0_attn_g_kva"].reshape(1, -1),
            w_kvb=inputs["L0_attn_w_kvb"],
            cos_own=np.ascontiguousarray(consts["cos_fm"][:, pos0:pos0 + TOK]),
            sin_own=np.ascontiguousarray(consts["sin_fm"][:, pos0:pos0 + TOK]),
            mask_rows=np.ascontiguousarray(mask[c * 512:(c + 1) * 512, :]),
            Gmat=consts["Gmat"],
            ident_bf=consts["ident_bf"],
            ident_f32=consts["ident_f32"],
            ones_f32=consts["ones_f32"],
        ))
    pA = build_launch_A()
    res = run_bass_kernel_spmd(pA.nc, in_maps, core_ids=list(range(NCORES)))
    return res.results



def emit_launch_C1(p, io):
    nc = p.nc
    _es = ExitStack()
    def _sbl(name, shape, dt=F32):
        return _es.enter_context(
            nc.sbuf_tensor(name, shape, dt, allow_name_mangling=True))
    attn_d = io["attn"]
    x_d = io["x"]
    w_o = io["w_o"]
    ident_bf_d = io["ident_bf"]
    y1 = io["y1"]
    ysum1 = io["ysum1"]

    ident_bf = _sbl("c1_ident", [128, 128], BF16)
    attnT = _sbl("c1_attnT", [128, 32 * TOK], BF16)
    c_ld = p.sem()
    with nc.Block() as blk:
        @blk.sync
        def _(s):
            c_ld.inc(s.dma_start(out=ident_bf[:], in_=ident_bf_d), 16)

    ldA, tpA, evA = emit_loadT(p, nc, attn_d, NH * VD, attnT, ident_bf, False,
                               entry_waits=[(c_ld, 16)], tag="c1lt")
    w_fr, mme, eve, sty, sts = emit_mm_res(
        p, nc, attnT, 32, w_o, D, x_d, y1, ysum1, None,
        entry_waits=[(evA, NT * 4)], tag="c1mm")
    _es.close()
    return sty + [sts]


def build_launch_C1():
    p = Prog()
    io = dict(
        attn=p.inp("attn", [TOK, NH * VD], BF16),
        x=p.inp("x", [TOK, D], F32),
        w_o=p.inp("w_o", [D, D], F32),
        ident_bf=p.inp("ident_bf", [128, 128], BF16),
        y1=p.outp("y1", [TOK, D], F32),
        ysum1=p.outp("ysum1", [128, NT], F32),
    )
    sts = emit_launch_C1(p, io)
    with p.nc.Block() as blk:
        @blk.sync
        def _(s):
            for cs in sts:
                s.wait_ge(cs.h, cs.v)
    return p


def run_launch_C1(inputs, resB, consts=None):
    consts = consts or host_consts()
    x_full = inputs["L0_attn_hidden_states"]
    in_maps = []
    for c in range(NCORES):
        b, off = divmod(c, 4)
        pos0 = off * TOK
        in_maps.append(dict(
            attn=resB[c]["attn"],
            x=np.ascontiguousarray(x_full[b, pos0:pos0 + TOK, :]),
            w_o=inputs["L0_attn_w_o"],
            ident_bf=consts["ident_bf"],
        ))
    pC = build_launch_C1()
    res = run_bass_kernel_spmd(pC.nc, in_maps, core_ids=list(range(NCORES)))
    return res.results


def emit_launch_C5(p, io):
    nc = p.nc
    evn, evT, st_f, tp = emit_ln(p, nc, io["y4"], io["ysum4"], io["out"],
                                 None, None, tag="c5ln")
    return st_f


def build_launch_C5():
    p = Prog()
    io = dict(
        y4=p.inp("y4", [TOK, D], F32),
        ysum4=p.inp("ysum4", [128, NT], F32),
        out=p.outp("out", [TOK, D], F32),
    )
    sts = emit_launch_C5(p, io)
    with p.nc.Block() as blk:
        @blk.sync
        def _(s):
            for cs in sts:
                if cs.v:
                    s.wait_ge(cs.h, cs.v)
    return p


def run_launch_C5(resC4):
    in_maps = [dict(y4=resC4[c]["y4"], ysum4=resC4[c]["ysum4"])
               for c in range(NCORES)]
    pC = build_launch_C5()
    res = run_bass_kernel_spmd(pC.nc, in_maps, core_ids=list(range(NCORES)))
    return res.results


def emit_launch_C2(p, io):
    nc = p.nc
    _es = ExitStack()
    def _sbl(name, shape, dt=F32):
        return _es.enter_context(
            nc.sbuf_tensor(name, shape, dt, allow_name_mangling=True))
    y1 = io["y1"]
    ysum1 = io["ysum1"]
    U1 = io["U1"]
    V1 = io["V1"]
    U2 = io["U2"]
    V2 = io["V2"]
    b1 = io["b1"]
    b2 = io["b2"]
    ident_bf_d = io["ident_bf"]
    ones_bf_d = io["ones_bf"]
    y2 = io["y2"]
    ysum2 = io["ysum2"]
    x1i = nc.dram_tensor("x1i", [TOK, D], F32, kind="Internal",
                         unsafe_allow_realloc=True).ap()

    ident_bf = _sbl("c2_ident", [128, 128], BF16)
    ones_bf = _sbl("c2_ones", [1, 512], BF16)
    b2_sb = _sbl("c2_b2", [1, D], BF16)
    PT = _sbl("c2_PT", [128, 8 * TOK], BF16)
    ZT = _sbl("c2_ZT", [128, 8 * TOK], BF16)
    c_ld = p.sem()
    c_ld2 = p.sem()
    with nc.Block() as blk:
        @blk.sync
        def _(s):
            c_ld.inc(s.dma_start(out=ident_bf[:], in_=ident_bf_d), 16)
            c_ld.inc(s.dma_start(out=ones_bf[:], in_=ones_bf_d), 16)

        @blk.gpsimd
        def _(g):
            c_ld2.inc(g.dma_start(out=b2_sb[:], in_=b2), 16)

    u_ld = [p.sem() for _ in range(4)]
    u_fr = p.sem()
    evP = p.sem()
    v_ld = [p.sem() for _ in range(4)]
    v_fr = p.sem()
    mmgu = p.sem()
    sgq = p.sem()
    eva = p.sem()
    u2_ld = [p.sem(), p.sem()]
    u2_fr = p.sem()
    evZ = p.sem()

    with nc.sbuf_tensor("c2_x1T", [128, 32 * TOK], BF16, allow_name_mangling=True) as x1T:
        evn1, evT1, stf1, _tp = emit_ln(p, nc, y1, ysum1, x1i, x1T, ident_bf,
                                        entry_waits=[(c_ld, 16 * 2), (c_ld2, 16)], tag="c2ln")

        u_cst = p.sem()
        with nc.sbuf_tensor("c2_u1w", [128, 4 * 512], BF16, allow_name_mangling=True) as u1w, \
             nc.sbuf_tensor("c2_u1r", [128, 4 * 512], F32, allow_name_mangling=True) as u1r, \
             nc.psum_tensor("c2_pp", [128, 4096], F32, allow_name_mangling=True) as pp, \
             nc.Block() as blk:
            @blk.sync
            def _(s):
                for rg in range(2):
                    for dc in range(32):
                        idx = rg * 32 + dc
                        if idx >= 4:
                            s.wait_ge(u_cst.h, idx - 3)
                        u_ld[idx % 4].inc(s.dma_start(
                            out=u1r[:, (idx % 4) * 512:(idx % 4 + 1) * 512],
                            in_=U1[dc * 128:(dc + 1) * 128,
                                   rg * 512:(rg + 1) * 512]), 16)

            @blk.tensor
            def _(t):
                t.wait_ge(evT1.h, evT1.v)
                for rg in range(2):
                    if rg > 0:
                        t.wait_ge(evP.h, 8)
                    for dc in range(32):
                        idx = rg * 32 + dc
                        t.wait_ge(u_cst.h, idx + 1)
                        last = None
                        for sub in range(8):
                            rc4, th = divmod(sub, 2)
                            last = t.matmul(
                                pp[:, sub * 512:(sub + 1) * 512],
                                u1w[:, (idx % 4) * 512 + rc4 * 128:
                                    (idx % 4) * 512 + (rc4 + 1) * 128],
                                chunk_ap(x1T, TOK, dc, th * 512, 512),
                                start=(dc == 0), stop=(dc == 31))
                        u_fr.inc(last)

            @blk.vector
            def _(v):
                def ucast(idx):
                    v.wait_ge(u_ld[idx % 4].h, 16 * (idx // 4 + 1))
                    if idx >= 4:
                        v.wait_ge(u_fr.h, idx - 3)
                    u_cst.inc(v.tensor_copy(
                        u1w[:, (idx % 4) * 512:(idx % 4 + 1) * 512],
                        u1r[:, (idx % 4) * 512:(idx % 4 + 1) * 512]))
                for idx in range(36):
                    ucast(idx)
                v.wait_ge(u_fr.h, 32)
                for sub in range(8):
                    rc4, th = divmod(sub, 2)
                    evP.inc(v.tensor_copy(
                        chunk_ap(PT, TOK, 0 * 4 + rc4, th * 512, 512),
                        pp[:, sub * 512:(sub + 1) * 512]))
                for idx in range(36, 64):
                    ucast(idx)
                v.wait_ge(u_fr.h, 64)
                for sub in range(8):
                    rc4, th = divmod(sub, 2)
                    evP.inc(v.tensor_copy(
                        chunk_ap(PT, TOK, 1 * 4 + rc4, th * 512, 512),
                        pp[:, sub * 512:(sub + 1) * 512]))

    JF = 22
    fgs = [(j * 512, min(512, INTER - j * 512)) for j in range(JF)]
    nf4s = [fw // 128 for (fo, fw) in fgs]
    FC_HALF = sum(nf4s)

    b1_ld = p.sem()
    vcst = p.sem()
    u2cst = p.sem()
    with nc.sbuf_tensor("c2_actT", [128, 86 * 512], BF16, allow_name_mangling=True) as actT, \
         nc.sbuf_tensor("c2_b1", [1, 2 * INTER], BF16, allow_name_mangling=True) as b1_sb, \
         nc.sbuf_tensor("c2_v1w", [128, 2 * 1024], BF16, allow_name_mangling=True) as v1w, \
         nc.sbuf_tensor("c2_v1r", [128, 4 * 1024], F32, allow_name_mangling=True) as v1r, \
         nc.sbuf_tensor("c2_sg", [128, 2 * 512], F32, allow_name_mangling=True) as sgbuf, \
         nc.sbuf_tensor("c2_u2w", [128, 2 * 1024], BF16, allow_name_mangling=True) as u2w, \
         nc.sbuf_tensor("c2_u2r", [128, 2 * 1024], F32, allow_name_mangling=True) as u2r, \
         nc.psum_tensor("c2_pgu", [128, 4096], F32, allow_name_mangling=True) as pgu, \
         nc.Block() as blk:
        @blk.gpsimd
        def _(g):
            g.wait_ge(evP.h, 16)
            for cs in stf1:
                g.wait_ge(cs.h, cs.v)
            b1_ld.inc(g.dma_start(out=b1_sb[:], in_=b1), 16)

        @blk.sync
        def _(s):
            s.wait_ge(evP.h, 16)
            for cs in stf1:
                s.wait_ge(cs.h, cs.v)
            for half in range(2):
                for j, (fo, fw) in enumerate(fgs):
                    for rc in range(8):
                        idx = (half * JF + j) * 8 + rc
                        if idx >= 4:
                            s.wait_ge(vcst.h, idx - 3)
                        g_src = V1[rc * 128:(rc + 1) * 128, :].rearrange(
                            "p (gu f) -> p gu f", gu=2)[:, :, fo:fo + fw]
                        v_ld[idx % 4].inc(s.dma_start(
                            out=v1r[:, (idx % 4) * 1024:(idx % 4) * 1024 + 2 * fw]
                                .rearrange("p (gu f) -> p gu f", gu=2),
                            in_=g_src), 16)
                for fc in range(86):
                    idx = half * 86 + fc
                    if idx >= 2:
                        s.wait_ge(u2cst.h, idx - 1)
                    u2_ld[idx % 2].inc(s.dma_start(
                        out=u2r[:, (idx % 2) * 1024:(idx % 2 + 1) * 1024],
                        in_=U2[fc * 128:(fc + 1) * 128, :]), 16)

        @blk.tensor
        def _(t):
            t.wait_ge(evP.h, 16)
            t.wait_ge(b1_ld.h, 16)
            for half in range(2):
                if half == 1:
                    t.wait_ge(evZ.h, 8)
                for j, (fo, fw) in enumerate(fgs):
                    nf4 = fw // 128
                    git = half * JF + j
                    done_fc4 = half * FC_HALF + sum(nf4s[:j])
                    if git > 0:
                        t.wait_ge(eva.h, done_fc4)
                        t.wait_ge(sgq.h, done_fc4)
                    for rc in range(8):
                        idx = git * 8 + rc
                        t.wait_ge(vcst.h, idx + 1)
                        last = None
                        for fc4 in range(nf4):
                            for gu in range(2):
                                last = t.matmul(
                                    pgu[:, (fc4 * 2 + gu) * 512:(fc4 * 2 + gu + 1) * 512],
                                    v1w[:, (idx % 2) * 1024 + gu * fw + fc4 * 128:
                                        (idx % 2) * 1024 + gu * fw + (fc4 + 1) * 128],
                                    chunk_ap(PT, TOK, rc, half * 512, 512),
                                    start=(rc == 0), stop=False)
                        v_fr.inc(last)
                    last = None
                    for fc4 in range(nf4):
                        t.matmul(
                            pgu[:, (fc4 * 2) * 512:(fc4 * 2) * 512 + 512],
                            b1_sb[0:1, fo + fc4 * 128:fo + (fc4 + 1) * 128],
                            ones_bf[0:1, 0:512], start=False, stop=True)
                        last = t.matmul(
                            pgu[:, (fc4 * 2 + 1) * 512:(fc4 * 2 + 1) * 512 + 512],
                            b1_sb[0:1, INTER + fo + fc4 * 128:INTER + fo + (fc4 + 1) * 128],
                            ones_bf[0:1, 0:512], start=False, stop=True)
                    mmgu.inc(last)
                t.wait_ge(eva.h, (half + 1) * FC_HALF)
                t.wait_ge(sgq.h, (half + 1) * FC_HALF)
                if half == 1:
                    pass
                for fc in range(86):
                    idx = half * 86 + fc
                    t.wait_ge(u2cst.h, idx + 1)
                    last = None
                    for r2c in range(8):
                        last = t.matmul(
                            pgu[:, r2c * 512:(r2c + 1) * 512],
                            u2w[:, (idx % 2) * 1024 + r2c * 128:
                                (idx % 2) * 1024 + (r2c + 1) * 128],
                            actT[:, fc * 512:(fc + 1) * 512],
                            start=(fc == 0), stop=(fc == 85))
                    u2_fr.inc(last)

        @blk.scalar
        def _(s):
            cnt = 0
            for half in range(2):
                for j, (fo, fw) in enumerate(fgs):
                    git = half * JF + j
                    nf4 = fw // 128
                    s.wait_ge(mmgu.h, git + 1)
                    for fc4 in range(nf4):
                        if cnt >= 2:
                            s.wait_ge(eva.h, cnt - 1)
                        sgq.inc(s.activation(
                            sgbuf[:, (cnt % 2) * 512:(cnt % 2 + 1) * 512],
                            pgu[:, (fc4 * 2) * 512:(fc4 * 2) * 512 + 512],
                            AF.Silu))
                        cnt += 1

        @blk.vector
        def _(v):
            cnt = [0]

            def vcast(idx):
                half_, rem = divmod(idx, JF * 8)
                j_, rc_ = divmod(rem, 8)
                fw_ = fgs[j_][1]
                v.wait_ge(v_ld[idx % 4].h, 16 * (idx // 4 + 1))
                if idx >= 2:
                    v.wait_ge(v_fr.h, idx - 1)
                vcst.inc(v.tensor_copy(
                    v1w[:, (idx % 2) * 1024:(idx % 2) * 1024 + 2 * fw_],
                    v1r[:, (idx % 4) * 1024:(idx % 4) * 1024 + 2 * fw_]))

            def u2cast(idx):
                v.wait_ge(u2_ld[idx % 2].h, 16 * (idx // 2 + 1))
                if idx >= 2:
                    v.wait_ge(u2_fr.h, idx - 1)
                u2cst.inc(v.tensor_copy(
                    u2w[:, (idx % 2) * 1024:(idx % 2 + 1) * 1024],
                    u2r[:, (idx % 2) * 1024:(idx % 2 + 1) * 1024]))

            def emit_evas(git):
                half_, j_ = divmod(git, JF)
                nf4 = nf4s[j_]
                nfc0 = sum(nf4s[:j_])
                for fc4 in range(nf4):
                    v.wait_ge(sgq.h, cnt[0] + 1)
                    eva.inc(v.tensor_tensor(
                        actT[:, (nfc0 + fc4) * 512:(nfc0 + fc4 + 1) * 512],
                        sgbuf[:, (cnt[0] % 2) * 512:(cnt[0] % 2 + 1) * 512],
                        pgu[:, (fc4 * 2 + 1) * 512:(fc4 * 2 + 1) * 512 + 512],
                        ALU.mult))
                    cnt[0] += 1

            for half in range(2):
                if half == 1:
                    v.wait_ge(u2_fr.h, 86)
                for j in range(JF):
                    git = half * JF + j
                    for rc in range(8):
                        vcast(git * 8 + rc)
                        if rc == 0 and j > 0:
                            emit_evas(git - 1)
                emit_evas(half * JF + JF - 1)
                for fc in range(86):
                    u2cast(half * 86 + fc)
                v.wait_ge(u2_fr.h, (half + 1) * 86)
                for r2c in range(8):
                    evZ.inc(v.tensor_copy(
                        chunk_ap(ZT, TOK, r2c, half * 512, 512),
                        pgu[:, r2c * 512:(r2c + 1) * 512]))

    w_fr2, mme2, eve2, sty2, sts2 = emit_mm_res(
        p, nc, ZT, 8, V2, D, x1i, y2, ysum2, ones_bf, bias_sb=b2_sb,
        entry_waits=[(evZ, 16), (stf1[0], stf1[0].v), (stf1[1], stf1[1].v)],
        tag="c2mm")
    _es.close()
    return sty2 + [sts2]


def build_launch_C2():
    p = Prog()
    io = dict(
        y1=p.inp("y1", [TOK, D], F32),
        ysum1=p.inp("ysum1", [128, NT], F32),
        U1=p.inp("U1", [D, RANK], F32),
        V1=p.inp("V1", [RANK, 2 * INTER], F32),
        U2=p.inp("U2", [INTER, RANK], F32),
        V2=p.inp("V2", [RANK, D], F32),
        b1=p.inp("b1", [1, 2 * INTER], F32),
        b2=p.inp("b2", [1, D], F32),
        ident_bf=p.inp("ident_bf", [128, 128], BF16),
        ones_bf=p.inp("ones_bf", [1, 512], BF16),
        y2=p.outp("y2", [TOK, D], F32),
        ysum2=p.outp("ysum2", [128, NT], F32),
    )
    sts = emit_launch_C2(p, io)
    with p.nc.Block() as blk:
        @blk.sync
        def _(s):
            for cs in sts:
                s.wait_ge(cs.h, cs.v)
    return p


def run_launch_C2(inputs, resC1, consts=None):
    consts = consts or host_consts()
    ones512 = np.ones((1, 512), dtype=BF)
    in_maps = []
    for c in range(NCORES):
        in_maps.append(dict(
            y1=resC1[c]["y1"], ysum1=resC1[c]["ysum1"],
            U1=inputs["L0_ffn_U1"], V1=inputs["L0_ffn_V1"],
            U2=inputs["L0_ffn_U2"], V2=inputs["L0_ffn_V2"],
            b1=inputs["L0_ffn_b1"].reshape(1, -1),
            b2=inputs["L0_ffn_b2"].reshape(1, -1),
            ident_bf=consts["ident_bf"], ones_bf=ones512,
        ))
    pC = build_launch_C2()
    res = run_bass_kernel_spmd(pC.nc, in_maps, core_ids=list(range(NCORES)))
    return res.results


def emit_launch_C3(p, io):
    nc = p.nc
    _es = ExitStack()
    def _sbl(name, shape, dt=F32):
        return _es.enter_context(
            nc.sbuf_tensor(name, shape, dt, allow_name_mangling=True))
    y2 = io["y2"]
    ysum2 = io["ysum2"]
    wT = io["wT"]
    bias = io["bias"]
    bmT = io["bmT"]
    Emat_d = io["Emat"]
    ident_bf_d = io["ident_bf"]
    ones_bf_d = io["ones_bf"]
    y3 = io["y3"]
    ysum3 = io["ysum3"]
    x2i = nc.dram_tensor("x2i", [TOK, D], F32, kind="Internal",
                         unsafe_allow_realloc=True).ap()

    ident_bf = _sbl("c3_ident", [128, 128], BF16)
    ones_bf = _sbl("c3_ones", [1, 512], BF16)
    bias_sb = _sbl("c3_bias", [1, D], BF16)
    Emat_s = _sbl("c3_E", [4, 128], F32)
    bmT4 = _sbl("c3_bmT4", [4, 32 * 128], F32)
    flg4 = _sbl("c3_flg4", [4, 32 * 128], F32)
    bmET = _sbl("c3_bmET", [128, 32 * 128], F32)
    x2T = _sbl("c3_x2T", [128, 32 * TOK], BF16)
    c_ld = p.sem()
    c_ld2 = p.sem()
    dvf = p.sem()
    mmE = p.sem()
    evEx = p.sem()
    with nc.psum_tensor("c3_pE", [128, 128], F32, allow_name_mangling=True) as pE, \
         nc.Block() as blk:
        @blk.sync
        def _(s):
            c_ld.inc(s.dma_start(out=ident_bf[:], in_=ident_bf_d), 16)
            c_ld.inc(s.dma_start(out=ones_bf[:], in_=ones_bf_d), 16)
            c_ld.inc(s.dma_start(out=Emat_s[:], in_=Emat_d), 16)
            c_ld.inc(s.dma_start(
                out=bmT4[:, :].rearrange("p (g nb) -> p g nb", g=32),
                in_=bmT.rearrange("(g p) nb -> p g nb", p=4)), 16)

        @blk.gpsimd
        def _(g):
            c_ld2.inc(g.dma_start(out=bias_sb[:], in_=bias), 16)

        @blk.vector
        def _(v):
            v.wait_ge(c_ld.h, 16 * 4)
            v.wait_ge(c_ld2.h, 16)
            dvf.inc(v.tensor_single_scalar(flg4[:, :], bmT4[:, :], 0.0, ALU.is_gt))

        @blk.tensor
        def _(t):
            t.wait_ge(dvf.h, 1)
            for kc in range(32):
                if kc >= 1:
                    t.wait_ge(evEx.h, kc)
                mmE.inc(t.matmul(pE[:], Emat_s[:, :],
                                 flg4[:, kc * 128:(kc + 1) * 128],
                                 start=True, stop=True))

        @blk.vector
        def _(v):
            for kc in range(32):
                v.wait_ge(mmE.h, kc + 1)
                evEx.inc(v.tensor_copy(bmET[:, kc * 128:(kc + 1) * 128], pE[:]))

    evn2, evT2, stf2, _tp = emit_ln(p, nc, y2, ysum2, x2i, x2T, ident_bf,
                                    entry_waits=[(evEx, 32)], tag="c3ln")

    def mask_view(dg, kc):
        return bmET[:, kc * 128 + dg * 16:kc * 128 + (dg + 1) * 16] \
            .unsqueeze(2).broadcast_to([128, 16, 32])

    w_fr3, mme3, eve3, sty3, sts3 = emit_mm_res(
        p, nc, x2T, 32, wT, D, x2i, y3, ysum3, ones_bf, bias_sb=bias_sb,
        entry_waits=[(evT2, evT2.v), (stf2[0], stf2[0].v), (stf2[1], stf2[1].v)],
        mask_fn=mask_view, tag="c3mm")
    _es.close()
    return sty3 + [sts3]


def build_launch_C3():
    p = Prog()
    io = dict(
        y2=p.inp("y2", [TOK, D], F32),
        ysum2=p.inp("ysum2", [128, NT], F32),
        wT=p.inp("wT", [D, D], F32),
        bias=p.inp("bias", [1, D], F32),
        bmT=p.inp("bmT", [128, 128], F32),
        Emat=p.inp("Emat", [4, 128], F32),
        ident_bf=p.inp("ident_bf", [128, 128], BF16),
        ones_bf=p.inp("ones_bf", [1, 512], BF16),
        y3=p.outp("y3", [TOK, D], F32),
        ysum3=p.outp("ysum3", [128, NT], F32),
    )
    sts = emit_launch_C3(p, io)
    with p.nc.Block() as blk:
        @blk.sync
        def _(s):
            for cs in sts:
                s.wait_ge(cs.h, cs.v)
    return p


def run_launch_C3(inputs, resC2, bm_full, consts=None):
    consts = consts or host_consts()
    ones512 = np.ones((1, 512), dtype=BF)
    Emat = np.zeros((4, 128), dtype=np.float32)
    for pp_ in range(128):
        Emat[pp_ // 32, pp_] = 1.0
    wTh = np.ascontiguousarray(inputs["L1_attn_weight"].T)
    bmT = np.ascontiguousarray(bm_full.T)
    in_maps = []
    for c in range(NCORES):
        in_maps.append(dict(
            y2=resC2[c]["y2"], ysum2=resC2[c]["ysum2"],
            wT=wTh, bias=inputs["L1_attn_bias"].reshape(1, -1),
            bmT=bmT, Emat=Emat,
            ident_bf=consts["ident_bf"], ones_bf=ones512,
        ))
    pC = build_launch_C3()
    res = run_bass_kernel_spmd(pC.nc, in_maps, core_ids=list(range(NCORES)))
    return res.results


SORT8_STAGES = [
    [(0, 1), (2, 3), (4, 5), (6, 7)],
    [(0, 2), (1, 3), (4, 6), (5, 7)],
    [(1, 2), (5, 6), (0, 4), (3, 7)],
    [(1, 5), (2, 6)],
    [(1, 4), (3, 6)],
    [(2, 4), (3, 5)],
    [(3, 4)],
]


def emit_launch_C4(p, io):
    nc = p.nc
    _es = ExitStack()
    def _sbl(name, shape, dt=F32):
        return _es.enter_context(
            nc.sbuf_tensor(name, shape, dt, allow_name_mangling=True))
    y3 = io["y3"]
    ysum3 = io["ysum3"]
    WbT = io["WbT"]
    b_base = io["b_base"]
    WrT = io["WrT"]
    b_router = io["b_router"]
    laA2 = io["laA2"]
    laB2 = io["laB2"]
    kvec_d = io["kvec"]
    R16_d = io["R16"]
    ident_f32_d = io["ident_f32"]
    ident_bf_d = io["ident_bf"]
    ones_bf_d = io["ones_bf"]
    y4 = io["y4"]
    ysum4 = io["ysum4"]
    x3i = nc.dram_tensor("x3i", [TOK, D], F32, kind="Internal",
                         unsafe_allow_realloc=True).ap()

    ident_bf = _sbl("c4_identb", [128, 128], BF16)
    ident_f32 = _sbl("c4_identf", [128, 128], F32)
    ones_bf = _sbl("c4_ones", [1, 512], BF16)
    bb_sb = _sbl("c4_bb", [1, D], BF16)
    br_sb = _sbl("c4_br", [1, E_EXP], BF16)
    kvec = _sbl("c4_kvec", [128, 8], F32)
    R16 = _sbl("c4_R16", [8, 128], F32)
    WrT_sb = _sbl("c4_WrT", [128, 32 * 8], BF16)
    laA_sb = _sbl("c4_laA", [128, 32 * 128], BF16)
    laB_sb = _sbl("c4_laB", [128, D], BF16)
    x3T = _sbl("c4_x3T", [128, 32 * TOK], BF16)
    hwT_sb = _sbl("c4_hwT", [128, NT * 128], BF16)
    c_ld = p.sem()
    c_ld2 = p.sem()
    with nc.Block() as blk:
        @blk.sync
        def _(s):
            c_ld.inc(s.dma_start(out=ident_bf[:], in_=ident_bf_d), 16)
            c_ld.inc(s.dma_start(out=ident_f32[:], in_=ident_f32_d), 16)
            c_ld.inc(s.dma_start(out=ones_bf[:], in_=ones_bf_d), 16)
            c_ld.inc(s.dma_start(out=kvec[:], in_=kvec_d), 16)
            c_ld.inc(s.dma_start(out=R16[:], in_=R16_d), 16)

        @blk.gpsimd
        def _(g):
            c_ld2.inc(g.dma_start(out=bb_sb[:], in_=b_base), 16)
            c_ld2.inc(g.dma_start(out=br_sb[:], in_=b_router), 16)
            c_ld2.inc(g.dma_start(
                out=WrT_sb[:, :].rearrange("p (c e) -> p c e", c=32),
                in_=WrT.rearrange("(c p) e -> p c e", p=128)), 16)
            c_ld2.inc(g.dma_start(out=laB_sb[:], in_=laB2), 16)
            for j in range(4):
                c_ld2.inc(g.dma_start(
                    out=laA_sb[:, j * 8 * 128:(j + 1) * 8 * 128]
                        .rearrange("p (c e) -> p c e", c=8),
                    in_=laA2.rearrange("(c p) e -> p c e", p=128)[
                        :, j * 8:(j + 1) * 8, :]), 16)

    evn3, evT3, stf3, _tp = emit_ln(p, nc, y3, ysum3, x3i, x3T, ident_bf,
                                    entry_waits=[(c_ld, c_ld.v), (c_ld2, c_ld2.v)],
                                    tag="c4ln")

    mml = p.sem()
    evl = p.sem()
    dsp = p.sem()
    mmh = p.sem()
    mmt = p.sem()
    evt = p.sem()
    mmx = p.sem()
    evx = p.sem()
    evw = p.sem()
    with nc.sbuf_tensor("c4_lg", [128, NT * 8], F32, allow_name_mangling=True) as lg, \
         nc.sbuf_tensor("c4_sa", [128, 8], F32, allow_name_mangling=True) as sa, \
         nc.sbuf_tensor("c4_sb2", [128, 8], F32, allow_name_mangling=True) as sb2, \
         nc.sbuf_tensor("c4_sc", [128, 16], F32, allow_name_mangling=True) as scx, \
         nc.sbuf_tensor("c4_tm8", [128, 8], F32, allow_name_mangling=True) as tm8, \
         nc.sbuf_tensor("c4_wt", [8, 128], F32, allow_name_mangling=True) as wt_sb, \
         nc.sbuf_tensor("c4_wfm", [128, 128], F32, allow_name_mangling=True) as wfm_sb, \
         nc.sbuf_tensor("c4_wtok", [128, NT * 8], F32, allow_name_mangling=True) as wtok, \
         nc.psum_tensor("c4_pq", [128, 4096], F32, allow_name_mangling=True) as pq, \
         nc.Block() as blk:
        @blk.tensor
        def _(t):
            t.wait_ge(evT3.h, evT3.v)
            for tt in range(NT):
                last = None
                for kc in range(32):
                    last = t.matmul(pq[:, tt * 512:tt * 512 + 8],
                                    chunk_ap(x3T, TOK, kc, tt * 128, 128),
                                    WrT_sb[:, kc * 8:(kc + 1) * 8],
                                    start=(kc == 0), stop=False)
                last = t.matmul(pq[:, tt * 512:tt * 512 + 8],
                                ones_bf[0:1, 0:128], br_sb[0:1, :],
                                start=False, stop=True)
                mml.inc(last)
                last = None
                for kc in range(32):
                    last = t.matmul(pq[:, tt * 512 + 128:tt * 512 + 256],
                                    laA_sb[:, kc * 128:(kc + 1) * 128],
                                    chunk_ap(x3T, TOK, kc, tt * 128, 128),
                                    start=(kc == 0), stop=(kc == 31))
                mmh.inc(last)
            for tt in range(NT):
                t.wait_ge(dsp.h, (tt + 1) * 100)
                t.wait_ge(evl.h, tt + 1)
                if tt >= 1:
                    t.wait_ge(evt.h, tt)
                mmt.inc(t.transpose(pq[0:8, tt * 512 + 256:tt * 512 + 384],
                                    wtok[:, tt * 8:(tt + 1) * 8],
                                    ident_f32[:]))
                t.wait_ge(evt.h, tt + 1)
                if tt >= 1:
                    t.wait_ge(evx.h, tt)
                mmx.inc(t.matmul(pq[:, tt * 512 + 384:tt * 512 + 512],
                                 R16[:, :], wt_sb[:, :], start=True, stop=True))

        @blk.vector
        def _(v):
            def ch(instr, n=1):
                dsp.inc(instr, n)
                v.wait_ge(dsp.h, dsp.v)
                return instr

            for tt in range(NT):
                base = dsp.v
                v.wait_ge(mml.h, tt + 1)
                v.wait_ge(mmh.h, tt + 1)
                evl.inc(v.tensor_copy(lg[:, tt * 8:(tt + 1) * 8],
                                      pq[:, tt * 512:tt * 512 + 8]))
                v.wait_ge(evl.h, evl.v)
                ch(v.tensor_copy(sa[:, :], lg[:, tt * 8:(tt + 1) * 8]))
                src, dst = sa, sb2
                for stage in SORT8_STAGES:
                    in_pairs = set()
                    last = None
                    for (i, j) in stage:
                        last = v.tensor_max(dst[:, i:i + 1], src[:, i:i + 1],
                                            src[:, j:j + 1])
                        last = v.tensor_tensor(dst[:, j:j + 1], src[:, i:i + 1],
                                               src[:, j:j + 1], ALU.min)
                        in_pairs.add(i)
                        in_pairs.add(j)
                    for k in range(8):
                        if k not in in_pairs:
                            last = v.tensor_copy(dst[:, k:k + 1], src[:, k:k + 1])
                    ch(last)
                    src, dst = dst, src
                ch(v.tensor_copy(scx[:, 0:1], src[:, 0:1]))
                for jj in range(1, 8):
                    ch(v.tensor_add(scx[:, jj:jj + 1], scx[:, jj - 1:jj],
                                    src[:, jj:jj + 1]))
                ch(v.tensor_tensor(tm8[:, :], src[:, :], kvec[:, :], ALU.mult))
                ch(v.tensor_scalar_add(src[:, :], tm8[:, :], 1.0))
                ch(v.tensor_tensor(dst[:, :], src[:, :], scx[:, 0:8], ALU.is_gt))
                ch(v.tensor_reduce(scx[:, 8:9], dst[:, :], AX.X, ALU.add))
                ch(v.tensor_sub(tm8[:, 0:7], dst[:, 0:7], dst[:, 1:8]))
                ch(v.tensor_copy(tm8[:, 7:8], dst[:, 7:8]))
                ch(v.tensor_tensor(dst[:, :], scx[:, 0:8], tm8[:, :], ALU.mult))
                ch(v.tensor_reduce(scx[:, 9:10], dst[:, :], AX.X, ALU.add))
                ch(v.reciprocal(scx[:, 10:11], scx[:, 8:9]))
                ch(v.tensor_scalar_add(scx[:, 11:12], scx[:, 9:10], -1.0))
                ch(v.tensor_mul(scx[:, 12:13], scx[:, 11:12], scx[:, 10:11]))
                ch(v.tensor_scalar(wtok[:, tt * 8:(tt + 1) * 8],
                                   lg[:, tt * 8:(tt + 1) * 8],
                                   scx[:, 12:13], 0.0, ALU.subtract, ALU.max))
                pad = base + 100 - dsp.v
                assert pad >= 0, dsp.v
                if pad:
                    dsp.inc(v.engine_nop(), pad)
            for tt in range(NT):
                v.wait_ge(mmt.h, tt + 1)
                evt.inc(v.tensor_copy(wt_sb[:, :],
                                      pq[0:8, tt * 512 + 256:tt * 512 + 384]))
                v.wait_ge(mmx.h, tt + 1)
                v.wait_ge(evt.h, tt + 1)
                if tt >= 1:
                    v.wait_ge(evw.h, tt)
                evx.inc(v.tensor_copy(wfm_sb[:, :],
                                      pq[:, tt * 512 + 384:tt * 512 + 512]))
                v.wait_ge(evx.h, tt + 1)
                v.wait_ge(mmh.h, tt + 1)
                evw.inc(v.tensor_tensor(hwT_sb[:, tt * 128:(tt + 1) * 128],
                                        pq[:, tt * 512 + 128:tt * 512 + 256],
                                        wfm_sb[:, :], ALU.mult))

    _c4pq = [None]

    def lora_mm(t, dg, tt):
        return t.matmul(
            _c4pq[0][:, tt * 512:(tt + 1) * 512],
            hwT_sb[:, tt * 128:(tt + 1) * 128],
            laB_sb[:, dg * 512:(dg + 1) * 512],
            start=False, stop=True)

    w_fr4, mme4, eve4, sty4, sts4 = emit_mm_res(
        p, nc, x3T, 32, WbT, D, x3i, y4, ysum4, ones_bf, bias_sb=bb_sb,
        entry_waits=[(stf3[0], stf3[0].v), (stf3[1], stf3[1].v)],
        extra_mm=lora_mm, extra_waits=[(evw, NT)], psum_ref=_c4pq, tag="c4mm")
    _es.close()
    return sty4 + [sts4]


def build_launch_C4():
    p = Prog()
    io = dict(
        y3=p.inp("y3", [TOK, D], F32),
        ysum3=p.inp("ysum3", [128, NT], F32),
        WbT=p.inp("WbT", [D, D], F32),
        b_base=p.inp("b_base", [1, D], F32),
        WrT=p.inp("WrT", [D, E_EXP], F32),
        b_router=p.inp("b_router", [1, E_EXP], F32),
        laA2=p.inp("laA2", [D, 128], F32),
        laB2=p.inp("laB2", [128, D], F32),
        kvec=p.inp("kvec", [128, 8], F32),
        R16=p.inp("R16", [8, 128], F32),
        ident_f32=p.inp("ident_f32", [128, 128], F32),
        ident_bf=p.inp("ident_bf", [128, 128], BF16),
        ones_bf=p.inp("ones_bf", [1, 512], BF16),
        y4=p.outp("y4", [TOK, D], F32),
        ysum4=p.outp("ysum4", [128, NT], F32),
    )
    sts = emit_launch_C4(p, io)
    with p.nc.Block() as blk:
        @blk.sync
        def _(s):
            for cs in sts:
                s.wait_ge(cs.h, cs.v)
    return p


def run_launch_C4(inputs, resC3, consts=None):
    consts = consts or host_consts()
    ones512 = np.ones((1, 512), dtype=BF)
    kvec = np.broadcast_to(np.arange(1, 9, dtype=np.float32), (128, 8)).copy()
    R16 = np.zeros((8, 128), dtype=np.float32)
    for pp_ in range(128):
        R16[pp_ // 16, pp_] = 1.0
    WbT = np.ascontiguousarray(inputs["L1_ffn_W_base"].T)
    WrT = np.ascontiguousarray(inputs["L1_ffn_W_router"].T)
    laA2 = np.ascontiguousarray(
        inputs["L1_ffn_lora_A"].transpose(1, 0, 2).reshape(D, 128))
    laB2 = np.ascontiguousarray(inputs["L1_ffn_lora_B"].reshape(128, D))
    in_maps = []
    for c in range(NCORES):
        in_maps.append(dict(
            y3=resC3[c]["y3"], ysum3=resC3[c]["ysum3"],
            WbT=WbT, b_base=inputs["L1_ffn_b_base"].reshape(1, -1),
            WrT=WrT, b_router=inputs["L1_ffn_b_router"].reshape(1, -1),
            laA2=laA2, laB2=laB2, kvec=kvec, R16=R16,
            ident_f32=consts["ident_f32"], ident_bf=consts["ident_bf"],
            ones_bf=ones512,
        ))
    pC = build_launch_C4()
    res = run_bass_kernel_spmd(pC.nc, in_maps, core_ids=list(range(NCORES)))
    return res.results




def phase_reset(p, store_ctrs=()):
    nc = p.nc
    live = [cs for cs in store_ctrs if cs is not None and cs.v]
    if live:
        with nc.Block() as blk:
            @blk.sync
            def _(s):
                for cs in live:
                    s.wait_ge(cs.h, cs.v)
    nc.all_engine_barrier()
    nc.clear_and_free_semaphores(nc.allocated_since(p.snap))
    nc.all_engine_barrier()
    nc.restore_sems(p.snap)


def store_join(p, store_ctrs=()):
    nc = p.nc
    live = [cs for cs in store_ctrs if cs is not None and cs.v]
    if live:
        with nc.Block() as blk:
            @blk.sync
            def _(s):
                for cs in live:
                    s.wait_ge(cs.h, cs.v)


def build_fused():
    p = Prog()
    nc = p.nc
    i_ = p.inp
    x = i_("x", [TOK, D])
    xT_own = i_("xT_own", [D, TOK])
    xbT = i_("xbT", [D, S])
    mask = i_("mask", [S, D], I32)
    w_qa = i_("w_qa", [D, QR]); g_qa = i_("g_qa", [1, QR])
    w_qb = i_("w_qb", [QR, NH * QK])
    w_kva = i_("w_kva", [D, KVR + RD]); g_kva = i_("g_kva", [1, KVR])
    w_kvb = i_("w_kvb", [KVR, NH * (ND + VD)])
    w_o = i_("w_o", [D, D])
    U1 = i_("U1", [D, RANK]); V1 = i_("V1", [RANK, 2 * INTER])
    U2 = i_("U2", [INTER, RANK]); V2 = i_("V2", [RANK, D])
    b1 = i_("b1", [1, 2 * INTER]); b2 = i_("b2", [1, D])
    wT = i_("wT", [D, D]); bias1 = i_("bias1", [1, D])
    WbT = i_("WbT", [D, D]); b_base = i_("b_base", [1, D])
    WrT = i_("WrT", [D, E_EXP]); b_router = i_("b_router", [1, E_EXP])
    laA2 = i_("laA2", [D, 128]); laB2 = i_("laB2", [128, D])
    cos_own = i_("cos_own", [RD, TOK]); sin_own = i_("sin_own", [RD, TOK])
    cos_full = i_("cos_full", [RD, S]); sin_full = i_("sin_full", [RD, S])
    ident_bf = i_("ident_bf", [128, 128], BF16)
    ident_f32 = i_("ident_f32", [128, 128])
    ones_f32 = i_("ones_f32", [1, 128])
    ones_bf = i_("ones_bf", [1, 512], BF16)
    Gmat = i_("Gmat", [128, 4]); Emat = i_("Emat", [4, 128])
    kvec = i_("kvec", [128, 8]); R16 = i_("R16", [8, 128])
    out = p.outp("out", [TOK, D], F32)

    def internal(name, shape, dt=F32):
        return nc.dram_tensor(name, list(shape), dt, kind="Internal").ap()

    Qf_i = internal("Qf_i", [NH * QK, TOK], BF16)
    KN_i = internal("KN_i", [NH * ND, S], BF16)
    KR_i = internal("KR_i", [RD, S], BF16)
    Vt_i = internal("Vt_i", [S, NH * VD], BF16)
    attn_i = internal("attn_i", [TOK, NH * VD], BF16)
    bm_i = internal("bm_i", [128, 128])
    bmT_i = internal("bmT_i", [128, 128])
    ys = {k: internal(f"ysum{k}_i", [128, NT]) for k in (1, 2, 3, 4)}
    yv = {k: internal(f"y{k}_i", [TOK, D]) for k in (1, 2, 3, 4)}

    p.snap = nc.snapshot_sems()

    st = emit_launch_A(p, dict(
        xT=xT_own, w_qa=w_qa, g_qa=g_qa, w_qb=w_qb, w_kva=w_kva, g_kva=g_kva,
        w_kvb=w_kvb, cos_own=cos_own, sin_own=sin_own,
        ident_bf=ident_bf, ident_f32=ident_f32, ones_f32=ones_f32,
        Qf=Qf_i, KN=KN_i, KR=KR_i, Vt=Vt_i), do_q=True, do_kv=False,
        do_bm=False)
    phase_reset(p, st)

    for sl in range(4):
        st = emit_launch_A(p, dict(
            xT=xbT[:, sl * TOK:(sl + 1) * TOK],
            w_kva=w_kva, g_kva=g_kva, w_kvb=w_kvb,
            cos_own=cos_full[:, sl * TOK:(sl + 1) * TOK],
            sin_own=sin_full[:, sl * TOK:(sl + 1) * TOK],
            ident_bf=ident_bf, ident_f32=ident_f32, ones_f32=ones_f32,
            KN=KN_i[:, sl * TOK:(sl + 1) * TOK],
            KR=KR_i[:, sl * TOK:(sl + 1) * TOK],
            Vt=Vt_i[sl * TOK:(sl + 1) * TOK, :]),
            do_q=False, do_kv=True, do_bm=False)
        if sl % 2 == 1:
            phase_reset(p, st)
        else:
            store_join(p, st)

    st = emit_bm(p, nc, mask, Gmat, bm_i, 32)
    store_join(p, st)
    stT = p.sem()
    ld_t = p.sem()
    mm_t = p.sem()
    ev_t = p.sem()
    with nc.sbuf_tensor("fb_bm", [128, 128], F32, allow_name_mangling=True) as bm_sb, \
         nc.sbuf_tensor("fb_id", [128, 128], F32, allow_name_mangling=True) as idf, \
         nc.sbuf_tensor("fb_bmT", [128, 128], F32, allow_name_mangling=True) as bmT_sb, \
         nc.psum_tensor("fb_pt", [128, 128], F32, allow_name_mangling=True) as pt, \
         nc.Block() as blk:
        @blk.sync
        def _(s):
            ld_t.inc(s.dma_start(out=bm_sb[:], in_=bm_i), 16)
            ld_t.inc(s.dma_start(out=idf[:], in_=ident_f32), 16)

        @blk.tensor
        def _(t):
            t.wait_ge(ld_t.h, 32)
            mm_t.inc(t.transpose(pt[:], bm_sb[:], idf[:]))

        @blk.vector
        def _(v):
            v.wait_ge(mm_t.h, 1)
            ev_t.inc(v.tensor_copy(bmT_sb[:], pt[:]))

        @blk.scalar
        def _(s):
            s.wait_ge(ev_t.h, 1)
            stT.inc(s.dma_start(out=bmT_i, in_=bmT_sb[:]), 16)
    store_join(p, [stT])

    st = emit_launch_B2(p, dict(Qf=Qf_i, KN_b=KN_i, KR_b=KR_i, Vt_b=Vt_i,
                                ident_bf=ident_bf, attn=attn_i))
    phase_reset(p, st)

    st = emit_launch_C1(p, dict(attn=attn_i, x=x, w_o=w_o, ident_bf=ident_bf,
                                y1=yv[1], ysum1=ys[1]))
    store_join(p, st)
    st = emit_launch_C2(p, dict(y1=yv[1], ysum1=ys[1], U1=U1, V1=V1, U2=U2,
                                V2=V2, b1=b1, b2=b2, ident_bf=ident_bf,
                                ones_bf=ones_bf, y2=yv[2], ysum2=ys[2]))
    phase_reset(p, st)
    st = emit_launch_C3(p, dict(y2=yv[2], ysum2=ys[2], wT=wT, bias=bias1,
                                bmT=bmT_i, Emat=Emat, ident_bf=ident_bf,
                                ones_bf=ones_bf, y3=yv[3], ysum3=ys[3]))
    store_join(p, st)
    st = emit_launch_C4(p, dict(y3=yv[3], ysum3=ys[3], WbT=WbT, b_base=b_base,
                                WrT=WrT, b_router=b_router, laA2=laA2,
                                laB2=laB2, kvec=kvec, R16=R16,
                                ident_f32=ident_f32, ident_bf=ident_bf,
                                ones_bf=ones_bf, y4=yv[4], ysum4=ys[4]))
    store_join(p, st)
    st = emit_launch_C5(p, dict(y4=yv[4], ysum4=ys[4], out=out))
    with nc.Block() as blk:
        @blk.sync
        def _(s):
            for cs in st:
                if cs.v:
                    s.wait_ge(cs.h, cs.v)
    return p


def kernel_fused(inputs):
    consts = host_consts()
    ones512 = np.ones((1, 512), dtype=BF)
    kvec = np.broadcast_to(np.arange(1, 9, dtype=np.float32), (128, 8)).copy()
    R16 = np.zeros((8, 128), dtype=np.float32)
    for pp_ in range(128):
        R16[pp_ // 16, pp_] = 1.0
    Emat = np.zeros((4, 128), dtype=np.float32)
    for pp_ in range(128):
        Emat[pp_ // 32, pp_] = 1.0
    x_full = inputs["L0_attn_hidden_states"]
    wTh = np.ascontiguousarray(inputs["L1_attn_weight"].T)
    WbT = np.ascontiguousarray(inputs["L1_ffn_W_base"].T)
    WrT = np.ascontiguousarray(inputs["L1_ffn_W_router"].T)
    laA2 = np.ascontiguousarray(
        inputs["L1_ffn_lora_A"].transpose(1, 0, 2).reshape(D, 128))
    laB2 = np.ascontiguousarray(inputs["L1_ffn_lora_B"].reshape(128, D))
    in_maps = []
    for c in range(NCORES):
        b, off = divmod(c, 4)
        pos0 = off * TOK
        in_maps.append(dict(
            x=np.ascontiguousarray(x_full[b, pos0:pos0 + TOK, :]),
            xT_own=np.ascontiguousarray(x_full[b, pos0:pos0 + TOK, :].T),
            xbT=np.ascontiguousarray(x_full[b].T),
            mask=inputs["L1_attn_mask"],
            w_qa=inputs["L0_attn_w_qa"],
            g_qa=inputs["L0_attn_g_qa"].reshape(1, -1),
            w_qb=inputs["L0_attn_w_qb"],
            w_kva=inputs["L0_attn_w_kva"],
            g_kva=inputs["L0_attn_g_kva"].reshape(1, -1),
            w_kvb=inputs["L0_attn_w_kvb"],
            w_o=inputs["L0_attn_w_o"],
            U1=inputs["L0_ffn_U1"], V1=inputs["L0_ffn_V1"],
            U2=inputs["L0_ffn_U2"], V2=inputs["L0_ffn_V2"],
            b1=inputs["L0_ffn_b1"].reshape(1, -1),
            b2=inputs["L0_ffn_b2"].reshape(1, -1),
            wT=wTh, bias1=inputs["L1_attn_bias"].reshape(1, -1),
            WbT=WbT, b_base=inputs["L1_ffn_b_base"].reshape(1, -1),
            WrT=WrT, b_router=inputs["L1_ffn_b_router"].reshape(1, -1),
            laA2=laA2, laB2=laB2,
            cos_own=np.ascontiguousarray(consts["cos_fm"][:, pos0:pos0 + TOK]),
            sin_own=np.ascontiguousarray(consts["sin_fm"][:, pos0:pos0 + TOK]),
            cos_full=consts["cos_fm"], sin_full=consts["sin_fm"],
            ident_bf=consts["ident_bf"], ident_f32=consts["ident_f32"],
            ones_f32=consts["ones_f32"], ones_bf=ones512,
            Gmat=consts["Gmat"], Emat=Emat, kvec=kvec, R16=R16,
        ))
    pF = build_fused()
    res = run_bass_kernel_spmd(pF.nc, in_maps, core_ids=list(range(NCORES)))
    outv = np.zeros((B, S, D), np.float32)
    for c in range(NCORES):
        b, off = divmod(c, 4)
        outv[b, off * TOK:(off + 1) * TOK] = res.results[c]["out"]
    return outv


FUSED = True

def kernel(**inputs):
    inputs = {k: np.ascontiguousarray(v) for k, v in inputs.items()}
    if FUSED:
        return kernel_fused(inputs)
    consts = host_consts()
    resA = kernel_resA = run_launch_A(inputs, consts)
    resB = run_launch_B(resA, consts)
    resC1 = run_launch_C1(inputs, resB, consts)
    resC2 = run_launch_C2(inputs, resC1, consts)
    bm_full = np.concatenate([resA[c]["bmp"] for c in range(NCORES)], axis=0)
    resC3 = run_launch_C3(inputs, resC2, bm_full, consts)
    resC4 = run_launch_C4(inputs, resC3, consts)
    resC5 = run_launch_C5(resC4)
    out = np.zeros((B, S, D), np.float32)
    for c in range(NCORES):
        b, off = divmod(c, 4)
        out[b, off * TOK:(off + 1) * TOK] = resC5[c]["out"]
    return out


# version 07
# speedup vs baseline: 1.2119x; 1.0357x over previous
import numpy as np
import ml_dtypes
import concourse.bass as bass
import concourse.mybir as mybir
from concourse.bass_utils import run_bass_kernel_spmd
from contextlib import ExitStack

F32 = mybir.dt.float32
BF16 = mybir.dt.bfloat16
I32 = mybir.dt.int32
AF = mybir.ActivationFunctionType
ALU = mybir.AluOpType
AX = mybir.AxisListType

B, S, D = 2, 4096, 4096
NCORES = 8
TOK = 1024
NT = TOK // 128
NH, QR, KVR, RD, ND, VD = 32, 1536, 512, 64, 128, 128
QK = ND + RD
EPS_RMS = 1e-6
EPS_LN = 1e-5
SM_SCALE = QK ** (-0.5)
INTER = 11008
RANK = 1024
E_EXP, R_LORA = 8, 16

BF = ml_dtypes.bfloat16


def host_consts():
    inv_freq = 10000.0 ** (-(np.arange(0, RD, 2, dtype=np.float64) / RD))
    t = np.arange(S, dtype=np.float64)
    freqs = np.outer(t, inv_freq)
    emb = np.concatenate([freqs, freqs], -1)
    Gmat = np.zeros((128, 4), dtype=np.float32)
    for r in range(128):
        Gmat[r, r // 32] = 1.0
    return dict(
        ident_bf=np.eye(128, dtype=BF),
        ident_f32=np.eye(128, dtype=np.float32),
        ones_f32=np.ones((1, 128), dtype=np.float32),
        ones_bf=np.ones((1, 128), dtype=BF),
        cos_fm=np.ascontiguousarray(np.cos(emb).T.astype(np.float32)),
        sin_fm=np.ascontiguousarray(np.sin(emb).T.astype(np.float32)),
        Gmat=Gmat,
    )


class Ctr:

    def __init__(self, h):
        self.h = h
        self.v = 0

    def inc(self, instr, n=1):
        instr.then_inc(self.h, n)
        self.v += n


class DStream:

    def __init__(self, p, nslots=2):
        self.sems = [p.sem() for _ in range(nslots)]
        self.n = nslots
        self.i = 0

    def issue(self, instr):
        s = self.sems[self.i % self.n]
        instr.then_inc(s.h, 16)
        s.v += 16
        self.i += 1
        return self.i - 1

    def wait(self, eng, idx):
        eng.wait_ge(self.sems[idx % self.n].h, 16 * (idx // self.n + 1))

    def wait_all(self, eng):
        for s in self.sems:
            if s.v:
                eng.wait_ge(s.h, s.v)


class Prog:
    def __init__(self):
        self.nc = bass.Bass("TRN2", target_bir_lowering=False, debug=False)
        self.ctx = ExitStack()
        self._sem_n = 0

    def sb(self, name, shape, dt=F32):
        return self.ctx.enter_context(
            self.nc.sbuf_tensor(name, shape, dt, allow_name_mangling=True))

    def ps(self, name, shape, dt=F32):
        return self.ctx.enter_context(
            self.nc.psum_tensor(name, shape, dt, allow_name_mangling=True))

    def sem(self, name=None):
        self._sem_n += 1
        name = name or f"s{self._sem_n}"
        return Ctr(self.ctx.enter_context(self.nc.semaphore(name)))

    def inp(self, name, shape, dt=F32):
        return self.nc.dram_tensor(name, list(shape), dt, kind="ExternalInput").ap()

    def outp(self, name, shape, dt=F32):
        return self.nc.dram_tensor(name, list(shape), dt, kind="ExternalOutput").ap()


def chunks_ap(tensor_sb, tok, c0, tt, nch):
    nch_total = tensor_sb.shape[1] // tok
    return tensor_sb[:, :].rearrange("p (c t) -> p c t", c=nch_total)[
        :, c0:c0 + nch, tt * 128:(tt + 1) * 128]


def chunk_ap(tensor_sb, tok, c, lo, w):
    return tensor_sb[:, c * tok + lo:c * tok + lo + w]


def emit_rope_fm(v, out_bf, src, cos, sin, tmp, tmp2, w, chain, prev_done=None):
    if chain.v:
        v.wait_ge(chain.h, chain.v)
    if prev_done is not None:
        v.wait_ge(prev_done[0].h, prev_done[1])
    v.tensor_mul(tmp[0:32, 0:w], src[32:64, 0:w], sin[0:32, 0:w])
    v.tensor_mul(tmp[32:64, 0:w], src[0:32, 0:w], sin[32:64, 0:w])
    v.tensor_mul(tmp2[0:32, 0:w], src[0:32, 0:w], cos[0:32, 0:w])
    chain.inc(v.tensor_mul(tmp2[32:64, 0:w], src[32:64, 0:w], cos[32:64, 0:w]))
    v.wait_ge(chain.h, chain.v)
    chain.inc(v.tensor_sub(out_bf[0:32, 0:w], tmp2[0:32, 0:w], tmp[0:32, 0:w]))
    return v.tensor_add(out_bf[32:64, 0:w], tmp2[32:64, 0:w], tmp[32:64, 0:w])


def emit_bm(p, nc, mask_rows, Gmat_d, bmp, bm_nch, entry_waits=()):
    g_ld = p.sem()
    m_ld = [p.sem() for _ in range(6)]
    ev6 = p.sem()
    mm6 = p.sem()
    ev6b = p.sem()
    stBM = p.sem()
    with nc.sbuf_tensor("bm_G", [128, 4], F32, allow_name_mangling=True) as Gmat_s, \
         nc.sbuf_tensor("mbuf", [128, 6 * D], I32, allow_name_mangling=True) as mbuf, \
         nc.sbuf_tensor("mred", [128, 2 * 128], F32, allow_name_mangling=True) as mred, \
         nc.sbuf_tensor("bms", [4, bm_nch * 128], F32, allow_name_mangling=True) as bms, \
         nc.psum_tensor("pbm", [4, 2 * 128], F32, allow_name_mangling=True) as pbm, \
         nc.Block() as blk:
        @blk.sync
        def _(s):
            for ctr, val in entry_waits:
                s.wait_ge(ctr.h, val)
            g_ld.inc(s.dma_start(out=Gmat_s[:], in_=Gmat_d), 16)
            for ch in range(bm_nch):
                if ch >= 6:
                    s.wait_ge(ev6.h, ch - 5)
                m_ld[ch % 6].inc(s.dma_start(
                    out=mbuf[:, (ch % 6) * D:(ch % 6 + 1) * D],
                    in_=mask_rows[ch * 128:(ch + 1) * 128, :]), 16)

        @blk.vector
        def _(v):
            for step in range(bm_nch + 1):
                if step < bm_nch:
                    v.wait_ge(m_ld[step % 6].h, 16 * (step // 6 + 1))
                    if step >= 2:
                        v.wait_ge(mm6.h, step - 1)
                    ev6.inc(v.tensor_reduce(
                        mred[:, (step % 2) * 128:(step % 2 + 1) * 128],
                        mbuf[:, (step % 6) * D:(step % 6 + 1) * D].rearrange(
                            "p (a b) -> p a b", b=32),
                        AX.X, ALU.add))
                if step >= 1:
                    cch = step - 1
                    v.wait_ge(mm6.h, cch + 1)
                    ev6b.inc(v.tensor_copy(
                        bms[:, cch * 128:(cch + 1) * 128],
                        pbm[:, (cch % 2) * 128:(cch % 2 + 1) * 128]))

        @blk.tensor
        def _(t):
            for ctr, val in entry_waits:
                t.wait_ge(ctr.h, val)
            t.wait_ge(g_ld.h, 16)
            for ch in range(bm_nch):
                t.wait_ge(ev6.h, ch + 1)
                if ch >= 2:
                    t.wait_ge(ev6b.h, ch - 1)
                mm6.inc(t.matmul(pbm[:, (ch % 2) * 128:(ch % 2 + 1) * 128],
                                 Gmat_s[:, :],
                                 mred[:, (ch % 2) * 128:(ch % 2 + 1) * 128],
                                 start=True, stop=True))

        @blk.scalar
        def _(s):
            s.wait_ge(ev6b.h, bm_nch)
            stBM.inc(s.dma_start(
                out=bmp.rearrange("(c p) w -> p c w", p=4),
                in_=bms[:, :].rearrange("p (c w) -> p c w", c=bm_nch)), 16)
    return [stBM]


def emit_launch_A(p, io, do_q=True, do_kv=True, do_bm=True, bm_nch=4,
                  wkvb_pre=None):
    nc = p.nc
    _es = ExitStack()
    def _sbl(name, shape, dt=F32):
        return _es.enter_context(
            nc.sbuf_tensor(name, shape, dt, allow_name_mangling=True))
    xT_src = io["xT"]
    w_qa = io.get("w_qa")
    g_qa = io.get("g_qa")
    w_qb = io.get("w_qb")
    w_kva = io["w_kva"]
    g_kva = io["g_kva"]
    w_kvb = io["w_kvb"]
    cos_own = io["cos_own"]
    sin_own = io["sin_own"]
    mask_rows = io.get("mask_rows")
    Gmat_d = io.get("Gmat")
    ident_bf_d = io["ident_bf"]
    ident_f32_d = io["ident_f32"]
    ones_f32_d = io["ones_f32"]
    Qf = io.get("Qf")
    KN = io["KN"]
    KR = io["KR"]
    Vt = io["Vt"]
    bmp = io.get("bmp")

    ident_bf = _sbl("ident_bf_s", [128, 128], BF16)
    ident_f32 = _sbl("ident_f32_s", [128, 128])
    ones_f32 = _sbl("ones_f32_s", [1, 128])
    cos_sb = _sbl("cos_sb", [RD, TOK])
    sin_sb = _sbl("sin_sb", [RD, TOK])
    g_kva_b = _sbl("g_kva_b", [128, KVR])
    g_row2 = _sbl("g_row2", [1, KVR])
    if do_q:
        g_qa_b = _sbl("g_qa_b", [128, QR])
        g_row = _sbl("g_row", [1, QR])
        qcT = _sbl("qcT", [128, 12 * TOK], BF16)
    kcT = _sbl("kcT", [128, 4 * TOK], BF16)
    kr_f32 = _sbl("kr_f32", [128, NT * RD])
    rope_tmp = _sbl("rope_tmp", [64, TOK])
    rope_tmp2 = _sbl("rope_tmp2", [64, TOK])
    rope_chain = p.sem()

    all_stores = []

    def new_store(nslots=1):
        cs = [p.sem() for _ in range(nslots)]
        all_stores.extend(cs)
        return cs

    c_ld = p.sem()
    mm0 = p.sem()
    ev0 = p.sem()
    kvp_ld = p.sem()
    with nc.psum_tensor("pg", [128, QR], allow_name_mangling=True) as pg, \
         nc.psum_tensor("pg2", [128, KVR], allow_name_mangling=True) as pg2, \
         nc.Block() as blk:
        if wkvb_pre is not None and wkvb_pre["mode"] == "load":
            @blk.gpsimd
            def _(g):
                wp = wkvb_pre["buf"]
                for r in range(4):
                    kvp_ld.inc(g.dma_start(
                        out=wp[:, r * 8192:(r + 1) * 8192],
                        in_=w_kvb[r * 128:(r + 1) * 128, :]), 16)

        @blk.sync
        def _(s):
            pairs = [(ident_bf[:], ident_bf_d), (ident_f32[:], ident_f32_d),
                     (ones_f32[:], ones_f32_d),
                     (cos_sb[:], cos_own), (sin_sb[:], sin_own)]
            if do_kv:
                pairs.append((g_row2[:], g_kva))
            if do_q:
                pairs.append((g_row[:], g_qa))
            for dst, srcd in pairs:
                c_ld.inc(s.dma_start(out=dst, in_=srcd), 16)

        @blk.tensor
        def _(t):
            t.wait_ge(c_ld.h, c_ld.v)
            last = None
            if do_q:
                for j in range(3):
                    last = t.matmul(pg[:, j * 512:(j + 1) * 512], ones_f32[:],
                                    g_row[0:1, j * 512:(j + 1) * 512],
                                    start=True, stop=True)
            if do_kv:
                last = t.matmul(pg2[:], ones_f32[:], g_row2[0:1, :],
                                start=True, stop=True)
            mm0.inc(last)

        @blk.vector
        def _(v):
            v.wait_ge(mm0.h, 1)
            last = None
            if do_q:
                last = v.tensor_copy(g_qa_b[:], pg[:])
            if do_kv:
                last = v.tensor_copy(g_kva_b[:], pg2[:])
            ev0.inc(last)

    with ExitStack() as _esx:
        xT = _esx.enter_context(nc.sbuf_tensor(
            "xT", [128, 32 * TOK], BF16, allow_name_mangling=True))
        if do_q:
            qc_bf = _esx.enter_context(nc.sbuf_tensor(
                "qc_bf", [128, NT * QR], BF16, allow_name_mangling=True))
            ss_q = _esx.enter_context(nc.sbuf_tensor(
                "ss_q", [128, NT * 3], F32, allow_name_mangling=True))
        kc_bf = _esx.enter_context(nc.sbuf_tensor(
            "kc_bf", [128, NT * KVR], BF16, allow_name_mangling=True))
        ss_k = _esx.enter_context(nc.sbuf_tensor(
            "ss_k", [128, NT], F32, allow_name_mangling=True))

        x_ld = [p.sem(), p.sem()]
        with nc.Block() as blk:
            @blk.gpsimd
            def _(g):
                for c in range(32):
                    x_ld[c % 2].inc(g.dma_start(
                        out=chunk_ap(xT, TOK, c, 0, TOK),
                        in_=xT_src[c * 128:(c + 1) * 128, :]), 16)
        xT_ready = [(x_ld[0], x_ld[0].v), (x_ld[1], x_ld[1].v), (ev0, ev0.v)]

        def proj(w_dram, nkc, dout, out_bf, ss, tag, prev_psum_free,
                 prev_sbuf_free=(), extra_cols=None, extra_dst=None):
            w_ld = [p.sem() for _ in range(4)]
            w_cst = p.sem()
            w_fr = p.sem()
            ev = p.sem()
            sq = p.sem()
            ndg = dout // 512
            ngrp = ndg + (1 if extra_cols else 0)
            with nc.sbuf_tensor(f"wbuf_{tag}", [128, 4 * 512], BF16, allow_name_mangling=True) as wbuf, \
                 nc.sbuf_tensor(f"wraw_{tag}", [128, 4 * 512], F32, allow_name_mangling=True) as wraw, \
                 nc.sbuf_tensor(f"scr_{tag}", [128, 2 * 512], F32, allow_name_mangling=True) as scr, \
                 nc.psum_tensor(f"pq_{tag}", [128, 4096], allow_name_mangling=True) as pq, \
                 nc.Block() as blk:
                def _gw(dg):
                    return 512 if dg < ndg else extra_cols[1] - extra_cols[0]

                @blk.sync
                def _(s):
                    for ctr, val in xT_ready:
                        s.wait_ge(ctr.h, val)
                    for ctr, val in prev_sbuf_free:
                        s.wait_ge(ctr.h, val)
                    for dg in range(ngrp):
                        for kc in range(nkc):
                            idx = dg * nkc + kc
                            if idx >= 4:
                                s.wait_ge(w_cst.h, idx - 3)
                            if dg < ndg:
                                srcd = w_dram[kc * 128:(kc + 1) * 128,
                                              dg * 512:(dg + 1) * 512]
                            else:
                                srcd = w_dram[kc * 128:(kc + 1) * 128,
                                              extra_cols[0]:extra_cols[1]]
                            w = _gw(dg)
                            w_ld[idx % 4].inc(s.dma_start(
                                out=wraw[:, (idx % 4) * 512:(idx % 4) * 512 + w],
                                in_=srcd), 16)

                @blk.tensor
                def _(t):
                    for ctr, val in prev_psum_free:
                        t.wait_ge(ctr.h, val)
                    for dg in range(ngrp):
                        w = _gw(dg)
                        if dg > 0:
                            t.wait_ge(ev.h, dg * NT)
                            t.wait_ge(sq.h, min(dg, ndg) * NT)
                        for kc in range(nkc):
                            idx = dg * nkc + kc
                            t.wait_ge(w_cst.h, idx + 1)
                            last = None
                            for tt in range(NT):
                                last = t.matmul(
                                    pq[:, tt * 512:tt * 512 + w],
                                    chunk_ap(xT, TOK, kc, tt * 128, 128),
                                    wbuf[:, (idx % 4) * 512:(idx % 4) * 512 + w],
                                    start=(kc == 0), stop=(kc == nkc - 1))
                            w_fr.inc(last)

                @blk.vector
                def _(v):
                    def wcast(idx):
                        w = _gw(idx // nkc)
                        v.wait_ge(w_ld[idx % 4].h, 16 * (idx // 4 + 1))
                        if idx >= 4:
                            v.wait_ge(w_fr.h, idx - 3)
                        w_cst.inc(v.tensor_copy(
                            wbuf[:, (idx % 4) * 512:(idx % 4) * 512 + w],
                            wraw[:, (idx % 4) * 512:(idx % 4) * 512 + w]))

                    def emit_ev(dg):
                        w = _gw(dg)
                        v.wait_ge(w_fr.h, (dg + 1) * nkc)
                        for tt in range(NT):
                            if dg < ndg:
                                dst = out_bf[:, tt * dout + dg * 512:
                                             tt * dout + dg * 512 + w]
                            else:
                                dst = extra_dst[:, tt * w:(tt + 1) * w]
                            ev.inc(v.tensor_copy(dst, pq[:, tt * 512:tt * 512 + w]))

                    for dg in range(ngrp):
                        for kc in range(nkc):
                            wcast(dg * nkc + kc)
                            if kc == 0 and dg > 0:
                                emit_ev(dg - 1)
                    emit_ev(ngrp - 1)

                @blk.scalar
                def _(s):
                    for dg in range(ndg):
                        for tt in range(NT):
                            s.wait_ge(ev.h, dg * NT + tt + 1)
                            idx = dg * NT + tt
                            if idx >= 2:
                                s.wait_ge(sq.h, idx - 1)
                            sq.inc(s.activation(
                                scr[:, (idx % 2) * 512:(idx % 2 + 1) * 512],
                                pq[:, tt * 512:(tt + 1) * 512],
                                AF.Square,
                                accum_out=ss[:, tt * ndg + dg:tt * ndg + dg + 1]))
            return w_fr, ev, sq

        def rmsnorm_T(src_bf, ss, g_bcast, dim, dstT, sq_prev, tag,
                      psum_free_waits):
            ev_a = p.sem()
            sq_b = p.sem()
            ev_c = p.sem()
            tp_d = p.sem()
            ev_e = p.sem()
            ch1 = p.sem()
            ch2 = p.sem()
            ch3 = p.sem()
            ndg = dim // 512
            nch = dim // 128
            nb = (nch + 7) // 8
            sq_prev_base = sq_prev.v - ndg * NT
            with nc.sbuf_tensor(f"ssum_{tag}", [128, NT], F32, allow_name_mangling=True) as ssum, \
                 nc.sbuf_tensor(f"rms_{tag}", [128, NT], F32, allow_name_mangling=True) as rms, \
                 nc.sbuf_tensor(f"rcp_{tag}", [128, NT], F32, allow_name_mangling=True) as rcp, \
                 nc.sbuf_tensor(f"rcp2_{tag}", [128, NT], F32, allow_name_mangling=True) as rcp2, \
                 nc.sbuf_tensor(f"ybf_{tag}", [128, 2 * dim], BF16, allow_name_mangling=True) as ybf, \
                 nc.psum_tensor(f"ptr_{tag}", [128, 2048], BF16, allow_name_mangling=True) as ptr, \
                 nc.Block() as blk:
                @blk.vector
                def _(v):
                    if ndg == 1:
                        for tt in range(NT):
                            v.wait_ge(sq_prev.h, sq_prev_base + tt + 1)
                            ev_a.inc(v.tensor_copy(ssum[:, tt:tt + 1], ss[:, tt:tt + 1]))
                    else:
                        for tt in range(NT):
                            v.wait_ge(sq_prev.h, sq_prev_base + (ndg - 1) * NT + tt + 1)
                            ch1.inc(v.tensor_add(ssum[:, tt:tt + 1],
                                                 ss[:, tt * ndg:tt * ndg + 1],
                                                 ss[:, tt * ndg + 1:tt * ndg + 2]))
                        for tt in range(NT):
                            v.wait_ge(ch1.h, tt + 1)
                            ev_a.inc(v.tensor_add(ssum[:, tt:tt + 1], ssum[:, tt:tt + 1],
                                                  ss[:, tt * ndg + 2:tt * ndg + 3]))

                @blk.scalar
                def _(s):
                    for tt in range(NT):
                        s.wait_ge(ev_a.h, tt + 1)
                        sq_b.inc(s.activation(rms[:, tt:tt + 1], ssum[:, tt:tt + 1],
                                              AF.Sqrt, scale=1.0 / dim))

                @blk.vector
                def _(v):
                    for tt in range(NT):
                        v.wait_ge(sq_b.h, tt + 1)
                        ch2.inc(v.tensor_scalar_add(rcp[:, tt:tt + 1],
                                                    rms[:, tt:tt + 1], EPS_RMS))
                    for tt in range(NT):
                        v.wait_ge(ch2.h, tt + 1)
                        ch3.inc(v.reciprocal(rcp2[:, tt:tt + 1], rcp[:, tt:tt + 1]))
                    def emit_evac(v, tt):
                        for b in range(nb):
                            n_in_b = min(8, nch - b * 8)
                            idx = tt * nb + b
                            v.wait_ge(tp_d.h, idx + 1)
                            po = (idx % 2) * 1024
                            ev_e.inc(v.tensor_copy(
                                chunks_ap(dstT, TOK, b * 8, tt, n_in_b),
                                ptr[:, po:po + n_in_b * 128]))

                    for tt in range(NT):
                        v.wait_ge(ch3.h, tt + 1)
                        if tt >= 2:
                            v.wait_ge(tp_d.h, (tt - 1) * nb)
                        last = None
                        for dg in range(ndg):
                            last = v.scalar_tensor_tensor(
                                ybf[:, (tt % 2) * dim + dg * 512:(tt % 2) * dim + (dg + 1) * 512],
                                src_bf[:, tt * dim + dg * 512:tt * dim + (dg + 1) * 512],
                                rcp2[:, tt:tt + 1],
                                g_bcast[:, dg * 512:(dg + 1) * 512],
                                ALU.mult, ALU.mult)
                        ev_c.inc(last)
                        if tt >= 1:
                            emit_evac(v, tt - 1)
                    emit_evac(v, NT - 1)

                @blk.tensor
                def _(t):
                    for ctr, val in psum_free_waits:
                        t.wait_ge(ctr.h, val)
                    for tt in range(NT):
                        t.wait_ge(ev_c.h, tt + 1)
                        for b in range(nb):
                            n_in_b = min(8, nch - b * 8)
                            idx = tt * nb + b
                            if idx >= 2:
                                t.wait_ge(ev_e.h, idx - 1)
                            po = (idx % 2) * 1024
                            last = None
                            for c8 in range(n_in_b):
                                c = b * 8 + c8
                                last = t.transpose(
                                    ptr[:, po + c8 * 128:po + (c8 + 1) * 128],
                                    ybf[:, (tt % 2) * dim + c * 128:(tt % 2) * dim + (c + 1) * 128],
                                    ident_bf[:])
                            tp_d.inc(last)
            return ev_c, ev_e, tp_d

        evT_q = evT_k = ev3 = None
        if do_q:
            wfr2, ev2, sq2 = proj(w_qa, 32, QR, qc_bf, ss_q, "qc",
                                  prev_psum_free=list(xT_ready))
            evc_q, evT_q, tpd_q = rmsnorm_T(qc_bf, ss_q, g_qa_b, QR, qcT,
                                            sq2, "qn",
                                            psum_free_waits=[(ev2, 3 * NT),
                                                             (sq2, 3 * NT)])
        if do_kv:
            if do_q:
                wfr3, ev3, sq3 = proj(w_kva, 32, KVR, kc_bf, ss_k, "kc",
                                      prev_psum_free=[(ev2, 3 * NT), (sq2, 3 * NT),
                                                      (evT_q, evT_q.v)],
                                      prev_sbuf_free=[(wfr2, 96)],
                                      extra_cols=(KVR, KVR + RD), extra_dst=kr_f32)
            else:
                wfr3, ev3, sq3 = proj(w_kva, 32, KVR, kc_bf, ss_k, "kc",
                                      prev_psum_free=list(xT_ready),
                                      extra_cols=(KVR, KVR + RD), extra_dst=kr_f32)
            evc_k, evT_k, tpd_k = rmsnorm_T(kc_bf, ss_k, g_kva_b, KVR, kcT,
                                            sq3, "kn",
                                            psum_free_waits=[(ev3, 2 * NT),
                                                             (sq3, NT)])

    if do_kv:
        tp3c = p.sem()
        ev3c = p.sem()
        stKR = new_store(1)[0]
        with nc.sbuf_tensor("kr_bf", [RD, TOK], BF16, allow_name_mangling=True) as kr_bf, \
             nc.psum_tensor("prope", [RD, TOK], allow_name_mangling=True) as prope, \
             nc.Block() as blk:
            @blk.tensor
            def _(t):
                t.wait_ge(evT_k.h, evT_k.v)
                t.wait_ge(ev3.h, 2 * NT)
                last = None
                for tt in range(NT):
                    last = t.transpose(prope[:, tt * 128:(tt + 1) * 128],
                                       kr_f32[:, tt * RD:(tt + 1) * RD],
                                       ident_f32[:])
                tp3c.inc(last)

            @blk.vector
            def _(v):
                v.wait_ge(tp3c.h, 1)
                ev3c.inc(emit_rope_fm(v, kr_bf, prope, cos_sb, sin_sb,
                                      rope_tmp, rope_tmp2, TOK, rope_chain))

            @blk.scalar
            def _(s):
                s.wait_ge(ev3c.h, 1)
                stKR.inc(s.dma_start(out=KR, in_=kr_bf[:]), 16)

    if do_q:
        w_ld4 = [p.sem() for _ in range(4)]
        w_cst4 = p.sem()
        mm4 = p.sem()
        ev4 = p.sem()
        stQ = new_store(2)
        with nc.sbuf_tensor("wqb_sb", [128, 2 * 12 * 768], BF16, allow_name_mangling=True) as wqb_sb, \
             nc.sbuf_tensor("wqb_raw", [128, 4 * 768], F32, allow_name_mangling=True) as wqb_raw, \
             nc.sbuf_tensor("qhi_bf", [128, 2 * TOK], BF16, allow_name_mangling=True) as qhi_bf, \
             nc.sbuf_tensor("qlo_bf", [64, 2 * TOK], BF16, allow_name_mangling=True) as qlo_bf, \
             nc.psum_tensor("phi", [128, 2 * TOK], allow_name_mangling=True) as phi, \
             nc.psum_tensor("plo", [64, 2 * TOK], allow_name_mangling=True) as plo, \
             nc.Block() as blk:
            @blk.sync
            def _(s):
                s.wait_ge(evT_q.h, evT_q.v)
                if do_kv:
                    s.wait_ge(evT_k.h, evT_k.v)
                    s.wait_ge(stKR.h, 16)
                for hg in range(8):
                    for c in range(12):
                        idx = hg * 12 + c
                        if idx >= 4:
                            s.wait_ge(w_cst4.h, idx - 3)
                        w_ld4[idx % 4].inc(s.dma_start(
                            out=wqb_raw[:, (idx % 4) * 768:(idx % 4 + 1) * 768],
                            in_=w_qb[c * 128:(c + 1) * 128, hg * 768:(hg + 1) * 768]), 16)

            @blk.tensor
            def _(t):
                if do_kv:
                    t.wait_ge(ev3c.h, 1)
                for h in range(NH):
                    hg, hh = divmod(h, 4)
                    par = h % 2
                    t.wait_ge(w_cst4.h, 12 * (hg + 1))
                    if h >= 2:
                        t.wait_ge(ev4.h, h - 1)
                    last = None
                    for c in range(12):
                        wb = (hg % 2) * 9216 + c * 768 + hh * 192
                        for half in range(2):
                            t.matmul(phi[:, par * TOK + half * 512:par * TOK + half * 512 + 512],
                                     wqb_sb[:, wb:wb + 128],
                                     chunk_ap(qcT, TOK, c, half * 512, 512),
                                     start=(c == 0), stop=(c == 11))
                            last = t.matmul(
                                plo[:, par * TOK + half * 512:par * TOK + half * 512 + 512],
                                wqb_sb[:, wb + 128:wb + 192],
                                chunk_ap(qcT, TOK, c, half * 512, 512),
                                start=(c == 0), stop=(c == 11))
                    mm4.inc(last)

            @blk.vector
            def _(v):
                def qcast(idx):
                    hg_ = idx // 12
                    v.wait_ge(w_ld4[idx % 4].h, 16 * (idx // 4 + 1))
                    if idx % 12 == 0 and hg_ >= 2:
                        v.wait_ge(mm4.h, 4 * (hg_ - 1))
                    w_cst4.inc(v.tensor_copy(
                        wqb_sb[:, (hg_ % 2) * 9216 + (idx % 12) * 768:
                               (hg_ % 2) * 9216 + (idx % 12 + 1) * 768],
                        wqb_raw[:, (idx % 4) * 768:(idx % 4 + 1) * 768]))

                def qrope(h):
                    par = h % 2
                    v.wait_ge(mm4.h, h + 1)
                    if h >= 2:
                        v.wait_ge(stQ[h % 2].h, 32 * (h // 2))
                    v.tensor_copy(qhi_bf[:, par * TOK:(par + 1) * TOK],
                                  phi[:, par * TOK:(par + 1) * TOK])
                    pd = ((ev3c, 1) if do_kv else None) if h == 0 else (ev4, h)
                    ev4.inc(emit_rope_fm(
                        v, qlo_bf[:, par * TOK:(par + 1) * TOK],
                        plo[:, par * TOK:(par + 1) * TOK],
                        cos_sb, sin_sb, rope_tmp, rope_tmp2, TOK, rope_chain,
                        prev_done=pd))

                for hg in range(8):
                    for c in range(12):
                        qcast(hg * 12 + c)
                    if hg >= 1:
                        for hh in range(4):
                            qrope((hg - 1) * 4 + hh)
                for hh in range(4):
                    qrope(7 * 4 + hh)

            @blk.scalar
            def _(s):
                for h in range(NH):
                    par = h % 2
                    s.wait_ge(ev4.h, h + 1)
                    stQ[h % 2].inc(s.dma_start(out=Qf[h * QK:h * QK + ND, :],
                                               in_=qhi_bf[:, par * TOK:(par + 1) * TOK]), 16)
                    stQ[h % 2].inc(s.dma_start(out=Qf[h * QK + ND:(h + 1) * QK, :],
                                               in_=qlo_bf[:, par * TOK:(par + 1) * TOK]), 16)

    if do_kv:
        kv_ld = [p.sem(), p.sem()]
        kv_cst = p.sem()
        mm5 = p.sem()
        ev5 = p.sem()
        mm5v = p.sem()
        ev5v = p.sem()
        stKN = new_store(2)
        stV = new_store(2)
        _es5 = ExitStack()
        if wkvb_pre is None:
            wkvb_sb = _es5.enter_context(nc.sbuf_tensor(
                "wkvb_sb", [128, 4 * 8192], BF16, allow_name_mangling=True))
            wkvb_raw = _es5.enter_context(nc.sbuf_tensor(
                "wkvb_raw", [128, 2 * 4096], F32, allow_name_mangling=True))
        else:
            wkvb_sb = wkvb_pre["buf"]
        with nc.sbuf_tensor("kn_bf", [128, 2 * TOK], BF16, allow_name_mangling=True) as kn_bf, \
             nc.sbuf_tensor("v_bf", [128, 2 * NH * VD], BF16, allow_name_mangling=True) as v_bf, \
             nc.psum_tensor("pkn", [128, 2 * TOK], allow_name_mangling=True) as pkn, \
             nc.psum_tensor("pv", [128, 2 * 512], allow_name_mangling=True) as pv, \
             nc.Block() as blk:
            if wkvb_pre is None:
                @blk.sync
                def _(s):
                    if do_q:
                        s.wait_ge(mm4.h, NH)
                        s.wait_ge(stQ[0].h, stQ[0].v)
                        s.wait_ge(stQ[1].h, stQ[1].v)
                    else:
                        s.wait_ge(evT_k.h, evT_k.v)
                        s.wait_ge(stKR.h, 16)
                    for idx in range(8):
                        r, hf = divmod(idx, 2)
                        if idx >= 2:
                            s.wait_ge(kv_cst.h, idx - 1)
                        kv_ld[idx % 2].inc(s.dma_start(
                            out=wkvb_raw[:, (idx % 2) * 4096:(idx % 2 + 1) * 4096],
                            in_=w_kvb[r * 128:(r + 1) * 128,
                                      hf * 4096:(hf + 1) * 4096]), 16)

            @blk.tensor
            def _(t):
                if wkvb_pre is None:
                    t.wait_ge(kv_cst.h, 8)
                elif wkvb_pre["mode"] == "load":
                    t.wait_ge(kvp_ld.h, 16 * 4)
                if do_q:
                    t.wait_ge(ev4.h, NH)
                else:
                    t.wait_ge(ev3c.h, 1)
                for h in range(NH):
                    par = h % 2
                    if h >= 2:
                        t.wait_ge(ev5.h, h - 1)
                    last = None
                    for r in range(4):
                        for half in range(2):
                            last = t.matmul(
                                pkn[:, par * TOK + half * 512:par * TOK + (half + 1) * 512],
                                wkvb_sb[:, r * 8192 + h * 256:r * 8192 + h * 256 + 128],
                                chunk_ap(kcT, TOK, r, half * 512, 512),
                                start=(r == 0), stop=(r == 3))
                    mm5.inc(last)
                for tt in range(NT):
                    for hg in range(8):
                        idx = tt * 8 + hg
                        par = idx % 2
                        if idx >= 2:
                            t.wait_ge(ev5v.h, idx - 1)
                        last = None
                        for r in range(4):
                            rhs = wkvb_sb[:, :].rearrange(
                                "p (r h c) -> p r h c", r=4, h=NH)[
                                :, r, hg * 4:(hg + 1) * 4, 128:256]
                            last = t.matmul(
                                pv[:, par * 512:(par + 1) * 512],
                                chunk_ap(kcT, TOK, r, tt * 128, 128),
                                rhs, start=(r == 0), stop=(r == 3))
                        mm5v.inc(last)

            @blk.vector
            def _(v):
                if wkvb_pre is not None:
                    v.wait_ge(stKR.h, 16)
                    v.wait_ge(evT_k.h, evT_k.v)
                if wkvb_pre is None:
                    for idx in range(8):
                        r, hf = divmod(idx, 2)
                        v.wait_ge(kv_ld[idx % 2].h, 16 * (idx // 2 + 1))
                        kv_cst.inc(v.tensor_copy(
                            wkvb_sb[:, r * 8192 + hf * 4096:
                                    r * 8192 + (hf + 1) * 4096],
                            wkvb_raw[:, (idx % 2) * 4096:(idx % 2 + 1) * 4096]))
                for h in range(NH):
                    par = h % 2
                    v.wait_ge(mm5.h, h + 1)
                    if h >= 2:
                        v.wait_ge(stKN[h % 2].h, 16 * (h // 2))
                    ev5.inc(v.tensor_copy(kn_bf[:, par * TOK:(par + 1) * TOK],
                                          pkn[:, par * TOK:(par + 1) * TOK]))
                for tt in range(NT):
                    for hg in range(8):
                        idx = tt * 8 + hg
                        par = idx % 2
                        v.wait_ge(mm5v.h, idx + 1)
                        if hg == 0 and tt >= 2:
                            v.wait_ge(stV[tt % 2].h, 16 * (tt // 2))
                        ev5v.inc(v.tensor_copy(
                            v_bf[:, (tt % 2) * NH * VD + hg * 512:(tt % 2) * NH * VD + (hg + 1) * 512],
                            pv[:, par * 512:(par + 1) * 512]))

            @blk.scalar
            def _(s):
                for h in range(NH):
                    par = h % 2
                    s.wait_ge(ev5.h, h + 1)
                    stKN[h % 2].inc(s.dma_start(out=KN[h * ND:(h + 1) * ND, :],
                                                in_=kn_bf[:, par * TOK:(par + 1) * TOK]), 16)
                for tt in range(NT):
                    s.wait_ge(ev5v.h, (tt + 1) * 8)
                    stV[tt % 2].inc(s.dma_start(out=Vt[tt * 128:(tt + 1) * 128, :],
                                                in_=v_bf[:, (tt % 2) * NH * VD:(tt % 2 + 1) * NH * VD]), 16)

        _es5.close()

    if do_bm:
        ew = []
        if do_kv:
            ew = [(mm5v, NT * 8), (ev5v, NT * 8)] + \
                [(cs, cs.v) for cs in (stKN + stV)]
        all_stores.extend(emit_bm(p, nc, mask_rows, Gmat_d, bmp, bm_nch,
                                  entry_waits=ew))

    _es.close()
    return all_stores


def build_launch_A():
    p = Prog()
    io = dict(
        xT=p.inp("xT", [D, TOK]),
        w_qa=p.inp("w_qa", [D, QR]),
        g_qa=p.inp("g_qa", [1, QR]),
        w_qb=p.inp("w_qb", [QR, NH * QK]),
        w_kva=p.inp("w_kva", [D, KVR + RD]),
        g_kva=p.inp("g_kva", [1, KVR]),
        w_kvb=p.inp("w_kvb", [KVR, NH * (ND + VD)]),
        cos_own=p.inp("cos_own", [RD, TOK]),
        sin_own=p.inp("sin_own", [RD, TOK]),
        mask_rows=p.inp("mask_rows", [512, D], I32),
        Gmat=p.inp("Gmat", [128, 4]),
        ident_bf=p.inp("ident_bf", [128, 128], BF16),
        ident_f32=p.inp("ident_f32", [128, 128]),
        ones_f32=p.inp("ones_f32", [1, 128]),
        Qf=p.outp("Qf", [NH * QK, TOK], BF16),
        KN=p.outp("KN", [NH * ND, TOK], BF16),
        KR=p.outp("KR", [RD, TOK], BF16),
        Vt=p.outp("Vt", [TOK, NH * VD], BF16),
        bmp=p.outp("bmp", [16, 128]),
    )
    all_stores = emit_launch_A(p, io)
    with p.nc.Block() as blk:
        @blk.sync
        def _(s):
            for cs in all_stores:
                if cs.v:
                    s.wait_ge(cs.h, cs.v)
    return p


def emit_launch_B(p, io):
    nc = p.nc
    _es = ExitStack()
    def _sbl(name, shape, dt=F32):
        return _es.enter_context(
            nc.sbuf_tensor(name, shape, dt, allow_name_mangling=True))
    Qf = io["Qf"]
    KN_b = io["KN_b"]
    KR_b = io["KR_b"]
    Vt_b = io["Vt_b"]
    ident_bf_d = io["ident_bf"]
    attn = io["attn"]

    ident_bf = _sbl("b_ident", [128, 128], BF16)
    KR_sb = _sbl("b_KR", [RD, S], BF16)
    attn_sb = _sbl("b_attn", [128, NT * NH * VD], BF16)
    qhi = _sbl("b_qhi", [128, 2 * TOK], BF16)
    qlo = _sbl("b_qlo", [64, 2 * TOK], BF16)
    KN_h = _sbl("b_KN", [128, 2 * S], BF16)
    V_h = _sbl("b_V", [128, 2 * S], BF16)
    p_sb = _sbl("b_p", [128, S], BF16)
    pT_sb = _sbl("b_pT", [128, S], BF16)
    m_col = _sbl("b_m", [128, 1])
    bias_col = _sbl("b_bias", [128, 1])
    s8 = _sbl("b_s8", [128, 8])
    S_col = _sbl("b_S", [128, 2])
    tmpO = _sbl("b_tmpO", [128, VD])

    c_ld = p.sem()
    qld = [p.sem(), p.sem()]
    mmq = p.sem()
    dv = p.sem()
    ex = p.sem()
    tp_b = p.sem()
    evp = p.sem()
    mm_av = p.sem()
    dn = p.sem()
    st_at = p.sem()

    ps = _es.enter_context(nc.psum_tensor(
        "b_ps", [128, 4096], F32, allow_name_mangling=True))
    ptr_v = ps[:, 512:1536].bitcast(BF16)
    po = ps[:, 0:VD]

    NIT = NH * NT

    with nc.Block() as blk:
        @blk.sync
        def _(s):
            c_ld.inc(s.dma_start(out=ident_bf[:], in_=ident_bf_d), 16)
            c_ld.inc(s.dma_start(out=KR_sb[:], in_=KR_b), 16)

        @blk.gpsimd
        def _(g):
            for h in range(NH):
                par = h % 2
                if h >= 2:
                    g.wait_ge(mm_av.h, (h - 1) * NT)
                qld[par].inc(g.dma_start(
                    out=qhi[:, par * TOK:(par + 1) * TOK],
                    in_=Qf[h * QK:h * QK + ND, :]), 16)
                qld[par].inc(g.dma_start(
                    out=qlo[:, par * TOK:(par + 1) * TOK],
                    in_=Qf[h * QK + ND:(h + 1) * QK, :]), 16)
                qld[par].inc(g.dma_start(
                    out=KN_h[:, par * S:(par + 1) * S],
                    in_=KN_b[h * ND:(h + 1) * ND, :]), 16)
                vsrc = Vt_b.rearrange("(kt p) c -> p kt c", p=128)
                for j in range(8):
                    qld[par].inc(g.dma_start(
                        out=V_h[:, par * S + j * 512:par * S + (j + 1) * 512]
                            .rearrange("p (kt v) -> p kt v", kt=4),
                        in_=vsrc[:, 4 * j:4 * (j + 1), h * VD:(h + 1) * VD]), 16)

        @blk.tensor
        def _(t):
            t.wait_ge(c_ld.h, 32)
            for it in range(NIT):
                h, qt = divmod(it, NT)
                par = h % 2
                if qt == 0:
                    t.wait_ge(qld[par].h, 16 * 11 * (h // 2 + 1))
                if it >= 1:
                    t.wait_ge(ex.h, 8 * it)
                    t.wait_ge(evp.h, 4 * it)
                    t.wait_ge(dn.h, 3 * it)
                for c4 in range(8):
                    t.matmul(ps[:, c4 * 512:(c4 + 1) * 512],
                             qhi[:, par * TOK + qt * 128:par * TOK + (qt + 1) * 128],
                             KN_h[:, par * S + c4 * 512:par * S + (c4 + 1) * 512],
                             start=True, stop=False)
                    last = t.matmul(ps[:, c4 * 512:(c4 + 1) * 512],
                                    qlo[:, par * TOK + qt * 128:par * TOK + (qt + 1) * 128],
                                    KR_sb[:, c4 * 512:(c4 + 1) * 512],
                                    start=False, stop=True)
                mmq.inc(last)
                t.wait_ge(ex.h, 8 * (it + 1))
                for b in range(4):
                    if b >= 2:
                        t.wait_ge(evp.h, 4 * it + b - 1)
                    po_off = (b % 2) * 1024
                    for c8 in range(8):
                        kt = b * 8 + c8
                        last = t.transpose(
                            ptr_v[:, po_off + c8 * 128:po_off + (c8 + 1) * 128],
                            p_sb[:, kt * 128:(kt + 1) * 128],
                            ident_bf[:])
                    tp_b.inc(last)
                for b in range(4):
                    t.wait_ge(evp.h, 4 * it + b + 1)
                    for c8 in range(8):
                        kt = b * 8 + c8
                        last = t.matmul(
                            po, pT_sb[:, kt * 128:(kt + 1) * 128],
                            V_h[:, par * S + kt * 128:par * S + (kt + 1) * 128],
                            start=(kt == 0), stop=(kt == 31))
                mm_av.inc(last)

        @blk.vector
        def _(v):
            for it in range(NIT):
                h, qt = divmod(it, NT)
                v.wait_ge(mmq.h, it + 1)
                if it >= 1:
                    v.wait_ge(dv.h, 2 * it)
                dv.inc(v.tensor_reduce(m_col[:], ps[:, :], AX.X, ALU.max,
                                       negate=True))
                v.wait_ge(dv.h, 2 * it + 1)
                dv.inc(v.tensor_scalar_mul(bias_col[:], m_col[:], SM_SCALE))
                if it >= 1:
                    v.wait_ge(mm_av.h, it)
                for b in range(4):
                    v.wait_ge(tp_b.h, 4 * it + b + 1)
                    po_off = (b % 2) * 1024
                    evp.inc(v.tensor_copy(
                        pT_sb[:, b * 1024:(b + 1) * 1024],
                        ptr_v[:, po_off:po_off + 1024]))
                v.wait_ge(ex.h, 8 * (it + 1))
                if it >= 1:
                    v.wait_ge(dn.h, 3 * it - 1)
                dn.inc(v.tensor_reduce(S_col[:, 0:1], s8[:], AX.X, ALU.add))
                v.wait_ge(dn.h, 3 * it + 1)
                dn.inc(v.reciprocal(S_col[:, 1:2], S_col[:, 0:1]))
                v.wait_ge(dn.h, 3 * it + 2)
                v.wait_ge(mm_av.h, it + 1)
                dn.inc(v.tensor_scalar_mul(
                    attn_sb[:, qt * NH * VD + h * VD:qt * NH * VD + (h + 1) * VD],
                    po, S_col[:, 1:2]))

        @blk.scalar
        def _(s):
            for it in range(NIT):
                h, qt = divmod(it, NT)
                s.wait_ge(dv.h, 2 * (it + 1))
                if it >= 1:
                    s.wait_ge(tp_b.h, 4 * it)
                    s.wait_ge(dn.h, 3 * it - 2)
                for c4 in range(8):
                    ex.inc(s.activation(
                        p_sb[:, c4 * 512:(c4 + 1) * 512],
                        ps[:, c4 * 512:(c4 + 1) * 512],
                        AF.Exp, bias=bias_col[:], scale=SM_SCALE,
                        accum_out=s8[:, c4:c4 + 1]))

        @blk.vector
        def _(v):
            v.wait_ge(dn.h, 3 * NIT)
            v.engine_nop()

        @blk.scalar
        def _(s):
            s.wait_ge(dn.h, 3 * NIT)
            for qt in range(NT):
                st_at.inc(s.dma_start(
                    out=attn[qt * 128:(qt + 1) * 128, :],
                    in_=attn_sb[:, qt * NH * VD:(qt + 1) * NH * VD]), 16)

    _es.close()
    return [st_at]


def emit_launch_B2(p, io):
    nc = p.nc
    _es = ExitStack()

    def _sbl(name, shape, dt=F32):
        return _es.enter_context(
            nc.sbuf_tensor(name, shape, dt, allow_name_mangling=True))

    Qf = io["Qf"]
    KN_b = io["KN_b"]
    KR_b = io["KR_b"]
    Vt_b = io["Vt_b"]
    ident_bf_d = io["ident_bf"]
    attn = io["attn"]

    ident_bf = _sbl("b2_ident", [128, 128], BF16)
    KR_sb = _sbl("b2_KR", [RD, S], BF16)
    attn_sb = _sbl("b2_attn", [128, NT * NH * VD], BF16)
    qhi = _sbl("b2_qhi", [128, 2 * TOK], BF16)
    qlo = _sbl("b2_qlo", [64, 2 * TOK], BF16)
    KN_h = _sbl("b2_KN", [128, 2 * S], BF16)
    V_h = _sbl("b2_V", [128, 2 * S], BF16)
    p_sb = _sbl("b2_p", [128, S], BF16)
    pT_sb = _sbl("b2_pT", [128, S], BF16)
    m8 = _sbl("b2_m8", [128, 8])
    mcols = _sbl("b2_m", [128, 4])
    wcol = _sbl("b2_w", [128, 2])
    bias2 = _sbl("b2_bias", [128, 2])
    s8 = _sbl("b2_s8", [128, 8])
    fcol = _sbl("b2_f", [128, 2])
    dcol = _sbl("b2_d", [128, 2])
    scol = _sbl("b2_s", [128, 2])
    Sc = _sbl("b2_S", [128, 2])
    negM = _sbl("b2_negM", [128, 1])
    tmpO = _sbl("b2_t0", [128, VD])
    tmpO2 = _sbl("b2_t1", [128, VD])

    c_ld = p.sem()
    qld = [p.sem(), p.sem()]
    mmq = p.sem()
    dvr = p.sem()
    dvt = p.sem()
    ex = p.sem()
    fx = p.sem()
    tpb = p.sem()
    evp = p.sem()
    mm_av = p.sem()
    dn = p.sem()
    st_at = p.sem()

    ps4 = _es.enter_context(nc.psum_tensor(
        "b2_ps", [128, 2048], F32, allow_name_mangling=True))
    ptr2 = _es.enter_context(nc.psum_tensor(
        "b2_ptr", [128, 2048], BF16, allow_name_mangling=True))
    poA = _es.enter_context(nc.psum_tensor(
        "b2_poA", [128, VD], F32, allow_name_mangling=True))
    poB = _es.enter_context(nc.psum_tensor(
        "b2_poB", [128, VD], F32, allow_name_mangling=True))
    po = [poA, poB]

    NIT = NH * NT
    NRD = NIT * 2

    with nc.Block() as blk:
        @blk.sync
        def _(s):
            c_ld.inc(s.dma_start(out=ident_bf[:], in_=ident_bf_d), 16)
            c_ld.inc(s.dma_start(out=KR_sb[:], in_=KR_b), 16)

        @blk.gpsimd
        def _(g):
            for h in range(NH):
                par = h % 2
                if h >= 2:
                    g.wait_ge(mm_av.h, (h - 1) * NT * 2)
                qld[par].inc(g.dma_start(
                    out=qhi[:, par * TOK:(par + 1) * TOK],
                    in_=Qf[h * QK:h * QK + ND, :]), 16)
                qld[par].inc(g.dma_start(
                    out=qlo[:, par * TOK:(par + 1) * TOK],
                    in_=Qf[h * QK + ND:(h + 1) * QK, :]), 16)
                qld[par].inc(g.dma_start(
                    out=KN_h[:, par * S:(par + 1) * S],
                    in_=KN_b[h * ND:(h + 1) * ND, :]), 16)
                vsrc = Vt_b.rearrange("(kt p) c -> p kt c", p=128)
                for j in range(8):
                    qld[par].inc(g.dma_start(
                        out=V_h[:, par * S + j * 512:par * S + (j + 1) * 512]
                            .rearrange("p (kt v) -> p kt v", kt=4),
                        in_=vsrc[:, 4 * j:4 * (j + 1), h * VD:(h + 1) * VD]), 16)

        def emit_T(t, rd):
            it, r = divmod(rd, 2)
            for b in range(2):
                gb = 2 * rd + b
                if gb >= 2:
                    t.wait_ge(evp.h, gb - 1)
                t.wait_ge(ex.h, 4 * rd + 2 * (b + 1))
                po_off = (gb % 2) * 1024
                last = None
                for c8 in range(8):
                    kt = b * 8 + c8
                    last = t.transpose(
                        ptr2[:, po_off + c8 * 128:po_off + (c8 + 1) * 128],
                        p_sb[:, r * 2048 + kt * 128:r * 2048 + (kt + 1) * 128],
                        ident_bf[:])
                tpb.inc(last)

        def emit_AV(t, rd):
            it, r = divmod(rd, 2)
            h, qt = divmod(it, NT)
            par = h % 2
            t.wait_ge(dn.h, 10 * it)
            last = None
            for b in range(2):
                t.wait_ge(evp.h, 2 * rd + b + 1)
                for c8 in range(8):
                    kt = b * 8 + c8
                    last = t.matmul(
                        po[r][:],
                        pT_sb[:, r * 2048 + kt * 128:r * 2048 + (kt + 1) * 128],
                        V_h[:, par * S + r * 2048 + kt * 128:
                            par * S + r * 2048 + (kt + 1) * 128],
                        start=(kt == 0), stop=(kt == 15))
            mm_av.inc(last)

        @blk.tensor
        def _(t):
            t.wait_ge(c_ld.h, 32)
            for rd in range(NRD):
                it, r = divmod(rd, 2)
                h, qt = divmod(it, NT)
                par = h % 2
                if r == 0 and qt == 0:
                    t.wait_ge(qld[par].h, 16 * 11 * (h // 2 + 1))
                for c4 in range(4):
                    if rd >= 1:
                        t.wait_ge(ex.h, 4 * (rd - 1) + c4 + 1)
                    ko = r * 2048 + c4 * 512
                    t.matmul(ps4[:, c4 * 512:(c4 + 1) * 512],
                             qhi[:, par * TOK + qt * 128:par * TOK + (qt + 1) * 128],
                             KN_h[:, par * S + ko:par * S + ko + 512],
                             start=True, stop=False)
                    last = t.matmul(ps4[:, c4 * 512:(c4 + 1) * 512],
                                    qlo[:, par * TOK + qt * 128:par * TOK + (qt + 1) * 128],
                                    KR_sb[:, ko:ko + 512],
                                    start=False, stop=True)
                    mmq.inc(last)
                if rd >= 1:
                    emit_T(t, rd - 1)
                    emit_AV(t, rd - 1)
            emit_T(t, NRD - 1)
            emit_AV(t, NRD - 1)

        @blk.vector
        def _(v):
            def emit_evacs(prd):
                it1, r1 = divmod(prd, 2)
                for b in range(2):
                    gb = 2 * prd + b
                    v.wait_ge(tpb.h, gb + 1)
                    if prd >= 2:
                        v.wait_ge(mm_av.h, prd - 1)
                    po_off = (gb % 2) * 1024
                    evp.inc(v.tensor_copy(
                        pT_sb[:, r1 * 2048 + b * 1024:r1 * 2048 + (b + 1) * 1024],
                        ptr2[:, po_off:po_off + 1024]))

            def part_negM(it):
                sl = (it % 2) * 2
                v.wait_ge(dvt.h, 4 * it + 4)
                if it >= 1:
                    v.wait_ge(dn.h, 10 * (it - 1) + 2)
                dn.inc(v.tensor_tensor(negM[:], mcols[:, sl:sl + 1],
                                       mcols[:, sl + 1:sl + 2], ALU.min))
                v.wait_ge(dn.h, 10 * it + 1)
                v.wait_ge(fx.h, it)
                dn.inc(v.tensor_scalar(dcol[:], mcols[:, sl:sl + 2],
                                       negM[:, 0:1], None, ALU.subtract))

            def part_scols(q):
                v.wait_ge(dn.h, 10 * q)
                v.wait_ge(ex.h, 8 * (q + 1))
                dn.inc(v.tensor_reduce(scol[:, 0:1], s8[:, 0:4], AX.X, ALU.add))
                v.wait_ge(dn.h, 10 * q + 3)
                dn.inc(v.tensor_reduce(scol[:, 1:2], s8[:, 4:8], AX.X, ALU.add))

            def part_wSrS(q):
                v.wait_ge(dn.h, 10 * q + 4)
                v.wait_ge(fx.h, q + 1)
                dn.inc(v.tensor_mul(wcol[:], fcol[:], scol[:]))
                v.wait_ge(dn.h, 10 * q + 5)
                dn.inc(v.tensor_reduce(Sc[:, 0:1], wcol[:], AX.X, ALU.add))
                v.wait_ge(dn.h, 10 * q + 6)
                dn.inc(v.reciprocal(Sc[:, 1:2], Sc[:, 0:1]))
                v.wait_ge(dn.h, 10 * q + 7)
                v.wait_ge(mm_av.h, 2 * q + 1)
                dn.inc(v.tensor_scalar_mul(tmpO[:], poA[:, :], fcol[:, 0:1]))

            def part_t1out(q):
                v.wait_ge(dn.h, 10 * q + 8)
                v.wait_ge(mm_av.h, 2 * q + 2)
                dn.inc(v.scalar_tensor_tensor(tmpO2[:], poB[:, :], fcol[:, 1:2],
                                              tmpO[:], ALU.mult, ALU.add))
                v.wait_ge(dn.h, 10 * q + 9)
                h, qt = divmod(q, NT)
                dn.inc(v.tensor_scalar_mul(
                    attn_sb[:, qt * NH * VD + h * VD:qt * NH * VD + (h + 1) * VD],
                    tmpO2[:], Sc[:, 1:2]))

            for rd in range(NRD):
                it, r = divmod(rd, 2)
                if r == 0 and it >= 1:
                    part_scols(it - 1)
                if r == 1 and it >= 1:
                    part_t1out(it - 1)
                for c4 in range(4):
                    v.wait_ge(mmq.h, 4 * rd + c4 + 1)
                    if rd >= 2:
                        v.wait_ge(dvt.h, 2 * (rd - 1))
                    dvr.inc(v.tensor_reduce(m8[:, r * 4 + c4:r * 4 + c4 + 1],
                                            ps4[:, c4 * 512:(c4 + 1) * 512],
                                            AX.X, ALU.max, negate=True))
                sl = (it % 2) * 2
                v.wait_ge(dvr.h, 4 * (rd + 1))
                if it >= 2:
                    v.wait_ge(dn.h, 10 * (it - 2) + 2)
                dvt.inc(v.tensor_reduce(mcols[:, sl + r:sl + r + 1],
                                        m8[:, r * 4:r * 4 + 4],
                                        AX.X, ALU.min))
                v.wait_ge(dvt.h, 2 * rd + 1)
                if rd >= 2:
                    v.wait_ge(ex.h, 4 * (rd - 1))
                dvt.inc(v.tensor_scalar_mul(bias2[:, r:r + 1],
                                            mcols[:, sl + r:sl + r + 1],
                                            SM_SCALE))
                if r == 1:
                    part_negM(it)
                if rd >= 1:
                    emit_evacs(rd - 1)
                if r == 0 and it >= 1:
                    part_wSrS(it - 1)
            emit_evacs(NRD - 1)
            part_scols(NIT - 1)
            part_wSrS(NIT - 1)
            part_t1out(NIT - 1)

        @blk.scalar
        def _(s):
            for rd in range(NRD):
                it, r = divmod(rd, 2)
                s.wait_ge(dvt.h, 2 * rd + 2)
                if rd >= 2:
                    s.wait_ge(tpb.h, 2 * (rd - 1))
                if it >= 1 and r == 0:
                    s.wait_ge(dn.h, 10 * (it - 1) + 4)
                for c4 in range(4):
                    ex.inc(s.activation(
                        p_sb[:, r * 2048 + c4 * 512:r * 2048 + (c4 + 1) * 512],
                        ps4[:, c4 * 512:(c4 + 1) * 512],
                        AF.Exp, bias=bias2[:, r:r + 1], scale=SM_SCALE,
                        accum_out=s8[:, r * 4 + c4:r * 4 + c4 + 1]))
                if r == 1:
                    it_ = it
                    s.wait_ge(dn.h, 10 * it_ + 2)
                    fx.inc(s.activation(fcol[:], dcol[:], AF.Exp,
                                        scale=-SM_SCALE))

        @blk.scalar
        def _(s):
            s.wait_ge(dn.h, 10 * NIT)
            for qt in range(NT):
                st_at.inc(s.dma_start(
                    out=attn[qt * 128:(qt + 1) * 128, :],
                    in_=attn_sb[:, qt * NH * VD:(qt + 1) * NH * VD]), 16)

    _es.close()
    return [st_at]


def build_launch_B():
    p = Prog()
    io = dict(
        Qf=p.inp("Qf", [NH * QK, TOK], BF16),
        KN_b=p.inp("KN_b", [NH * ND, S], BF16),
        KR_b=p.inp("KR_b", [RD, S], BF16),
        Vt_b=p.inp("Vt_b", [S, NH * VD], BF16),
        ident_bf=p.inp("ident_bf", [128, 128], BF16),
        attn=p.outp("attn", [TOK, NH * VD], BF16),
    )
    sts = emit_launch_B(p, io)
    with p.nc.Block() as blk:
        @blk.sync
        def _(s):
            for cs in sts:
                s.wait_ge(cs.h, cs.v)
    return p


def run_launch_B(resA, consts=None):
    consts = consts or host_consts()
    in_maps = []
    for c in range(NCORES):
        b = c // 4
        grp = range(b * 4, b * 4 + 4)
        KN_full = np.concatenate([resA[g]["KN"] for g in grp], axis=1)
        KR_full = np.concatenate([resA[g]["KR"] for g in grp], axis=1)
        Vt_full = np.concatenate([resA[g]["Vt"] for g in grp], axis=0)
        in_maps.append(dict(
            Qf=resA[c]["Qf"], KN_b=KN_full, KR_b=KR_full, Vt_b=Vt_full,
            ident_bf=consts["ident_bf"],
        ))
    pB = build_launch_B()
    res = run_bass_kernel_spmd(pB.nc, in_maps, core_ids=list(range(NCORES)))
    return res.results


def emit_loadT(p, nc, src_dram, width, dstT, ident_bf, cast, entry_waits=(),
               tag="lt"):
    ld = [p.sem(), p.sem()]
    tp = p.sem()
    ev = p.sem()
    nch = width // 128
    nb = (nch + 7) // 8
    with nc.sbuf_tensor(f"{tag}_buf", [128, 2 * width], BF16, allow_name_mangling=True) as buf, \
         nc.psum_tensor(f"{tag}_ptr", [128, 2048], BF16, allow_name_mangling=True) as ptr, \
         nc.Block() as blk:
        @blk.gpsimd
        def _(g):
            for ctr, val in entry_waits:
                g.wait_ge(ctr.h, val)
            for tt in range(NT):
                if tt >= 2:
                    g.wait_ge(tp.h, (tt - 1) * nb)
                ld[tt % 2].inc(g.dma_start(
                    out=buf[:, (tt % 2) * width:(tt % 2 + 1) * width],
                    in_=src_dram[tt * 128:(tt + 1) * 128, :]), 16)

        @blk.tensor
        def _(t):
            for ctr, val in entry_waits:
                t.wait_ge(ctr.h, val)
            for tt in range(NT):
                t.wait_ge(ld[tt % 2].h, 16 * (tt // 2 + 1))
                for b in range(nb):
                    n_in_b = min(8, nch - b * 8)
                    idx = tt * nb + b
                    if idx >= 2:
                        t.wait_ge(ev.h, idx - 1)
                    po = (idx % 2) * 1024
                    last = None
                    for c8 in range(n_in_b):
                        c = b * 8 + c8
                        last = t.transpose(
                            ptr[:, po + c8 * 128:po + (c8 + 1) * 128],
                            buf[:, (tt % 2) * width + c * 128:(tt % 2) * width + (c + 1) * 128],
                            ident_bf[:])
                    tp.inc(last)

        @blk.vector
        def _(v):
            for idx in range(NT * nb):
                tt, b = divmod(idx, nb)
                n_in_b = min(8, nch - b * 8)
                v.wait_ge(tp.h, idx + 1)
                po = (idx % 2) * 1024
                ev.inc(v.tensor_copy(chunks_ap(dstT, TOK, b * 8, tt, n_in_b),
                                     ptr[:, po:po + n_in_b * 128]))
    return ld, tp, ev


def emit_ln(p, nc, y_dram, ysum_dram, out_f32_dram, dstT, ident_bf,
            entry_waits=(), tag="ln"):
    yld = [p.sem(), p.sem()]
    sld = p.sem()
    sq_v = p.sem()
    sq_s = p.sem()
    dvm = p.sem()
    dvr = p.sem()
    evn = p.sem()
    evc = p.sem()
    tp = p.sem()
    evT = p.sem()
    st_f = [p.sem(), p.sem()]
    nch = D // 128
    nb = nch // 8
    with nc.sbuf_tensor(f"{tag}_y", [128, 2 * D], F32, allow_name_mangling=True) as ybuf, \
         nc.sbuf_tensor(f"{tag}_scr", [128, D], F32, allow_name_mangling=True) as scr, \
         nc.sbuf_tensor(f"{tag}_x", [128, D], F32, allow_name_mangling=True) as xbuf, \
         nc.sbuf_tensor(f"{tag}_xb", [128, D], BF16, allow_name_mangling=True) as xbbuf, \
         nc.sbuf_tensor(f"{tag}_ys", [128, 8], F32, allow_name_mangling=True) as ys, \
         nc.sbuf_tensor(f"{tag}_st", [128, 4 * 8], F32, allow_name_mangling=True) as stt, \
         nc.sbuf_tensor(f"{tag}_eps", [128, 1], F32, allow_name_mangling=True) as eps_col, \
         nc.psum_tensor(f"{tag}_ptr", [128, 2048], BF16, allow_name_mangling=True) as ptr, \
         nc.Block() as blk:
        @blk.sync
        def _(s):
            for ctr, val in entry_waits:
                s.wait_ge(ctr.h, val)
            sld.inc(s.dma_start(out=ys[:, :], in_=ysum_dram), 16)
            for tt in range(NT):
                if tt >= 2:
                    s.wait_ge(evn.h, tt - 1)
                yld[tt % 2].inc(s.dma_start(
                    out=ybuf[:, (tt % 2) * D:(tt % 2 + 1) * D],
                    in_=y_dram[tt * 128:(tt + 1) * 128, :]), 16)

        @blk.vector
        def _(v):
            v.wait_ge(sld.h, 16)
            dvm.inc(v.memset(eps_col[:, :], EPS_LN))
            for tt in range(NT):
                dvm.inc(v.tensor_scalar_mul(stt[:, tt * 4:tt * 4 + 1],
                                            ys[:, tt:tt + 1], -1.0 / D))

        @blk.scalar
        def _(s):
            def emit_store(tt):
                s.wait_ge(evn.h, tt + 1)
                if dstT is not None:
                    s.wait_ge(evc.h, tt + 1)
                st_f[tt % 2].inc(s.dma_start(
                    out=out_f32_dram[tt * 128:(tt + 1) * 128, :],
                    in_=xbuf[:, :]), 16)

            for tt in range(NT):
                s.wait_ge(dvm.h, tt + 2)
                s.wait_ge(yld[tt % 2].h, 16 * (tt // 2 + 1))
                if tt >= 1:
                    s.wait_ge(sq_v.h, tt)
                sq_v.inc(s.activation(scr[:, :],
                                      ybuf[:, (tt % 2) * D:(tt % 2 + 1) * D],
                                      AF.Square, bias=stt[:, tt * 4:tt * 4 + 1],
                                      accum_out=stt[:, tt * 4 + 1:tt * 4 + 2]))
                s.wait_ge(sq_v.h, tt + 1)
                sq_s.inc(s.activation(stt[:, tt * 4 + 2:tt * 4 + 3],
                                      stt[:, tt * 4 + 1:tt * 4 + 2],
                                      AF.Sqrt, bias=eps_col[:, :], scale=1.0 / D))
                if out_f32_dram is not None and tt >= 1:
                    emit_store(tt - 1)
            if out_f32_dram is not None:
                emit_store(NT - 1)

        @blk.vector
        def _(v):
            def emit_evacT(tt):
                for b in range(nb):
                    idx = tt * nb + b
                    v.wait_ge(tp.h, idx + 1)
                    po = (idx % 2) * 1024
                    evT.inc(v.tensor_copy(chunks_ap(dstT, TOK, b * 8, tt, 8),
                                          ptr[:, po:po + 1024]))

            for tt in range(NT):
                v.wait_ge(sq_s.h, tt + 1)
                dvr.inc(v.reciprocal(stt[:, tt * 4 + 3:tt * 4 + 4],
                                     stt[:, tt * 4 + 2:tt * 4 + 3]))
                v.wait_ge(dvr.h, tt + 1)
                if tt >= 1:
                    if out_f32_dram is not None:
                        v.wait_ge(st_f[(tt - 1) % 2].h,
                                  16 * ((tt - 1) // 2 + 1))
                    if dstT is not None:
                        v.wait_ge(evc.h, tt)
                evn.inc(v.tensor_scalar(
                    xbuf[:, :],
                    ybuf[:, (tt % 2) * D:(tt % 2 + 1) * D],
                    stt[:, tt * 4:tt * 4 + 1], stt[:, tt * 4 + 3:tt * 4 + 4],
                    ALU.add, ALU.mult))
                if dstT is not None:
                    if tt >= 1:
                        emit_evacT(tt - 1)
                    v.wait_ge(evn.h, tt + 1)
                    if tt >= 1:
                        v.wait_ge(tp.h, tt * nb)
                    evc.inc(v.tensor_copy(xbbuf[:, :], xbuf[:, :]))
            if dstT is not None:
                emit_evacT(NT - 1)

        if dstT is not None:
            @blk.tensor
            def _(t):
                for ctr, val in entry_waits:
                    t.wait_ge(ctr.h, val)
                for tt in range(NT):
                    t.wait_ge(evc.h, tt + 1)
                    for b in range(nb):
                        idx = tt * nb + b
                        if idx >= 2:
                            t.wait_ge(evT.h, idx - 1)
                        po = (idx % 2) * 1024
                        last = None
                        for c8 in range(8):
                            c = b * 8 + c8
                            last = t.transpose(
                                ptr[:, po + c8 * 128:po + (c8 + 1) * 128],
                                xbbuf[:, c * 128:(c + 1) * 128],
                                ident_bf[:])
                        tp.inc(last)

    return evn, evT, st_f, tp


def emit_mm_res(p, nc, lhsT_cache, nkc, w_dram, dout, xres_dram, y_dram,
                ysum_dram, ones_bf, bias_sb=None, entry_waits=(),
                extra_mm=None, extra_waits=(), mask_fn=None, psum_ref=None,
                tag="mm"):
    w_ld = [p.sem() for _ in range(4)]
    x_ld = [p.sem() for _ in range(4)]
    w_fr = p.sem()
    msk = p.sem()
    mme = p.sem()
    eve = p.sem()
    dvs = p.sem()
    sty = [p.sem() for _ in range(4)]
    sts = p.sem()
    ndg = dout // 512
    has_tail = (bias_sb is not None) or (extra_mm is not None)
    with nc.sbuf_tensor(f"{tag}_w", [128, 4 * 512], BF16, allow_name_mangling=True) as wbuf, \
         nc.sbuf_tensor(f"{tag}_wr", [128, 4 * 512], F32, allow_name_mangling=True) as wraw, \
         nc.sbuf_tensor(f"{tag}_xr", [128, 4 * 512], F32, allow_name_mangling=True) as xrbuf, \
         nc.sbuf_tensor(f"{tag}_yb", [128, 4 * 512], F32, allow_name_mangling=True) as ybuf, \
         nc.sbuf_tensor(f"{tag}_sp", [128, NT * 8], F32, allow_name_mangling=True) as spart, \
         nc.sbuf_tensor(f"{tag}_ys", [128, 8], F32, allow_name_mangling=True) as ysb, \
         nc.psum_tensor(f"{tag}_pq", [128, 4096], F32, allow_name_mangling=True) as pq, \
         nc.Block() as blk:
        if psum_ref is not None:
            psum_ref[0] = pq

        @blk.sync
        def _(s):
            for ctr, val in entry_waits:
                s.wait_ge(ctr.h, val)
            for dg in range(ndg):
                for kc in range(nkc):
                    idx = dg * nkc + kc
                    if idx >= 4:
                        s.wait_ge(msk.h, idx - 3)
                    w_ld[idx % 4].inc(s.dma_start(
                        out=wraw[:, (idx % 4) * 512:(idx % 4 + 1) * 512],
                        in_=w_dram[kc * 128:(kc + 1) * 128,
                                   dg * 512:(dg + 1) * 512]), 16)


        @blk.tensor
        def _(t):
            for ctr, val in entry_waits:
                t.wait_ge(ctr.h, val)
            for ctr, val in extra_waits:
                t.wait_ge(ctr.h, val)
            for dg in range(ndg):
                if dg > 0:
                    t.wait_ge(eve.h, dg * NT)
                for kc in range(nkc):
                    idx = dg * nkc + kc
                    t.wait_ge(msk.h, idx + 1)
                    last = None
                    for tt in range(NT):
                        last = t.matmul(
                            pq[:, tt * 512:(tt + 1) * 512],
                            chunk_ap(lhsT_cache, TOK, kc, tt * 128, 128),
                            wbuf[:, (idx % 4) * 512:(idx % 4 + 1) * 512],
                            start=(kc == 0),
                            stop=(kc == nkc - 1 and not has_tail))
                    w_fr.inc(last)
                if has_tail:
                    last = None
                    for tt in range(NT):
                        if bias_sb is not None:
                            last = t.matmul(
                                pq[:, tt * 512:(tt + 1) * 512],
                                ones_bf[0:1, 0:128],
                                bias_sb[0:1, dg * 512:(dg + 1) * 512],
                                start=False, stop=(extra_mm is None))
                        if extra_mm is not None:
                            last = extra_mm(t, dg, tt)
                    mme.inc(last)

        @blk.vector
        def _(v):
            def emit_mask(dg, kc):
                idx = dg * nkc + kc
                v.wait_ge(w_ld[idx % 4].h, 16 * (idx // 4 + 1))
                if idx >= 4:
                    v.wait_ge(w_fr.h, idx - 3)
                if mask_fn is not None:
                    msk.inc(v.tensor_tensor(
                        wbuf[:, (idx % 4) * 512:(idx % 4 + 1) * 512],
                        wraw[:, (idx % 4) * 512:(idx % 4 + 1) * 512],
                        mask_fn(dg, kc), ALU.mult))
                else:
                    msk.inc(v.tensor_copy(
                        wbuf[:, (idx % 4) * 512:(idx % 4 + 1) * 512],
                        wraw[:, (idx % 4) * 512:(idx % 4 + 1) * 512]))

            def emit_evac(dg):
                for tt in range(NT):
                    idx = dg * NT + tt
                    v.wait_ge(w_fr.h, (dg + 1) * nkc)
                    if has_tail:
                        v.wait_ge(mme.h, dg + 1)
                    v.wait_ge(x_ld[idx % 4].h, 16 * (idx // 4 + 1))
                    if idx >= 4:
                        v.wait_ge(sty[idx % 4].h, 16 * (idx // 4))
                    eve.inc(v.scalar_tensor_tensor(
                        ybuf[:, (idx % 4) * 512:(idx % 4 + 1) * 512],
                        pq[:, tt * 512:(tt + 1) * 512], 1.0,
                        xrbuf[:, (idx % 4) * 512:(idx % 4 + 1) * 512],
                        ALU.mult, ALU.add,
                        accum_out=spart[:, tt * 8 + dg:tt * 8 + dg + 1]))

            for dg in range(ndg):
                emit_mask(dg, 0)
                emit_mask(dg, 1)
                if dg >= 1:
                    emit_evac(dg - 1)
                for kc in range(2, nkc):
                    emit_mask(dg, kc)
            emit_evac(ndg - 1)
            for tt in range(NT):
                v.wait_ge(eve.h, (ndg - 1) * NT + tt + 1)
                dvs.inc(v.tensor_reduce(ysb[:, tt:tt + 1],
                                        spart[:, tt * 8:tt * 8 + ndg],
                                        AX.X, ALU.add))

        @blk.scalar
        def _(s):
            def xload(idx):
                dg, tt = divmod(idx, NT)
                x_ld[idx % 4].inc(s.dma_start(
                    out=xrbuf[:, (idx % 4) * 512:(idx % 4 + 1) * 512],
                    in_=xres_dram[tt * 128:(tt + 1) * 128,
                                  dg * 512:(dg + 1) * 512]), 16)
            for ctr, val in entry_waits:
                s.wait_ge(ctr.h, val)
            for j in range(4):
                xload(j)
            for dg in range(ndg):
                for tt in range(NT):
                    idx = dg * NT + tt
                    s.wait_ge(eve.h, idx + 1)
                    sty[idx % 4].inc(s.dma_start(
                        out=y_dram[tt * 128:(tt + 1) * 128,
                                   dg * 512:(dg + 1) * 512],
                        in_=ybuf[:, (idx % 4) * 512:(idx % 4 + 1) * 512]), 16)
                    if idx + 4 < ndg * NT:
                        xload(idx + 4)
            s.wait_ge(dvs.h, NT)
            sts.inc(s.dma_start(out=ysum_dram, in_=ysb[:, :]), 16)
    return w_fr, mme, eve, sty, sts


def run_launch_A(inputs, consts=None):
    consts = consts or host_consts()
    x_full = inputs["L0_attn_hidden_states"]
    mask = inputs["L1_attn_mask"]
    in_maps = []
    for c in range(NCORES):
        b, off = divmod(c, 4)
        pos0 = off * TOK
        in_maps.append(dict(
            xT=np.ascontiguousarray(x_full[b, pos0:pos0 + TOK, :].T),
            w_qa=inputs["L0_attn_w_qa"],
            g_qa=inputs["L0_attn_g_qa"].reshape(1, -1),
            w_qb=inputs["L0_attn_w_qb"],
            w_kva=inputs["L0_attn_w_kva"],
            g_kva=inputs["L0_attn_g_kva"].reshape(1, -1),
            w_kvb=inputs["L0_attn_w_kvb"],
            cos_own=np.ascontiguousarray(consts["cos_fm"][:, pos0:pos0 + TOK]),
            sin_own=np.ascontiguousarray(consts["sin_fm"][:, pos0:pos0 + TOK]),
            mask_rows=np.ascontiguousarray(mask[c * 512:(c + 1) * 512, :]),
            Gmat=consts["Gmat"],
            ident_bf=consts["ident_bf"],
            ident_f32=consts["ident_f32"],
            ones_f32=consts["ones_f32"],
        ))
    pA = build_launch_A()
    res = run_bass_kernel_spmd(pA.nc, in_maps, core_ids=list(range(NCORES)))
    return res.results



def emit_launch_C1(p, io):
    nc = p.nc
    _es = ExitStack()
    def _sbl(name, shape, dt=F32):
        return _es.enter_context(
            nc.sbuf_tensor(name, shape, dt, allow_name_mangling=True))
    attn_d = io["attn"]
    x_d = io["x"]
    w_o = io["w_o"]
    ident_bf_d = io["ident_bf"]
    y1 = io["y1"]
    ysum1 = io["ysum1"]

    ident_bf = _sbl("c1_ident", [128, 128], BF16)
    attnT = _sbl("c1_attnT", [128, 32 * TOK], BF16)
    c_ld = p.sem()
    with nc.Block() as blk:
        @blk.sync
        def _(s):
            c_ld.inc(s.dma_start(out=ident_bf[:], in_=ident_bf_d), 16)

    ldA, tpA, evA = emit_loadT(p, nc, attn_d, NH * VD, attnT, ident_bf, False,
                               entry_waits=[(c_ld, 16)], tag="c1lt")
    w_fr, mme, eve, sty, sts = emit_mm_res(
        p, nc, attnT, 32, w_o, D, x_d, y1, ysum1, None,
        entry_waits=[(evA, NT * 4)], tag="c1mm")
    _es.close()
    return sty + [sts]


def build_launch_C1():
    p = Prog()
    io = dict(
        attn=p.inp("attn", [TOK, NH * VD], BF16),
        x=p.inp("x", [TOK, D], F32),
        w_o=p.inp("w_o", [D, D], F32),
        ident_bf=p.inp("ident_bf", [128, 128], BF16),
        y1=p.outp("y1", [TOK, D], F32),
        ysum1=p.outp("ysum1", [128, NT], F32),
    )
    sts = emit_launch_C1(p, io)
    with p.nc.Block() as blk:
        @blk.sync
        def _(s):
            for cs in sts:
                s.wait_ge(cs.h, cs.v)
    return p


def run_launch_C1(inputs, resB, consts=None):
    consts = consts or host_consts()
    x_full = inputs["L0_attn_hidden_states"]
    in_maps = []
    for c in range(NCORES):
        b, off = divmod(c, 4)
        pos0 = off * TOK
        in_maps.append(dict(
            attn=resB[c]["attn"],
            x=np.ascontiguousarray(x_full[b, pos0:pos0 + TOK, :]),
            w_o=inputs["L0_attn_w_o"],
            ident_bf=consts["ident_bf"],
        ))
    pC = build_launch_C1()
    res = run_bass_kernel_spmd(pC.nc, in_maps, core_ids=list(range(NCORES)))
    return res.results


def emit_launch_C5(p, io):
    nc = p.nc
    evn, evT, st_f, tp = emit_ln(p, nc, io["y4"], io["ysum4"], io["out"],
                                 None, None, tag="c5ln")
    return st_f


def build_launch_C5():
    p = Prog()
    io = dict(
        y4=p.inp("y4", [TOK, D], F32),
        ysum4=p.inp("ysum4", [128, NT], F32),
        out=p.outp("out", [TOK, D], F32),
    )
    sts = emit_launch_C5(p, io)
    with p.nc.Block() as blk:
        @blk.sync
        def _(s):
            for cs in sts:
                if cs.v:
                    s.wait_ge(cs.h, cs.v)
    return p


def run_launch_C5(resC4):
    in_maps = [dict(y4=resC4[c]["y4"], ysum4=resC4[c]["ysum4"])
               for c in range(NCORES)]
    pC = build_launch_C5()
    res = run_bass_kernel_spmd(pC.nc, in_maps, core_ids=list(range(NCORES)))
    return res.results


def emit_launch_C2(p, io):
    nc = p.nc
    _es = ExitStack()
    def _sbl(name, shape, dt=F32):
        return _es.enter_context(
            nc.sbuf_tensor(name, shape, dt, allow_name_mangling=True))
    y1 = io["y1"]
    ysum1 = io["ysum1"]
    U1 = io["U1"]
    V1 = io["V1"]
    U2 = io["U2"]
    V2 = io["V2"]
    b1 = io["b1"]
    b2 = io["b2"]
    ident_bf_d = io["ident_bf"]
    ones_bf_d = io["ones_bf"]
    y2 = io["y2"]
    ysum2 = io["ysum2"]
    x1i = nc.dram_tensor("x1i", [TOK, D], F32, kind="Internal",
                         unsafe_allow_realloc=True).ap()

    ident_bf = _sbl("c2_ident", [128, 128], BF16)
    ones_bf = _sbl("c2_ones", [1, 512], BF16)
    b2_sb = _sbl("c2_b2", [1, D], BF16)
    PT = _sbl("c2_PT", [128, 8 * TOK], BF16)
    ZT = _sbl("c2_ZT", [128, 8 * TOK], BF16)
    c_ld = p.sem()
    c_ld2 = p.sem()
    with nc.Block() as blk:
        @blk.sync
        def _(s):
            c_ld.inc(s.dma_start(out=ident_bf[:], in_=ident_bf_d), 16)
            c_ld.inc(s.dma_start(out=ones_bf[:], in_=ones_bf_d), 16)

        @blk.gpsimd
        def _(g):
            c_ld2.inc(g.dma_start(out=b2_sb[:], in_=b2), 16)

    u_ld = [p.sem() for _ in range(4)]
    u_fr = p.sem()
    evP = p.sem()
    v_ld = [p.sem() for _ in range(4)]
    v_fr = p.sem()
    mmgu = p.sem()
    sgq = p.sem()
    eva = p.sem()
    u2_ld = [p.sem(), p.sem()]
    u2_fr = p.sem()
    evZ = p.sem()

    with nc.sbuf_tensor("c2_x1T", [128, 32 * TOK], BF16, allow_name_mangling=True) as x1T:
        evn1, evT1, stf1, _tp = emit_ln(p, nc, y1, ysum1, x1i, x1T, ident_bf,
                                        entry_waits=[(c_ld, 16 * 2), (c_ld2, 16)], tag="c2ln")

        u_cst = p.sem()
        with nc.sbuf_tensor("c2_u1w", [128, 4 * 512], BF16, allow_name_mangling=True) as u1w, \
             nc.sbuf_tensor("c2_u1r", [128, 4 * 512], F32, allow_name_mangling=True) as u1r, \
             nc.psum_tensor("c2_pp", [128, 4096], F32, allow_name_mangling=True) as pp, \
             nc.Block() as blk:
            @blk.sync
            def _(s):
                for rg in range(2):
                    for dc in range(32):
                        idx = rg * 32 + dc
                        if idx >= 4:
                            s.wait_ge(u_cst.h, idx - 3)
                        u_ld[idx % 4].inc(s.dma_start(
                            out=u1r[:, (idx % 4) * 512:(idx % 4 + 1) * 512],
                            in_=U1[dc * 128:(dc + 1) * 128,
                                   rg * 512:(rg + 1) * 512]), 16)

            @blk.tensor
            def _(t):
                t.wait_ge(evT1.h, evT1.v)
                for rg in range(2):
                    if rg > 0:
                        t.wait_ge(evP.h, 8)
                    for dc in range(32):
                        idx = rg * 32 + dc
                        t.wait_ge(u_cst.h, idx + 1)
                        last = None
                        for sub in range(8):
                            rc4, th = divmod(sub, 2)
                            last = t.matmul(
                                pp[:, sub * 512:(sub + 1) * 512],
                                u1w[:, (idx % 4) * 512 + rc4 * 128:
                                    (idx % 4) * 512 + (rc4 + 1) * 128],
                                chunk_ap(x1T, TOK, dc, th * 512, 512),
                                start=(dc == 0), stop=(dc == 31))
                        u_fr.inc(last)

            @blk.vector
            def _(v):
                def ucast(idx):
                    v.wait_ge(u_ld[idx % 4].h, 16 * (idx // 4 + 1))
                    if idx >= 4:
                        v.wait_ge(u_fr.h, idx - 3)
                    u_cst.inc(v.tensor_copy(
                        u1w[:, (idx % 4) * 512:(idx % 4 + 1) * 512],
                        u1r[:, (idx % 4) * 512:(idx % 4 + 1) * 512]))
                for idx in range(36):
                    ucast(idx)
                v.wait_ge(u_fr.h, 32)
                for sub in range(8):
                    rc4, th = divmod(sub, 2)
                    evP.inc(v.tensor_copy(
                        chunk_ap(PT, TOK, 0 * 4 + rc4, th * 512, 512),
                        pp[:, sub * 512:(sub + 1) * 512]))
                for idx in range(36, 64):
                    ucast(idx)
                v.wait_ge(u_fr.h, 64)
                for sub in range(8):
                    rc4, th = divmod(sub, 2)
                    evP.inc(v.tensor_copy(
                        chunk_ap(PT, TOK, 1 * 4 + rc4, th * 512, 512),
                        pp[:, sub * 512:(sub + 1) * 512]))

    JF = 22
    fgs = [(j * 512, min(512, INTER - j * 512)) for j in range(JF)]
    nf4s = [fw // 128 for (fo, fw) in fgs]
    FC_HALF = sum(nf4s)

    b1_ld = p.sem()
    vcst = p.sem()
    u2cst = p.sem()
    with nc.sbuf_tensor("c2_actT", [128, 86 * 512], BF16, allow_name_mangling=True) as actT, \
         nc.sbuf_tensor("c2_b1", [1, 2 * INTER], BF16, allow_name_mangling=True) as b1_sb, \
         nc.sbuf_tensor("c2_v1w", [128, 2 * 1024], BF16, allow_name_mangling=True) as v1w, \
         nc.sbuf_tensor("c2_v1r", [128, 4 * 1024], F32, allow_name_mangling=True) as v1r, \
         nc.sbuf_tensor("c2_sg", [128, 2 * 512], F32, allow_name_mangling=True) as sgbuf, \
         nc.sbuf_tensor("c2_u2w", [128, 2 * 1024], BF16, allow_name_mangling=True) as u2w, \
         nc.sbuf_tensor("c2_u2r", [128, 2 * 1024], F32, allow_name_mangling=True) as u2r, \
         nc.psum_tensor("c2_pgu", [128, 4096], F32, allow_name_mangling=True) as pgu, \
         nc.Block() as blk:
        @blk.gpsimd
        def _(g):
            g.wait_ge(evP.h, 16)
            for cs in stf1:
                g.wait_ge(cs.h, cs.v)
            b1_ld.inc(g.dma_start(out=b1_sb[:], in_=b1), 16)

        @blk.sync
        def _(s):
            s.wait_ge(evP.h, 16)
            for cs in stf1:
                s.wait_ge(cs.h, cs.v)
            for half in range(2):
                for j, (fo, fw) in enumerate(fgs):
                    for rc in range(8):
                        idx = (half * JF + j) * 8 + rc
                        if idx >= 4:
                            s.wait_ge(vcst.h, idx - 3)
                        g_src = V1[rc * 128:(rc + 1) * 128, :].rearrange(
                            "p (gu f) -> p gu f", gu=2)[:, :, fo:fo + fw]
                        v_ld[idx % 4].inc(s.dma_start(
                            out=v1r[:, (idx % 4) * 1024:(idx % 4) * 1024 + 2 * fw]
                                .rearrange("p (gu f) -> p gu f", gu=2),
                            in_=g_src), 16)
                for fc in range(86):
                    idx = half * 86 + fc
                    if idx >= 2:
                        s.wait_ge(u2cst.h, idx - 1)
                    u2_ld[idx % 2].inc(s.dma_start(
                        out=u2r[:, (idx % 2) * 1024:(idx % 2 + 1) * 1024],
                        in_=U2[fc * 128:(fc + 1) * 128, :]), 16)

        @blk.tensor
        def _(t):
            t.wait_ge(evP.h, 16)
            t.wait_ge(b1_ld.h, 16)
            for half in range(2):
                if half == 1:
                    t.wait_ge(evZ.h, 8)
                for j, (fo, fw) in enumerate(fgs):
                    nf4 = fw // 128
                    git = half * JF + j
                    done_fc4 = half * FC_HALF + sum(nf4s[:j])
                    if git > 0:
                        t.wait_ge(eva.h, done_fc4)
                        t.wait_ge(sgq.h, done_fc4)
                    for rc in range(8):
                        idx = git * 8 + rc
                        t.wait_ge(vcst.h, idx + 1)
                        last = None
                        for fc4 in range(nf4):
                            for gu in range(2):
                                last = t.matmul(
                                    pgu[:, (fc4 * 2 + gu) * 512:(fc4 * 2 + gu + 1) * 512],
                                    v1w[:, (idx % 2) * 1024 + gu * fw + fc4 * 128:
                                        (idx % 2) * 1024 + gu * fw + (fc4 + 1) * 128],
                                    chunk_ap(PT, TOK, rc, half * 512, 512),
                                    start=(rc == 0), stop=False)
                        v_fr.inc(last)
                    last = None
                    for fc4 in range(nf4):
                        t.matmul(
                            pgu[:, (fc4 * 2) * 512:(fc4 * 2) * 512 + 512],
                            b1_sb[0:1, fo + fc4 * 128:fo + (fc4 + 1) * 128],
                            ones_bf[0:1, 0:512], start=False, stop=True)
                        last = t.matmul(
                            pgu[:, (fc4 * 2 + 1) * 512:(fc4 * 2 + 1) * 512 + 512],
                            b1_sb[0:1, INTER + fo + fc4 * 128:INTER + fo + (fc4 + 1) * 128],
                            ones_bf[0:1, 0:512], start=False, stop=True)
                    mmgu.inc(last)
                t.wait_ge(eva.h, (half + 1) * FC_HALF)
                t.wait_ge(sgq.h, (half + 1) * FC_HALF)
                if half == 1:
                    pass
                for fc in range(86):
                    idx = half * 86 + fc
                    t.wait_ge(u2cst.h, idx + 1)
                    last = None
                    for r2c in range(8):
                        last = t.matmul(
                            pgu[:, r2c * 512:(r2c + 1) * 512],
                            u2w[:, (idx % 2) * 1024 + r2c * 128:
                                (idx % 2) * 1024 + (r2c + 1) * 128],
                            actT[:, fc * 512:(fc + 1) * 512],
                            start=(fc == 0), stop=(fc == 85))
                    u2_fr.inc(last)

        @blk.scalar
        def _(s):
            cnt = 0
            for half in range(2):
                for j, (fo, fw) in enumerate(fgs):
                    git = half * JF + j
                    nf4 = fw // 128
                    s.wait_ge(mmgu.h, git + 1)
                    for fc4 in range(nf4):
                        if cnt >= 2:
                            s.wait_ge(eva.h, cnt - 1)
                        sgq.inc(s.activation(
                            sgbuf[:, (cnt % 2) * 512:(cnt % 2 + 1) * 512],
                            pgu[:, (fc4 * 2) * 512:(fc4 * 2) * 512 + 512],
                            AF.Silu))
                        cnt += 1

        @blk.vector
        def _(v):
            cnt = [0]

            def vcast(idx):
                half_, rem = divmod(idx, JF * 8)
                j_, rc_ = divmod(rem, 8)
                fw_ = fgs[j_][1]
                v.wait_ge(v_ld[idx % 4].h, 16 * (idx // 4 + 1))
                if idx >= 2:
                    v.wait_ge(v_fr.h, idx - 1)
                vcst.inc(v.tensor_copy(
                    v1w[:, (idx % 2) * 1024:(idx % 2) * 1024 + 2 * fw_],
                    v1r[:, (idx % 4) * 1024:(idx % 4) * 1024 + 2 * fw_]))

            def u2cast(idx):
                v.wait_ge(u2_ld[idx % 2].h, 16 * (idx // 2 + 1))
                if idx >= 2:
                    v.wait_ge(u2_fr.h, idx - 1)
                u2cst.inc(v.tensor_copy(
                    u2w[:, (idx % 2) * 1024:(idx % 2 + 1) * 1024],
                    u2r[:, (idx % 2) * 1024:(idx % 2 + 1) * 1024]))

            def emit_evas(git):
                half_, j_ = divmod(git, JF)
                nf4 = nf4s[j_]
                nfc0 = sum(nf4s[:j_])
                for fc4 in range(nf4):
                    v.wait_ge(sgq.h, cnt[0] + 1)
                    eva.inc(v.tensor_tensor(
                        actT[:, (nfc0 + fc4) * 512:(nfc0 + fc4 + 1) * 512],
                        sgbuf[:, (cnt[0] % 2) * 512:(cnt[0] % 2 + 1) * 512],
                        pgu[:, (fc4 * 2 + 1) * 512:(fc4 * 2 + 1) * 512 + 512],
                        ALU.mult))
                    cnt[0] += 1

            for half in range(2):
                if half == 1:
                    v.wait_ge(u2_fr.h, 86)
                for j in range(JF):
                    git = half * JF + j
                    for rc in range(8):
                        vcast(git * 8 + rc)
                        if rc == 0 and j > 0:
                            emit_evas(git - 1)
                emit_evas(half * JF + JF - 1)
                for fc in range(86):
                    u2cast(half * 86 + fc)
                v.wait_ge(u2_fr.h, (half + 1) * 86)
                for r2c in range(8):
                    evZ.inc(v.tensor_copy(
                        chunk_ap(ZT, TOK, r2c, half * 512, 512),
                        pgu[:, r2c * 512:(r2c + 1) * 512]))

    w_fr2, mme2, eve2, sty2, sts2 = emit_mm_res(
        p, nc, ZT, 8, V2, D, x1i, y2, ysum2, ones_bf, bias_sb=b2_sb,
        entry_waits=[(evZ, 16), (stf1[0], stf1[0].v), (stf1[1], stf1[1].v)],
        tag="c2mm")
    _es.close()
    return sty2 + [sts2]


def build_launch_C2():
    p = Prog()
    io = dict(
        y1=p.inp("y1", [TOK, D], F32),
        ysum1=p.inp("ysum1", [128, NT], F32),
        U1=p.inp("U1", [D, RANK], F32),
        V1=p.inp("V1", [RANK, 2 * INTER], F32),
        U2=p.inp("U2", [INTER, RANK], F32),
        V2=p.inp("V2", [RANK, D], F32),
        b1=p.inp("b1", [1, 2 * INTER], F32),
        b2=p.inp("b2", [1, D], F32),
        ident_bf=p.inp("ident_bf", [128, 128], BF16),
        ones_bf=p.inp("ones_bf", [1, 512], BF16),
        y2=p.outp("y2", [TOK, D], F32),
        ysum2=p.outp("ysum2", [128, NT], F32),
    )
    sts = emit_launch_C2(p, io)
    with p.nc.Block() as blk:
        @blk.sync
        def _(s):
            for cs in sts:
                s.wait_ge(cs.h, cs.v)
    return p


def run_launch_C2(inputs, resC1, consts=None):
    consts = consts or host_consts()
    ones512 = np.ones((1, 512), dtype=BF)
    in_maps = []
    for c in range(NCORES):
        in_maps.append(dict(
            y1=resC1[c]["y1"], ysum1=resC1[c]["ysum1"],
            U1=inputs["L0_ffn_U1"], V1=inputs["L0_ffn_V1"],
            U2=inputs["L0_ffn_U2"], V2=inputs["L0_ffn_V2"],
            b1=inputs["L0_ffn_b1"].reshape(1, -1),
            b2=inputs["L0_ffn_b2"].reshape(1, -1),
            ident_bf=consts["ident_bf"], ones_bf=ones512,
        ))
    pC = build_launch_C2()
    res = run_bass_kernel_spmd(pC.nc, in_maps, core_ids=list(range(NCORES)))
    return res.results


def emit_launch_C3(p, io):
    nc = p.nc
    _es = ExitStack()
    def _sbl(name, shape, dt=F32):
        return _es.enter_context(
            nc.sbuf_tensor(name, shape, dt, allow_name_mangling=True))
    y2 = io["y2"]
    ysum2 = io["ysum2"]
    wT = io["wT"]
    bias = io["bias"]
    bmT = io["bmT"]
    Emat_d = io["Emat"]
    ident_bf_d = io["ident_bf"]
    ones_bf_d = io["ones_bf"]
    y3 = io["y3"]
    ysum3 = io["ysum3"]
    x2i = nc.dram_tensor("x2i", [TOK, D], F32, kind="Internal",
                         unsafe_allow_realloc=True).ap()

    ident_bf = _sbl("c3_ident", [128, 128], BF16)
    ones_bf = _sbl("c3_ones", [1, 512], BF16)
    bias_sb = _sbl("c3_bias", [1, D], BF16)
    Emat_s = _sbl("c3_E", [4, 128], F32)
    bmT4 = _sbl("c3_bmT4", [4, 32 * 128], F32)
    flg4 = _sbl("c3_flg4", [4, 32 * 128], F32)
    bmET = _sbl("c3_bmET", [128, 32 * 128], F32)
    x2T = _sbl("c3_x2T", [128, 32 * TOK], BF16)
    c_ld = p.sem()
    c_ld2 = p.sem()
    dvf = p.sem()
    mmE = p.sem()
    evEx = p.sem()
    with nc.psum_tensor("c3_pE", [128, 128], F32, allow_name_mangling=True) as pE, \
         nc.Block() as blk:
        @blk.sync
        def _(s):
            c_ld.inc(s.dma_start(out=ident_bf[:], in_=ident_bf_d), 16)
            c_ld.inc(s.dma_start(out=ones_bf[:], in_=ones_bf_d), 16)
            c_ld.inc(s.dma_start(out=Emat_s[:], in_=Emat_d), 16)
            c_ld.inc(s.dma_start(
                out=bmT4[:, :].rearrange("p (g nb) -> p g nb", g=32),
                in_=bmT.rearrange("(g p) nb -> p g nb", p=4)), 16)

        @blk.gpsimd
        def _(g):
            c_ld2.inc(g.dma_start(out=bias_sb[:], in_=bias), 16)

        @blk.vector
        def _(v):
            v.wait_ge(c_ld.h, 16 * 4)
            v.wait_ge(c_ld2.h, 16)
            dvf.inc(v.tensor_single_scalar(flg4[:, :], bmT4[:, :], 0.0, ALU.is_gt))

        @blk.tensor
        def _(t):
            t.wait_ge(dvf.h, 1)
            for kc in range(32):
                if kc >= 1:
                    t.wait_ge(evEx.h, kc)
                mmE.inc(t.matmul(pE[:], Emat_s[:, :],
                                 flg4[:, kc * 128:(kc + 1) * 128],
                                 start=True, stop=True))

        @blk.vector
        def _(v):
            for kc in range(32):
                v.wait_ge(mmE.h, kc + 1)
                evEx.inc(v.tensor_copy(bmET[:, kc * 128:(kc + 1) * 128], pE[:]))

    evn2, evT2, stf2, _tp = emit_ln(p, nc, y2, ysum2, x2i, x2T, ident_bf,
                                    entry_waits=[(evEx, 32)], tag="c3ln")

    def mask_view(dg, kc):
        return bmET[:, kc * 128 + dg * 16:kc * 128 + (dg + 1) * 16] \
            .unsqueeze(2).broadcast_to([128, 16, 32])

    w_fr3, mme3, eve3, sty3, sts3 = emit_mm_res(
        p, nc, x2T, 32, wT, D, x2i, y3, ysum3, ones_bf, bias_sb=bias_sb,
        entry_waits=[(evT2, evT2.v), (stf2[0], stf2[0].v), (stf2[1], stf2[1].v)],
        mask_fn=mask_view, tag="c3mm")
    _es.close()
    return sty3 + [sts3]


def build_launch_C3():
    p = Prog()
    io = dict(
        y2=p.inp("y2", [TOK, D], F32),
        ysum2=p.inp("ysum2", [128, NT], F32),
        wT=p.inp("wT", [D, D], F32),
        bias=p.inp("bias", [1, D], F32),
        bmT=p.inp("bmT", [128, 128], F32),
        Emat=p.inp("Emat", [4, 128], F32),
        ident_bf=p.inp("ident_bf", [128, 128], BF16),
        ones_bf=p.inp("ones_bf", [1, 512], BF16),
        y3=p.outp("y3", [TOK, D], F32),
        ysum3=p.outp("ysum3", [128, NT], F32),
    )
    sts = emit_launch_C3(p, io)
    with p.nc.Block() as blk:
        @blk.sync
        def _(s):
            for cs in sts:
                s.wait_ge(cs.h, cs.v)
    return p


def run_launch_C3(inputs, resC2, bm_full, consts=None):
    consts = consts or host_consts()
    ones512 = np.ones((1, 512), dtype=BF)
    Emat = np.zeros((4, 128), dtype=np.float32)
    for pp_ in range(128):
        Emat[pp_ // 32, pp_] = 1.0
    wTh = np.ascontiguousarray(inputs["L1_attn_weight"].T)
    bmT = np.ascontiguousarray(bm_full.T)
    in_maps = []
    for c in range(NCORES):
        in_maps.append(dict(
            y2=resC2[c]["y2"], ysum2=resC2[c]["ysum2"],
            wT=wTh, bias=inputs["L1_attn_bias"].reshape(1, -1),
            bmT=bmT, Emat=Emat,
            ident_bf=consts["ident_bf"], ones_bf=ones512,
        ))
    pC = build_launch_C3()
    res = run_bass_kernel_spmd(pC.nc, in_maps, core_ids=list(range(NCORES)))
    return res.results


SORT8_STAGES = [
    [(0, 1), (2, 3), (4, 5), (6, 7)],
    [(0, 2), (1, 3), (4, 6), (5, 7)],
    [(1, 2), (5, 6), (0, 4), (3, 7)],
    [(1, 5), (2, 6)],
    [(1, 4), (3, 6)],
    [(2, 4), (3, 5)],
    [(3, 4)],
]


def emit_launch_C4(p, io):
    nc = p.nc
    _es = ExitStack()
    def _sbl(name, shape, dt=F32):
        return _es.enter_context(
            nc.sbuf_tensor(name, shape, dt, allow_name_mangling=True))
    y3 = io["y3"]
    ysum3 = io["ysum3"]
    WbT = io["WbT"]
    b_base = io["b_base"]
    WrT = io["WrT"]
    b_router = io["b_router"]
    laA2 = io["laA2"]
    laB2 = io["laB2"]
    kvec_d = io["kvec"]
    R16_d = io["R16"]
    ident_f32_d = io["ident_f32"]
    ident_bf_d = io["ident_bf"]
    ones_bf_d = io["ones_bf"]
    y4 = io["y4"]
    ysum4 = io["ysum4"]
    x3i = nc.dram_tensor("x3i", [TOK, D], F32, kind="Internal",
                         unsafe_allow_realloc=True).ap()

    ident_bf = _sbl("c4_identb", [128, 128], BF16)
    ident_f32 = _sbl("c4_identf", [128, 128], F32)
    ones_bf = _sbl("c4_ones", [1, 512], BF16)
    bb_sb = _sbl("c4_bb", [1, D], BF16)
    br_sb = _sbl("c4_br", [1, E_EXP], BF16)
    kvec = _sbl("c4_kvec", [128, 8], F32)
    R16 = _sbl("c4_R16", [8, 128], F32)
    WrT_sb = _sbl("c4_WrT", [128, 32 * 8], BF16)
    laA_sb = _sbl("c4_laA", [128, 32 * 128], BF16)
    laB_sb = _sbl("c4_laB", [128, D], BF16)
    x3T = _sbl("c4_x3T", [128, 32 * TOK], BF16)
    hwT_sb = _sbl("c4_hwT", [128, NT * 128], BF16)
    c_ld = p.sem()
    c_ld2 = p.sem()
    with nc.Block() as blk:
        @blk.sync
        def _(s):
            c_ld.inc(s.dma_start(out=ident_bf[:], in_=ident_bf_d), 16)
            c_ld.inc(s.dma_start(out=ident_f32[:], in_=ident_f32_d), 16)
            c_ld.inc(s.dma_start(out=ones_bf[:], in_=ones_bf_d), 16)
            c_ld.inc(s.dma_start(out=kvec[:], in_=kvec_d), 16)
            c_ld.inc(s.dma_start(out=R16[:], in_=R16_d), 16)

        @blk.gpsimd
        def _(g):
            c_ld2.inc(g.dma_start(out=bb_sb[:], in_=b_base), 16)
            c_ld2.inc(g.dma_start(out=br_sb[:], in_=b_router), 16)
            c_ld2.inc(g.dma_start(
                out=WrT_sb[:, :].rearrange("p (c e) -> p c e", c=32),
                in_=WrT.rearrange("(c p) e -> p c e", p=128)), 16)
            c_ld2.inc(g.dma_start(out=laB_sb[:], in_=laB2), 16)
            for j in range(4):
                c_ld2.inc(g.dma_start(
                    out=laA_sb[:, j * 8 * 128:(j + 1) * 8 * 128]
                        .rearrange("p (c e) -> p c e", c=8),
                    in_=laA2.rearrange("(c p) e -> p c e", p=128)[
                        :, j * 8:(j + 1) * 8, :]), 16)

    evn3, evT3, stf3, _tp = emit_ln(p, nc, y3, ysum3, x3i, x3T, ident_bf,
                                    entry_waits=[(c_ld, c_ld.v), (c_ld2, c_ld2.v)],
                                    tag="c4ln")

    mml = p.sem()
    evl = p.sem()
    dsp = p.sem()
    mmh = p.sem()
    mmt = p.sem()
    evt = p.sem()
    mmx = p.sem()
    evx = p.sem()
    evw = p.sem()
    with nc.sbuf_tensor("c4_lg", [128, NT * 8], F32, allow_name_mangling=True) as lg, \
         nc.sbuf_tensor("c4_sa", [128, 8], F32, allow_name_mangling=True) as sa, \
         nc.sbuf_tensor("c4_sb2", [128, 8], F32, allow_name_mangling=True) as sb2, \
         nc.sbuf_tensor("c4_sc", [128, 16], F32, allow_name_mangling=True) as scx, \
         nc.sbuf_tensor("c4_tm8", [128, 8], F32, allow_name_mangling=True) as tm8, \
         nc.sbuf_tensor("c4_wt", [8, 128], F32, allow_name_mangling=True) as wt_sb, \
         nc.sbuf_tensor("c4_wfm", [128, 128], F32, allow_name_mangling=True) as wfm_sb, \
         nc.sbuf_tensor("c4_wtok", [128, NT * 8], F32, allow_name_mangling=True) as wtok, \
         nc.psum_tensor("c4_pq", [128, 4096], F32, allow_name_mangling=True) as pq, \
         nc.Block() as blk:
        @blk.tensor
        def _(t):
            t.wait_ge(evT3.h, evT3.v)
            for tt in range(NT):
                last = None
                for kc in range(32):
                    last = t.matmul(pq[:, tt * 512:tt * 512 + 8],
                                    chunk_ap(x3T, TOK, kc, tt * 128, 128),
                                    WrT_sb[:, kc * 8:(kc + 1) * 8],
                                    start=(kc == 0), stop=False)
                last = t.matmul(pq[:, tt * 512:tt * 512 + 8],
                                ones_bf[0:1, 0:128], br_sb[0:1, :],
                                start=False, stop=True)
                mml.inc(last)
                last = None
                for kc in range(32):
                    last = t.matmul(pq[:, tt * 512 + 128:tt * 512 + 256],
                                    laA_sb[:, kc * 128:(kc + 1) * 128],
                                    chunk_ap(x3T, TOK, kc, tt * 128, 128),
                                    start=(kc == 0), stop=(kc == 31))
                mmh.inc(last)
            for tt in range(NT):
                t.wait_ge(dsp.h, (tt + 1) * 100)
                t.wait_ge(evl.h, tt + 1)
                if tt >= 1:
                    t.wait_ge(evt.h, tt)
                mmt.inc(t.transpose(pq[0:8, tt * 512 + 256:tt * 512 + 384],
                                    wtok[:, tt * 8:(tt + 1) * 8],
                                    ident_f32[:]))
                t.wait_ge(evt.h, tt + 1)
                if tt >= 1:
                    t.wait_ge(evx.h, tt)
                mmx.inc(t.matmul(pq[:, tt * 512 + 384:tt * 512 + 512],
                                 R16[:, :], wt_sb[:, :], start=True, stop=True))

        @blk.vector
        def _(v):
            def ch(instr, n=1):
                dsp.inc(instr, n)
                v.wait_ge(dsp.h, dsp.v)
                return instr

            for tt in range(NT):
                base = dsp.v
                v.wait_ge(mml.h, tt + 1)
                v.wait_ge(mmh.h, tt + 1)
                evl.inc(v.tensor_copy(lg[:, tt * 8:(tt + 1) * 8],
                                      pq[:, tt * 512:tt * 512 + 8]))
                v.wait_ge(evl.h, evl.v)
                ch(v.tensor_copy(sa[:, :], lg[:, tt * 8:(tt + 1) * 8]))
                src, dst = sa, sb2
                for stage in SORT8_STAGES:
                    in_pairs = set()
                    last = None
                    for (i, j) in stage:
                        last = v.tensor_max(dst[:, i:i + 1], src[:, i:i + 1],
                                            src[:, j:j + 1])
                        last = v.tensor_tensor(dst[:, j:j + 1], src[:, i:i + 1],
                                               src[:, j:j + 1], ALU.min)
                        in_pairs.add(i)
                        in_pairs.add(j)
                    for k in range(8):
                        if k not in in_pairs:
                            last = v.tensor_copy(dst[:, k:k + 1], src[:, k:k + 1])
                    ch(last)
                    src, dst = dst, src
                ch(v.tensor_copy(scx[:, 0:1], src[:, 0:1]))
                for jj in range(1, 8):
                    ch(v.tensor_add(scx[:, jj:jj + 1], scx[:, jj - 1:jj],
                                    src[:, jj:jj + 1]))
                ch(v.tensor_tensor(tm8[:, :], src[:, :], kvec[:, :], ALU.mult))
                ch(v.tensor_scalar_add(src[:, :], tm8[:, :], 1.0))
                ch(v.tensor_tensor(dst[:, :], src[:, :], scx[:, 0:8], ALU.is_gt))
                ch(v.tensor_reduce(scx[:, 8:9], dst[:, :], AX.X, ALU.add))
                ch(v.tensor_sub(tm8[:, 0:7], dst[:, 0:7], dst[:, 1:8]))
                ch(v.tensor_copy(tm8[:, 7:8], dst[:, 7:8]))
                ch(v.tensor_tensor(dst[:, :], scx[:, 0:8], tm8[:, :], ALU.mult))
                ch(v.tensor_reduce(scx[:, 9:10], dst[:, :], AX.X, ALU.add))
                ch(v.reciprocal(scx[:, 10:11], scx[:, 8:9]))
                ch(v.tensor_scalar_add(scx[:, 11:12], scx[:, 9:10], -1.0))
                ch(v.tensor_mul(scx[:, 12:13], scx[:, 11:12], scx[:, 10:11]))
                ch(v.tensor_scalar(wtok[:, tt * 8:(tt + 1) * 8],
                                   lg[:, tt * 8:(tt + 1) * 8],
                                   scx[:, 12:13], 0.0, ALU.subtract, ALU.max))
                pad = base + 100 - dsp.v
                assert pad >= 0, dsp.v
                if pad:
                    dsp.inc(v.engine_nop(), pad)
            for tt in range(NT):
                v.wait_ge(mmt.h, tt + 1)
                evt.inc(v.tensor_copy(wt_sb[:, :],
                                      pq[0:8, tt * 512 + 256:tt * 512 + 384]))
                v.wait_ge(mmx.h, tt + 1)
                v.wait_ge(evt.h, tt + 1)
                if tt >= 1:
                    v.wait_ge(evw.h, tt)
                evx.inc(v.tensor_copy(wfm_sb[:, :],
                                      pq[:, tt * 512 + 384:tt * 512 + 512]))
                v.wait_ge(evx.h, tt + 1)
                v.wait_ge(mmh.h, tt + 1)
                evw.inc(v.tensor_tensor(hwT_sb[:, tt * 128:(tt + 1) * 128],
                                        pq[:, tt * 512 + 128:tt * 512 + 256],
                                        wfm_sb[:, :], ALU.mult))

    _c4pq = [None]

    def lora_mm(t, dg, tt):
        return t.matmul(
            _c4pq[0][:, tt * 512:(tt + 1) * 512],
            hwT_sb[:, tt * 128:(tt + 1) * 128],
            laB_sb[:, dg * 512:(dg + 1) * 512],
            start=False, stop=True)

    w_fr4, mme4, eve4, sty4, sts4 = emit_mm_res(
        p, nc, x3T, 32, WbT, D, x3i, y4, ysum4, ones_bf, bias_sb=bb_sb,
        entry_waits=[(stf3[0], stf3[0].v), (stf3[1], stf3[1].v)],
        extra_mm=lora_mm, extra_waits=[(evw, NT)], psum_ref=_c4pq, tag="c4mm")
    _es.close()
    return sty4 + [sts4]


def build_launch_C4():
    p = Prog()
    io = dict(
        y3=p.inp("y3", [TOK, D], F32),
        ysum3=p.inp("ysum3", [128, NT], F32),
        WbT=p.inp("WbT", [D, D], F32),
        b_base=p.inp("b_base", [1, D], F32),
        WrT=p.inp("WrT", [D, E_EXP], F32),
        b_router=p.inp("b_router", [1, E_EXP], F32),
        laA2=p.inp("laA2", [D, 128], F32),
        laB2=p.inp("laB2", [128, D], F32),
        kvec=p.inp("kvec", [128, 8], F32),
        R16=p.inp("R16", [8, 128], F32),
        ident_f32=p.inp("ident_f32", [128, 128], F32),
        ident_bf=p.inp("ident_bf", [128, 128], BF16),
        ones_bf=p.inp("ones_bf", [1, 512], BF16),
        y4=p.outp("y4", [TOK, D], F32),
        ysum4=p.outp("ysum4", [128, NT], F32),
    )
    sts = emit_launch_C4(p, io)
    with p.nc.Block() as blk:
        @blk.sync
        def _(s):
            for cs in sts:
                s.wait_ge(cs.h, cs.v)
    return p


def run_launch_C4(inputs, resC3, consts=None):
    consts = consts or host_consts()
    ones512 = np.ones((1, 512), dtype=BF)
    kvec = np.broadcast_to(np.arange(1, 9, dtype=np.float32), (128, 8)).copy()
    R16 = np.zeros((8, 128), dtype=np.float32)
    for pp_ in range(128):
        R16[pp_ // 16, pp_] = 1.0
    WbT = np.ascontiguousarray(inputs["L1_ffn_W_base"].T)
    WrT = np.ascontiguousarray(inputs["L1_ffn_W_router"].T)
    laA2 = np.ascontiguousarray(
        inputs["L1_ffn_lora_A"].transpose(1, 0, 2).reshape(D, 128))
    laB2 = np.ascontiguousarray(inputs["L1_ffn_lora_B"].reshape(128, D))
    in_maps = []
    for c in range(NCORES):
        in_maps.append(dict(
            y3=resC3[c]["y3"], ysum3=resC3[c]["ysum3"],
            WbT=WbT, b_base=inputs["L1_ffn_b_base"].reshape(1, -1),
            WrT=WrT, b_router=inputs["L1_ffn_b_router"].reshape(1, -1),
            laA2=laA2, laB2=laB2, kvec=kvec, R16=R16,
            ident_f32=consts["ident_f32"], ident_bf=consts["ident_bf"],
            ones_bf=ones512,
        ))
    pC = build_launch_C4()
    res = run_bass_kernel_spmd(pC.nc, in_maps, core_ids=list(range(NCORES)))
    return res.results




def phase_reset(p, store_ctrs=()):
    nc = p.nc
    live = [cs for cs in store_ctrs if cs is not None and cs.v]
    if live:
        with nc.Block() as blk:
            @blk.sync
            def _(s):
                for cs in live:
                    s.wait_ge(cs.h, cs.v)
    nc.all_engine_barrier()
    nc.clear_and_free_semaphores(nc.allocated_since(p.snap))
    nc.all_engine_barrier()
    nc.restore_sems(p.snap)


def store_join(p, store_ctrs=()):
    nc = p.nc
    live = [cs for cs in store_ctrs if cs is not None and cs.v]
    if live:
        with nc.Block() as blk:
            @blk.sync
            def _(s):
                for cs in live:
                    s.wait_ge(cs.h, cs.v)


def build_fused():
    p = Prog()
    nc = p.nc
    i_ = p.inp
    x = i_("x", [TOK, D])
    xT_own = i_("xT_own", [D, TOK])
    xbT = i_("xbT", [D, S])
    mask = i_("mask", [S, D], I32)
    w_qa = i_("w_qa", [D, QR]); g_qa = i_("g_qa", [1, QR])
    w_qb = i_("w_qb", [QR, NH * QK])
    w_kva = i_("w_kva", [D, KVR + RD]); g_kva = i_("g_kva", [1, KVR])
    w_kvb = i_("w_kvb", [KVR, NH * (ND + VD)])
    w_o = i_("w_o", [D, D])
    U1 = i_("U1", [D, RANK]); V1 = i_("V1", [RANK, 2 * INTER])
    U2 = i_("U2", [INTER, RANK]); V2 = i_("V2", [RANK, D])
    b1 = i_("b1", [1, 2 * INTER]); b2 = i_("b2", [1, D])
    wT = i_("wT", [D, D]); bias1 = i_("bias1", [1, D])
    WbT = i_("WbT", [D, D]); b_base = i_("b_base", [1, D])
    WrT = i_("WrT", [D, E_EXP]); b_router = i_("b_router", [1, E_EXP])
    laA2 = i_("laA2", [D, 128]); laB2 = i_("laB2", [128, D])
    cos_own = i_("cos_own", [RD, TOK]); sin_own = i_("sin_own", [RD, TOK])
    cos_full = i_("cos_full", [RD, S]); sin_full = i_("sin_full", [RD, S])
    ident_bf = i_("ident_bf", [128, 128], BF16)
    ident_f32 = i_("ident_f32", [128, 128])
    ones_f32 = i_("ones_f32", [1, 128])
    ones_bf = i_("ones_bf", [1, 512], BF16)
    Gmat = i_("Gmat", [128, 4]); Emat = i_("Emat", [4, 128])
    kvec = i_("kvec", [128, 8]); R16 = i_("R16", [8, 128])
    out = p.outp("out", [TOK, D], F32)

    def internal(name, shape, dt=F32):
        return nc.dram_tensor(name, list(shape), dt, kind="Internal").ap()

    Qf_i = internal("Qf_i", [NH * QK, TOK], BF16)
    KN_i = internal("KN_i", [NH * ND, S], BF16)
    KR_i = internal("KR_i", [RD, S], BF16)
    Vt_i = internal("Vt_i", [S, NH * VD], BF16)
    attn_i = internal("attn_i", [TOK, NH * VD], BF16)
    bm_i = internal("bm_i", [128, 128])
    bmT_i = internal("bmT_i", [128, 128])
    ys = {k: internal(f"ysum{k}_i", [128, NT]) for k in (1, 2, 3, 4)}
    yv = {k: internal(f"y{k}_i", [TOK, D]) for k in (1, 2, 3, 4)}

    p.snap = nc.snapshot_sems()

    st = emit_launch_A(p, dict(
        xT=xT_own, w_qa=w_qa, g_qa=g_qa, w_qb=w_qb, w_kva=w_kva, g_kva=g_kva,
        w_kvb=w_kvb, cos_own=cos_own, sin_own=sin_own,
        ident_bf=ident_bf, ident_f32=ident_f32, ones_f32=ones_f32,
        Qf=Qf_i, KN=KN_i, KR=KR_i, Vt=Vt_i), do_q=True, do_kv=False,
        do_bm=False)
    phase_reset(p, st)

    _eskv = ExitStack()
    wkvb_p = _eskv.enter_context(nc.sbuf_tensor(
        "wkvb_p", [128, 4 * 8192], BF16, allow_name_mangling=True))
    for sl in range(4):
        st = emit_launch_A(p, dict(
            xT=xbT[:, sl * TOK:(sl + 1) * TOK],
            w_kva=w_kva, g_kva=g_kva, w_kvb=w_kvb,
            cos_own=cos_full[:, sl * TOK:(sl + 1) * TOK],
            sin_own=sin_full[:, sl * TOK:(sl + 1) * TOK],
            ident_bf=ident_bf, ident_f32=ident_f32, ones_f32=ones_f32,
            KN=KN_i[:, sl * TOK:(sl + 1) * TOK],
            KR=KR_i[:, sl * TOK:(sl + 1) * TOK],
            Vt=Vt_i[sl * TOK:(sl + 1) * TOK, :]),
            do_q=False, do_kv=True, do_bm=False,
            wkvb_pre=dict(buf=wkvb_p.ap() if hasattr(wkvb_p, "ap") else wkvb_p,
                          mode="load" if sl == 0 else "use"))
        if sl % 2 == 1:
            phase_reset(p, st)
        else:
            store_join(p, st)
    _eskv.close()

    st = emit_bm(p, nc, mask, Gmat, bm_i, 32)
    store_join(p, st)
    stT = p.sem()
    ld_t = p.sem()
    mm_t = p.sem()
    ev_t = p.sem()
    with nc.sbuf_tensor("fb_bm", [128, 128], F32, allow_name_mangling=True) as bm_sb, \
         nc.sbuf_tensor("fb_id", [128, 128], F32, allow_name_mangling=True) as idf, \
         nc.sbuf_tensor("fb_bmT", [128, 128], F32, allow_name_mangling=True) as bmT_sb, \
         nc.psum_tensor("fb_pt", [128, 128], F32, allow_name_mangling=True) as pt, \
         nc.Block() as blk:
        @blk.sync
        def _(s):
            ld_t.inc(s.dma_start(out=bm_sb[:], in_=bm_i), 16)
            ld_t.inc(s.dma_start(out=idf[:], in_=ident_f32), 16)

        @blk.tensor
        def _(t):
            t.wait_ge(ld_t.h, 32)
            mm_t.inc(t.transpose(pt[:], bm_sb[:], idf[:]))

        @blk.vector
        def _(v):
            v.wait_ge(mm_t.h, 1)
            ev_t.inc(v.tensor_copy(bmT_sb[:], pt[:]))

        @blk.scalar
        def _(s):
            s.wait_ge(ev_t.h, 1)
            stT.inc(s.dma_start(out=bmT_i, in_=bmT_sb[:]), 16)
    store_join(p, [stT])

    st = emit_launch_B2(p, dict(Qf=Qf_i, KN_b=KN_i, KR_b=KR_i, Vt_b=Vt_i,
                                ident_bf=ident_bf, attn=attn_i))
    phase_reset(p, st)

    st = emit_launch_C1(p, dict(attn=attn_i, x=x, w_o=w_o, ident_bf=ident_bf,
                                y1=yv[1], ysum1=ys[1]))
    store_join(p, st)
    st = emit_launch_C2(p, dict(y1=yv[1], ysum1=ys[1], U1=U1, V1=V1, U2=U2,
                                V2=V2, b1=b1, b2=b2, ident_bf=ident_bf,
                                ones_bf=ones_bf, y2=yv[2], ysum2=ys[2]))
    phase_reset(p, st)
    st = emit_launch_C3(p, dict(y2=yv[2], ysum2=ys[2], wT=wT, bias=bias1,
                                bmT=bmT_i, Emat=Emat, ident_bf=ident_bf,
                                ones_bf=ones_bf, y3=yv[3], ysum3=ys[3]))
    store_join(p, st)
    st = emit_launch_C4(p, dict(y3=yv[3], ysum3=ys[3], WbT=WbT, b_base=b_base,
                                WrT=WrT, b_router=b_router, laA2=laA2,
                                laB2=laB2, kvec=kvec, R16=R16,
                                ident_f32=ident_f32, ident_bf=ident_bf,
                                ones_bf=ones_bf, y4=yv[4], ysum4=ys[4]))
    store_join(p, st)
    st = emit_launch_C5(p, dict(y4=yv[4], ysum4=ys[4], out=out))
    with nc.Block() as blk:
        @blk.sync
        def _(s):
            for cs in st:
                if cs.v:
                    s.wait_ge(cs.h, cs.v)
    return p


def kernel_fused(inputs):
    consts = host_consts()
    ones512 = np.ones((1, 512), dtype=BF)
    kvec = np.broadcast_to(np.arange(1, 9, dtype=np.float32), (128, 8)).copy()
    R16 = np.zeros((8, 128), dtype=np.float32)
    for pp_ in range(128):
        R16[pp_ // 16, pp_] = 1.0
    Emat = np.zeros((4, 128), dtype=np.float32)
    for pp_ in range(128):
        Emat[pp_ // 32, pp_] = 1.0
    x_full = inputs["L0_attn_hidden_states"]
    wTh = np.ascontiguousarray(inputs["L1_attn_weight"].T)
    WbT = np.ascontiguousarray(inputs["L1_ffn_W_base"].T)
    WrT = np.ascontiguousarray(inputs["L1_ffn_W_router"].T)
    laA2 = np.ascontiguousarray(
        inputs["L1_ffn_lora_A"].transpose(1, 0, 2).reshape(D, 128))
    laB2 = np.ascontiguousarray(inputs["L1_ffn_lora_B"].reshape(128, D))
    in_maps = []
    for c in range(NCORES):
        b, off = divmod(c, 4)
        pos0 = off * TOK
        in_maps.append(dict(
            x=np.ascontiguousarray(x_full[b, pos0:pos0 + TOK, :]),
            xT_own=np.ascontiguousarray(x_full[b, pos0:pos0 + TOK, :].T),
            xbT=np.ascontiguousarray(x_full[b].T),
            mask=inputs["L1_attn_mask"],
            w_qa=inputs["L0_attn_w_qa"],
            g_qa=inputs["L0_attn_g_qa"].reshape(1, -1),
            w_qb=inputs["L0_attn_w_qb"],
            w_kva=inputs["L0_attn_w_kva"],
            g_kva=inputs["L0_attn_g_kva"].reshape(1, -1),
            w_kvb=inputs["L0_attn_w_kvb"],
            w_o=inputs["L0_attn_w_o"],
            U1=inputs["L0_ffn_U1"], V1=inputs["L0_ffn_V1"],
            U2=inputs["L0_ffn_U2"], V2=inputs["L0_ffn_V2"],
            b1=inputs["L0_ffn_b1"].reshape(1, -1),
            b2=inputs["L0_ffn_b2"].reshape(1, -1),
            wT=wTh, bias1=inputs["L1_attn_bias"].reshape(1, -1),
            WbT=WbT, b_base=inputs["L1_ffn_b_base"].reshape(1, -1),
            WrT=WrT, b_router=inputs["L1_ffn_b_router"].reshape(1, -1),
            laA2=laA2, laB2=laB2,
            cos_own=np.ascontiguousarray(consts["cos_fm"][:, pos0:pos0 + TOK]),
            sin_own=np.ascontiguousarray(consts["sin_fm"][:, pos0:pos0 + TOK]),
            cos_full=consts["cos_fm"], sin_full=consts["sin_fm"],
            ident_bf=consts["ident_bf"], ident_f32=consts["ident_f32"],
            ones_f32=consts["ones_f32"], ones_bf=ones512,
            Gmat=consts["Gmat"], Emat=Emat, kvec=kvec, R16=R16,
        ))
    pF = build_fused()
    res = run_bass_kernel_spmd(pF.nc, in_maps, core_ids=list(range(NCORES)))
    outv = np.zeros((B, S, D), np.float32)
    for c in range(NCORES):
        b, off = divmod(c, 4)
        outv[b, off * TOK:(off + 1) * TOK] = res.results[c]["out"]
    return outv


FUSED = True

def kernel(**inputs):
    inputs = {k: np.ascontiguousarray(v) for k, v in inputs.items()}
    if FUSED:
        return kernel_fused(inputs)
    consts = host_consts()
    resA = kernel_resA = run_launch_A(inputs, consts)
    resB = run_launch_B(resA, consts)
    resC1 = run_launch_C1(inputs, resB, consts)
    resC2 = run_launch_C2(inputs, resC1, consts)
    bm_full = np.concatenate([resA[c]["bmp"] for c in range(NCORES)], axis=0)
    resC3 = run_launch_C3(inputs, resC2, bm_full, consts)
    resC4 = run_launch_C4(inputs, resC3, consts)
    resC5 = run_launch_C5(resC4)
    out = np.zeros((B, S, D), np.float32)
    for c in range(NCORES):
        b, off = divmod(c, 4)
        out[b, off * TOK:(off + 1) * TOK] = resC5[c]["out"]
    return out
